# version 44
# baseline (speedup 1.0000x reference)
import numpy as np
from contextlib import ExitStack
import concourse.bass as bass
import concourse.mybir as mybir
from concourse.bass import IndirectOffsetOnAxis

F32 = mybir.dt.float32
BF16 = mybir.dt.bfloat16
U32 = mybir.dt.uint32
AF = mybir.ActivationFunctionType
ALU = mybir.AluOpType
AX = mybir.AxisListType

GELU_C = 0.044715
GELU_S2 = 1.5957691216057308
LNEPS = 1e-5
BIG = float(1 << 20)


class Cfg:
    def __init__(self, S=4096, D=1024, H=16, F=4096, E=8, NC=8, TB=1024, C=640, BIS=30):
        self.S, self.D, self.H, self.F, self.E, self.NC = S, D, H, F, E, NC
        self.TB, self.C, self.BIS = TB, C, BIS
        self.HD = 64
        assert D == H * self.HD
        self.DT = D // 128
        self.NT = S // 128
        self.FT = F // 128
        self.ST = S // 128
        self.CT = C // 128
        self.NTB = S // TB
        self.N = NC * S
        self.KSEL = self.N // E
        self.TQ = min(1024, S)
        self.WC = min(512, F)
        assert C % 128 == 0 and TB % 512 == 0 and S % TB == 0 and TB <= 1024
        assert self.N % 128 == 0 and (self.N // 128) >= 1
        assert 128 % NC == 0


class Sem:
    def __init__(self, ctx, nc, name):
        self.h = ctx.enter_context(nc.semaphore(name))
        self.n = 0


def chunks(lo, hi, step=512):
    c = lo
    while c < hi:
        c1 = min(hi, (c // step) * step + step)
        yield c, c1
        c = c1


class MK:
    ENGS = ("tensor", "scalar", "vector", "gpsimd", "sync")

    def __init__(self, cfg: Cfg):
        self.cfg = cfg
        self.q = {e: [] for e in self.ENGS}
        self.sems = []

    def on(self, eng, build, waits=(), inc=None, inc_by=1):
        waits = tuple((s.h, v) for (s, v) in waits if v > 0)
        def thunk(E, build=build, waits=waits, inc=inc, inc_by=inc_by):
            for h, v in waits:
                E.wait_ge(h, v)
            inst = build(E)
            if inc is not None and inst is not None:
                inst.then_inc(inc.h, inc_by)
        self.q[eng].append(thunk)
        if inc is not None:
            inc.n += inc_by
        return inc.n if inc else None

    def barrier(self):
        vals = [(s, s.n) for s in self.sems if s.n > 0]
        for eng in self.ENGS:
            self.on(eng, (lambda En: None), waits=vals)

    def SM(self, name):
        s = Sem(self.ctx, self.nc, name)
        self.sems.append(s)
        return s

    def dma(self, eng, out, in_, sem, waits=()):
        self.on(eng, (lambda En, o=out, i=in_: En.dma_start(out=o, in_=i)),
                waits=waits, inc=sem, inc_by=16)

    def build(self):
        c = self.cfg
        S, D, H, F, E, TB, C = c.S, c.D, c.H, c.F, c.E, c.TB, c.C
        DT, NT, FT, ST, CT, NTB, TQ, WC = c.DT, c.NT, c.FT, c.ST, c.CT, c.NTB, c.TQ, c.WC
        nc = bass.Bass("TRN2", target_bir_lowering=False)
        self.nc = nc
        self.ctx = ExitStack()
        ctx = self.ctx

        ap = lambda t: t.ap()
        x_in = ap(nc.dram_tensor("x", [S, D], F32, kind="ExternalInput"))
        xT_in = ap(nc.dram_tensor("xT", [D, S], F32, kind="ExternalInput"))
        wqkvT_in = ap(nc.dram_tensor("wqkvT", [D, 3 * D], F32, kind="ExternalInput"))
        bqkvT_in = ap(nc.dram_tensor("bqkvT", [128, 3 * D // 128], F32, kind="ExternalInput"))
        bqkv_row_in = ap(nc.dram_tensor("bqkv_row", [1, 3 * D], F32, kind="ExternalInput"))
        woutT_in = ap(nc.dram_tensor("woutT", [D, D], F32, kind="ExternalInput"))
        bout_row_in = ap(nc.dram_tensor("bout_row", [1, D], F32, kind="ExternalInput"))
        ws_in = ap(nc.dram_tensor("ws", [D, E], F32, kind="ExternalInput"))
        bs_row_in = ap(nc.dram_tensor("bs_row", [1, E], F32, kind="ExternalInput"))
        w1_in = ap(nc.dram_tensor("w1", [E, D, F], F32, kind="ExternalInput"))
        b1T_in = ap(nc.dram_tensor("b1T", [E, 128, F // 128], F32, kind="ExternalInput"))
        w2_in = ap(nc.dram_tensor("w2", [E, F, D], F32, kind="ExternalInput"))
        b2_row_in = ap(nc.dram_tensor("b2_row", [E, 1, D], F32, kind="ExternalInput"))
        ident_in = ap(nc.dram_tensor("ident", [128, 128], F32, kind="ExternalInput"))
        tri01_in = ap(nc.dram_tensor("tri01", [128, 128], BF16, kind="ExternalInput"))
        triu_in = ap(nc.dram_tensor("triu", [128, 128], F32, kind="ExternalInput"))
        tokid_in = ap(nc.dram_tensor("tokid", [128, NT], U32, kind="ExternalInput"))
        tokbigE_in = ap(nc.dram_tensor("tokbigE", [128, NT * E], F32, kind="ExternalInput"))
        slotinit_in = ap(nc.dram_tensor("slotinit", [E * C, 1], U32, kind="ExternalInput"))
        rowidx_in = ap(nc.dram_tensor("rowidx", [128, 1], U32, kind="ExternalInput"))
        y_out = ap(nc.dram_tensor("y", [S, D], F32, kind="ExternalOutput"))

        q_dram = ap(nc.dram_tensor("q_dram", [D, S], BF16))
        k_dram = ap(nc.dram_tensor("k_dram", [D, S], BF16))
        vp_dram = ap(nc.dram_tensor("vp_dram", [ST, 128, H * 128], BF16))
        x1_dram = ap(nc.dram_tensor("x1_dram", [S, D], BF16))
        acc_dram = ap(nc.dram_tensor("acc_dram", [S + C, D], F32))
        g_dram = ap(nc.dram_tensor("g_dram", [S + C, E], F32))
        slot_dram = ap(nc.dram_tensor("slot_dram", [E * C, 1], U32))
        pin_dram = ap(nc.dram_tensor("pin_dram", [E, S], F32))
        pout_dram = ap(nc.dram_tensor("pout_dram", [c.NC * E, S], F32))
        pe_dram = ap(nc.dram_tensor("pe_dram", [c.NC, S], F32))
        tin_dram = ap(nc.dram_tensor("tin_dram", [1, 1], F32))
        nst1_dram = [ap(nc.dram_tensor(f"nst1_{i}", [1, TB], F32)) for i in range(4)]
        nst2_dram = [ap(nc.dram_tensor(f"nst2_{i}", [1, TB], F32)) for i in range(4)]
        tout_dram = ap(nc.dram_tensor("tout_dram", [c.NC, 1], F32))

        T = lambda scope, name, shape, dt: scope.enter_context(nc.sbuf_tensor(name, shape, dt))

        psA = ctx.enter_context(nc.psum_tensor("psA", [128, 2048], F32))
        psB = ctx.enter_context(nc.psum_tensor("psB", [128, 2048], F32))

        P = ctx
        ident_sb = T(P, "ident_sb", [128, 128], F32)
        tri01_sb = T(P, "tri01_sb", [128, 128], BF16)
        triu_sb = T(P, "triu_sb", [128, 128], F32)
        tokid_sb = T(P, "tokid_sb", [128, NT], U32)
        tokbigE_sb = T(P, "tokbigE_sb", [128, NT * E], F32)
        bqkvT_sb = T(P, "bqkvT_sb", [128, 3 * D // 128], F32)
        bqkvTk_sb = T(P, "bqkvTk_sb", [128, D // 128], F32)
        bv_sb = T(P, "bv_sb", [1, D], BF16)
        bout_sb = T(P, "bout_sb", [1, D], BF16)
        bs_sb = T(P, "bs_sb", [1, E], F32)
        b1T_sb = T(P, "b1T_sb", [128, E * FT], F32)
        ones_col_f = T(P, "ones_col_f", [128, 1], F32)
        ones_row_f = T(P, "ones_row_f", [1, 128], F32)
        ones_row_b = T(P, "ones_row_b", [1, 128], BF16)
        ws_sb = T(P, "ws_sb", [128, DT * E], F32)
        probs_sb = T(P, "probs_sb", [128, NT * E], F32)
        mask_sb = T(P, "mask_sb", [128, NT * E], F32)
        idxu_sb = T(P, "idxu_sb", [128, NT * E], U32)
        sidx_sb = T(P, "sidx_sb", [128, E * CT + 1], U32)
        st1_sb = T(P, "st1_sb", [128, 8], F32)

        sC = self.SM("sC")
        sG = self.SM("sG")
        sPE = self.SM("sPE")
        sACT = self.SM("sACT")
        sDVE = self.SM("sDVE")
        sQKst = [self.SM("sQKst0"), self.SM("sQKst1")]
        sVst = [self.SM("sVst0"), self.SM("sVst1")]
        sKld = [self.SM("sKld0"), self.SM("sKld1")]
        sXld = [self.SM("sXld0"), self.SM("sXld1")]
        sX1st = self.SM("sX1st")
        sCC = self.SM("sCC")
        sPr = self.SM("sPr")
        sBis = self.SM("sBis")
        sSc = [self.SM(f"sSc{e}") for e in range(E)]
        sSld = [self.SM(f"sSld{e}") for e in range(E)]
        sXg = [self.SM("sXg0"), self.SM("sXg1")]
        sGg = [self.SM("sGg0"), self.SM("sGg1")]
        sW1 = [self.SM("sW10"), self.SM("sW11")]
        sW2 = self.SM("sW2")
        sSA = self.SM("sSA")
        sAld = [self.SM("sAld0"), self.SM("sAld1")]
        sOut = self.SM("sOut")
        on, dma = self.on, self.dma
        regs = {}

        def mkregs(En):
            regs["C1"] = En.to_reg(C - 1)
            regs["S1"] = En.to_reg(S - 1)
            regs["R1"] = En.to_reg(c.NC * E - 1)
        on("gpsimd", (lambda En: mkregs(En)))

        dma("sync", ident_sb[:, :], ident_in, sC)
        dma("sync", tri01_sb[:, :], tri01_in, sC)
        dma("sync", triu_sb[:, :], triu_in, sC)
        dma("sync", tokid_sb[:, :], tokid_in, sC)
        dma("sync", tokbigE_sb[:, :], tokbigE_in, sC)
        dma("sync", bqkvT_sb[:, :], bqkvT_in, sC)
        dma("sync", bs_sb[:, :], bs_row_in, sC)
        dma("sync", ws_sb[:, :].rearrange("p (dt e) -> p dt e", e=E),
            ws_in.rearrange("(dt p) e -> p dt e", p=128), sC)
        dma("sync", sidx_sb[:, E * CT:E * CT + 1], rowidx_in, sC)
        nconst = sC.n

        dma("gpsimd", bv_sb[:, :], bqkv_row_in[:, 2 * D:3 * D], sG)
        dma("gpsimd", bout_sb[:, :], bout_row_in, sG)
        dma("gpsimd", b1T_sb[:, :].rearrange("p (e f) -> p e f", e=E),
            b1T_in.rearrange("e p f -> p e f"), sG)
        dma("gpsimd", slot_dram, slotinit_in, sG)
        nG0 = sG.n

        on("vector", lambda En: En.memset(ones_col_f[:, :], 1.0), inc=sDVE)
        on("vector", lambda En: En.memset(ones_row_f[:, :], 1.0), inc=sDVE)
        on("vector", lambda En: En.memset(ones_row_b[:, :], 1.0), inc=sDVE)
        on("vector", (lambda En: En.tensor_scalar_mul(bqkvTk_sb[:, :], bqkvT_sb[:, D // 128:2 * D // 128], 0.125)),
           waits=[(sC, nconst)], inc=sDVE)
        nset = sDVE.n

        scopeA = ExitStack()
        ctx3_sb = T(scopeA, "ctx3_sb", [128, (H // 2) * S], BF16)
        k_sb0 = T(scopeA, "k_sb0", [128, S], BF16)
        sVp0 = self.SM("sVp0")

        s1 = ExitStack()
        wqkv_sb = T(s1, "wqkv_sb", [128, DT * 3 * D], BF16)
        xTb_sb = T(s1, "xTb_sb", [128, DT * S], BF16)
        stg_sb = [T(s1, f"stg{i}", [128, TQ], BF16) for i in range(2)]
        vst_sb = [T(s1, f"vst{i}", [128, H * 128], BF16) for i in range(2)]

        sGp = [self.SM(f"sGp{i}") for i in range(DT)]
        for dt_ in range(DT):
            dma("gpsimd", wqkv_sb[:, :].rearrange("p (dt o) -> p dt o", o=3 * D)[:, dt_:dt_ + 1, :],
                wqkvT_in.rearrange("(dt p) o -> p dt o", p=128)[:, dt_:dt_ + 1, :], sGp[dt_])
            dma("gpsimd", xTb_sb[:, :].rearrange("p (dt t) -> p dt t", t=S)[:, dt_:dt_ + 1, :],
                xT_in.rearrange("(dt p) t -> p dt t", p=128)[:, dt_:dt_ + 1, :], sGp[dt_])
        nG1 = sG.n

        OT2 = 2 * D // 128
        ot_order = list(range(D // 128, OT2)) + list(range(D // 128))
        p1m = {"it": 0}

        def emit_qk_ot(ot):
            is_k = ot >= D // 128
            for tq in range(S // TQ):
                par = p1m["it"] % 2
                ps = psA[:, par * 1024: par * 1024 + TQ]
                for dt in range(DT):
                    for c0, c1 in chunks(0, TQ):
                        w = []
                        if p1m["it"] == 0 and c0 == 0:
                            w = [(sGp[dt], 32), (sDVE, sDVE.n)]
                        elif dt == 0 and c0 == 0:
                            w = [(sGp[DT - 1], 32), (sDVE, sDVE.n), (sACT, sACT.n)]
                        on("tensor", (lambda En, o=ps[:, c0:c1],
                                      l=wqkv_sb[:, dt * 3 * D + ot * 128: dt * 3 * D + (ot + 1) * 128],
                                      r=xTb_sb[:, dt * S + tq * TQ + c0: dt * S + tq * TQ + c1],
                                      st=(dt == 0), sp=(dt == DT - 1):
                                      En.matmul(o, l, r, start=st, stop=sp)),
                           waits=w, inc=(sPE if (dt == DT - 1 and c1 == TQ) else None))
                bt = bqkvTk_sb[:, ot - D // 128: ot - D // 128 + 1] if is_k else bqkvT_sb[:, ot:ot + 1]
                sc = 0.125 if is_k else 1.0
                on("vector", (lambda En, o=stg_sb[par][:, :], i=ps, b=bt, s=sc:
                              En.tensor_scalar(o, i, b, s, ALU.add, ALU.mult)),
                   waits=[(sPE, sPE.n), (sC, nconst), (sDVE, nset), (sQKst[par], sQKst[par].n)],
                   inc=sDVE)
                dst = (k_dram if is_k else q_dram)
                row0 = (ot - D // 128 if is_k else ot) * 128
                dma("sync" if par == 0 else "gpsimd",
                    dst[row0:row0 + 128, tq * TQ:(tq + 1) * TQ], stg_sb[par][:, :],
                    sQKst[par], waits=[(sDVE, sDVE.n)])
                p1m["it"] += 1

        for ot in ot_order[:D // 128]:
            emit_qk_ot(ot)
            if ot == D // 128:
                p1m["k0"] = [(sQKst[0], sQKst[0].n), (sQKst[1], sQKst[1].n)]
        for i in range(2):
            on("vector", (lambda En, o=vst_sb[i][:, :]: En.memset(o, 0.0)), inc=sDVE)
            on("vector", (lambda En, o=vst_sb[i].ap().rearrange("p (h x) -> p h x", x=128)[:, :, 64:65]:
                          En.memset(o, 1.0)),
               waits=[(sDVE, sDVE.n)], inc=sDVE)
        vst_init = sDVE.n
        for st_i in range(ST):
            par = st_i % 2
            ps = psA[:, par * 1024: par * 1024 + D]
            for dt in range(DT):
                for c0, c1 in chunks(0, D):
                    w = []
                    if dt == 0 and c0 == 0:
                        w = [(sDVE, sDVE.n), (sACT, sACT.n)]
                    on("tensor", (lambda En, o=ps[:, c0:c1],
                                  l=xTb_sb[:, dt * S + st_i * 128: dt * S + (st_i + 1) * 128],
                                  r=wqkv_sb[:, dt * 3 * D + 2 * D + c0: dt * 3 * D + 2 * D + c1],
                                  st=(dt == 0): En.matmul(o, l, r, start=st, stop=False)),
                       waits=w, inc=None)
            for c0, c1 in chunks(0, D):
                on("tensor", (lambda En, o=ps[:, c0:c1], l=ones_row_b[0:1, 0:128], r=bv_sb[0:1, c0:c1]:
                              En.matmul(o, l, r, start=False, stop=True)),
                   waits=([(sDVE, nset), (sG, sG.n)] if (st_i == 0 and c0 == 0) else []),
                   inc=(sPE if c1 == D else None))
            on("scalar", (lambda En, o=vst_sb[par][:, :].rearrange("p (h x) -> p h x", x=128)[:, :, 0:64],
                          i=ps.rearrange("p (h x) -> p h x", x=64):
                          En.activation(o, i, AF.Copy)),
               waits=[(sPE, sPE.n), (sVst[par], sVst[par].n), (sDVE, vst_init)], inc=sACT)
            dma("sync" if par == 0 else "gpsimd",
                vp_dram[st_i], vst_sb[par][:, :], sVst[par], waits=[(sACT, sACT.n)])
        v_all = [(sVst[0], sVst[0].n), (sVst[1], sVst[1].n)]
        for ot in ot_order[D // 128:]:
            emit_qk_ot(ot)
            if ot == 0:
                dma("sync", k_sb0[0:64, :], k_dram[0:64, :], sKld[0], waits=p1m["k0"])
        qk_all = [(sQKst[0], sQKst[0].n), (sQKst[1], sQKst[1].n)]
        self.barrier()
        s1.close()

        s2 = ExitStack()
        k_sb = [k_sb0, T(s2, "k_sb1", [128, S], BF16)]
        vp_sb = [T(s2, f"vp_sb{i}", [128, ST * 128], BF16) for i in range(2)]
        qh_sb = [T(s2, f"qh_sb{i}", [128, S], BF16) for i in range(2)]
        qw_sb = [T(s2, f"qw_sb{i}", [128, S], BF16) for i in range(2)]
        e_sb = [T(s2, f"e_sb{i}", [128, 2 * TB], BF16) for i in range(3)]
        ctxu_sb = [T(s2, f"ctxu{i}", [65, TB], F32) for i in range(4)]
        denp_sb = [T(s2, f"denp{i}", [128, TB // 128], F32) for i in range(4)]
        recp_sb = [T(s2, f"recp{i}", [128, TB // 128], F32) for i in range(4)]
        bden_sb = [T(s2, f"bden{i}", [64, TB], F32) for i in range(4)]
        sNrm4 = [self.SM(f"sNrm{i}") for i in range(4)]
        sPool = self.SM("sPool")

        for tt_ in (k_sb[0], k_sb[1], qh_sb[0], qh_sb[1], qw_sb[0], qw_sb[1]):
            on("vector", (lambda En, o=tt_[64:128, :]: En.memset(o, 0.0)), inc=sDVE)
        pad_init = sDVE.n
        last_head_done = [0, 0]
        ectx_done = {0: 0, 1: 0, 2: 0}

        def head_loads(h, par):
            w0 = qk_all + ([(sPE, last_head_done[par])] if last_head_done[par] else [])
            dma("sync", k_sb[par][0:64, :], k_dram[h * 64:(h + 1) * 64, :], sKld[par],
                waits=w0 + [(sDVE, pad_init)])
            dma("sync", vp_sb[par][:, :],
                vp_dram[:, :, h * 128:(h + 1) * 128].rearrange("st p x -> p st x"),
                sKld[par], waits=v_all)
            dma("sync", qh_sb[par][0:64, :], q_dram[h * 64:(h + 1) * 64, :], sKld[par])

        dma("sync", qh_sb[0][0:64, :], q_dram[0:64, :], sKld[0])
        dma("sync", vp_sb[0][:, :],
            vp_dram[:, :, 0:128].rearrange("st p x -> p st x"), sVp0, waits=v_all)
        nrm = {"gtb": 0, "chains": [], "multv": {}, "mult4": [0, 0, 0, 0],
               "ctxevac": [0, 0], "recipv": [0, 0, 0, 0], "d3v": [0, 0, 0, 0]}

        def chain_advance_all(flush=False):
            if flush:
                while nrm["chains"]:
                    for chn in list(nrm["chains"]):
                        _adv_one(chn)
            else:
                for chn in list(nrm["chains"]):
                    _adv_one(chn)

        def _adv_one(chn):
            st = chn["step"]
            p4 = chn["p4"]
            sN = sNrm4[p4]
            if st == 0:
                dma("sync", nst1_dram[p4], ctxu_sb[p4][64:65, 0:TB], sN,
                    waits=[(sDVE, chn["evacv"])])
            elif st == 1:
                dma("sync", denp_sb[p4][:, :],
                    nst1_dram[p4].rearrange("one (p f) -> (one p) f", p=128), sN,
                    waits=[(sN, chn["v"])])
            elif st == 2:
                on("vector", (lambda En, o=recp_sb[p4][:, :], i=denp_sb[p4][:, :]:
                              En.reciprocal(o, i)),
                   waits=[(sN, chn["v"])], inc=sDVE)
                nrm["recipv"][p4] = sDVE.n
                dma("sync", nst2_dram[p4], recp_sb[p4][:, :], sN,
                    waits=[(sDVE, sDVE.n)])
            elif st == 3:
                dma("sync", bden_sb[p4][:, :], nst2_dram[p4].to_broadcast([64, TB]), sN,
                    waits=[(sN, chn["v"]), (sDVE, nrm["mult4"][p4])])
            elif st == 4:
                on("vector", (lambda En, o=chn["dst"], i0=ctxu_sb[p4][0:64, 0:TB], i1=bden_sb[p4][:, :]:
                              En.tensor_tensor(o, i0, i1, ALU.mult)),
                   waits=[(sN, chn["v"]), (sDVE, sDVE.n)], inc=sDVE)
                nrm["multv"][chn["key"]] = sDVE.n
                nrm["mult4"][p4] = sDVE.n
                nrm["chains"].remove(chn)
                chn["step"] = 5
                return
            chn["step"] += 1
            chn["v"] = sN.n

        exp_done = [0, 0]
        nrm["pend"] = []

        def emit_ctx(pc):
            cslot = pc["cslot"]
            for c0, c1 in chunks(pc["a0"], TB):
                bank_last = pc["t0b"] // 128 + (c0 // 512) * 4 + 3
                on("tensor", (lambda En, o=psB[0:128, cslot * 1024 + c0: cslot * 1024 + c1],
                              l=vp_sb[pc["par"]][:, pc["si"] * 128:(pc["si"] + 1) * 128],
                              rr=e_sb[pc["ep"]][:, c0:c1], st=pc["start"],
                              sp=(pc["si"] == bank_last):
                              En.matmul(o, l, rr, start=st, stop=sp)),
                   waits=(pc["eready"] + [(sDVE, nrm["ctxevac"][cslot])]) if c0 == pc["a0"] else [],
                   inc=(sPE if c1 == TB else None))
            ectx_done[pc["ep"]] = sPE.n
            if pc["stop"]:
                p4 = pc["p4"]
                on("vector", (lambda En, o=ctxu_sb[p4][:, 0:TB], i=psB[0:65, cslot * 1024: cslot * 1024 + TB]:
                              En.tensor_copy(o, i)),
                   waits=[(sPE, sPE.n), (sDVE, nrm["mult4"][p4])],
                   inc=sDVE)
                nrm["ctxevac"][cslot] = sDVE.n
                chn = {"step": 0, "p4": p4, "v": sNrm4[p4].n, "evacv": sDVE.n,
                       "dst": pc["dst"], "key": pc["key"]}
                nrm["chains"].append(chn)
                _adv_one(chn)

        for h in range(H):
            par = h % 2
            if h + 1 < H:
                head_loads(h + 1, (h + 1) % 2)
            for r in range(1, 4):
                qsrc = qh_sb[par] if r == 1 else qw_sb[r % 2]
                for tb in range(NTB):
                    chain_advance_all()
                    chain_advance_all()
                    t0b, t1b = tb * TB, (tb + 1) * TB
                    s_last = t1b // 128
                    gtb = nrm["gtb"]
                    cslot = gtb % 2
                    p4 = gtb % 4
                    if r < 3:
                        dst = qw_sb[(r + 1) % 2][0:64, t0b:t1b]
                    else:
                        dst = ctx3_sb[(h % 2) * 64:(h % 2) * 64 + 64, (h // 2) * S + t0b:(h // 2) * S + t1b]

                    for si in range(s_last):
                        t0 = max(t0b, si * 128)
                        a0 = t0 - t0b
                        ep = si % 3
                        sslot = si % 2
                        first = True
                        for c0, c1 in chunks(a0, TB):
                            w = []
                            if first:
                                if r > 1 and (h, r - 1, tb) not in nrm["multv"]:
                                    while any(pc["key"] == (h, r - 1, tb) for pc in nrm["pend"]):
                                        emit_ctx(nrm["pend"].pop(0))
                                    for chn in list(nrm["chains"]):
                                        if chn["key"] == (h, r - 1, tb):
                                            while chn["step"] < 5:
                                                _adv_one(chn)
                                w = [(sACT, exp_done[sslot]), (sKld[par], sKld[par].n)]
                                if h == 0 and r == 1 and tb == 0 and si == 0:
                                    w.append((sDVE, pad_init))
                                if r > 1:
                                    w.append((sDVE, nrm["multv"][(h, r - 1, tb)]))
                            on("tensor", (lambda En, o=psA[:, sslot * 1024 + c0: sslot * 1024 + c1],
                                          l=k_sb[par][0:128, si * 128:(si + 1) * 128],
                                          rr=qsrc[0:128, t0b + c0: t0b + c1]:
                                          En.matmul(o, l, rr, start=True, stop=True)),
                               waits=w, inc=(sPE if c1 == TB else None))
                            first = False
                        spe_scores = sPE.n
                        while any(pc["ep"] == ep for pc in nrm["pend"]):
                            emit_ctx(nrm["pend"].pop(0))
                        on("scalar", (lambda En, o=e_sb[ep][:, a0:TB], i=psA[:, sslot * 1024 + a0: sslot * 1024 + TB]:
                                      En.activation(o, i, AF.Exp)),
                           waits=[(sPE, spe_scores), (sPE, ectx_done[ep])],
                           inc=sACT)
                        exp_done[sslot] = sACT.n
                        has_diag = si * 128 >= t0b
                        if has_diag:
                            on("gpsimd", (lambda En, o=e_sb[ep][:, a0:a0 + 128], m=tri01_sb[:, :]:
                                          En.tensor_tensor(o, o, m, ALU.mult)),
                               waits=[(sACT, sACT.n), (sC, nconst)], inc=sPool)
                        eready = [(sPool, sPool.n)] if has_diag else [(sACT, sACT.n)]
                        if h == 0 and r == 1 and tb == 0 and si == 0:
                            eready = eready + [(sVp0, 16)]
                        nrm["pend"].append({"si": si, "a0": a0, "ep": ep, "eready": eready,
                                            "cslot": cslot, "par": par, "p4": p4, "t0b": t0b,
                                            "start": si == 0, "stop": si == s_last - 1,
                                            "dst": dst, "key": (h, r, tb)})
                        if len(nrm["pend"]) > 2:
                            emit_ctx(nrm["pend"].pop(0))
                    nrm["gtb"] += 1
            while any(pc["key"][0] == h for pc in nrm["pend"]):
                emit_ctx(nrm["pend"].pop(0))
            last_head_done[par] = sPE.n
        for pc in nrm["pend"]:
            emit_ctx(pc)
        nrm["pend"] = []
        chain_advance_all(flush=True)
        p2_te_end = sPE.n
        ctx3_mult = {tb_: max(nrm["multv"][(h_, 3, tb_)] for h_ in range(H))
                     for tb_ in range(NTB)}
        s2.close()

        s3 = ExitStack()
        wout_sb = T(s3, "wout_sb", [128, DT * D], BF16)
        x_sb = [T(s3, f"x_sb{i}", [128, D], F32) for i in range(2)]
        z_sb = [T(s3, f"z_sb{i}", [128, D], F32) for i in range(2)]
        sq_sb = [T(s3, f"sq_sb{i}", [128, D], F32) for i in range(2)]
        st3_sb = T(s3, "st3_sb", [128, 16], F32)
        x1f_sb = [T(s3, f"x1f{i}", [128, D], F32) for i in range(2)]
        x1b_sb = [T(s3, f"x1b{i}", [128, D], BF16) for i in range(2)]
        x1T_sb = [T(s3, f"x1T{i}", [128, 128], F32) for i in range(2)]
        e8_sb = [T(s3, f"e8_{i}", [128, E], F32) for i in range(2)]
        lgc_sb = [T(s3, f"lgc_{i}", [128, E], F32) for i in range(2)]
        probsT_sb = T(s3, "probsT_sb", [8, S], F32)
        zer_sb = T(s3, "zer_sb", [128, D], F32)

        dma("gpsimd", wout_sb[:, :].rearrange("p (dt o) -> p dt o", o=D),
            woutT_in.rearrange("(dt p) o -> p dt o", p=128), sG,
            waits=[(sPE, p2_te_end)])
        on("vector", lambda En: En.memset(zer_sb[:, :], 0.0), inc=sDVE)

        p3 = {"zadd": [0, 0], "stf": [(0, 0), (0, 0)], "x1use": [0, 0],
              "trv": [0, 0], "lgread": (0, 0), "prtr": 0, "x1v": [0, 0],
              "prv": {}, "zv": [0, 0], "lgv": [0, 0], "mic": [0, 0],
              "rmaxv": [0, 0], "e8rd": [0, 0], "e8v": [0, 0]}
        x1_store_cnt = 0

        def p3_head(tt):
            par = tt % 2
            dma("gpsimd", x_sb[par][:, :], x_in[tt * 128:(tt + 1) * 128, :], sXld[par],
                waits=[(sDVE, p3["zadd"][par])] + ([(sPE, p2_te_end)] if tt < 2 else []))
            ps = psA[:, par * 1024: par * 1024 + D]
            for pr in range(H // 2):
                for c0, c1 in chunks(0, D):
                    w = []
                    if pr == 0 and c0 == 0:
                        w = [(sG, sG.n), (sDVE, p3["zadd"][par]),
                             (sDVE, ctx3_mult[tt * 128 // TB])]
                        if tt < 2:
                            w.append((sDVE, sDVE.n))
                    on("tensor", (lambda En, o=ps[:, c0:c1],
                                  l=ctx3_sb[:, pr * S + tt * 128: pr * S + (tt + 1) * 128],
                                  rr=wout_sb[:, pr * D + c0: pr * D + c1],
                                  st=(pr == 0): En.matmul(o, l, rr, start=st, stop=False)),
                       waits=w, inc=None)
            for c0, c1 in chunks(0, D):
                on("tensor", (lambda En, o=ps[:, c0:c1], l=ones_row_b[0:1, 0:128], rr=bout_sb[0:1, c0:c1]:
                              En.matmul(o, l, rr, start=False, stop=True)),
                   waits=([(sG, sG.n)] if tt == 0 and c0 == 0 else []),
                   inc=(sPE if c1 == D else None))
            on("vector", (lambda En, o=z_sb[par][:, :], i0=ps, i1=x_sb[par][:, :]:
                          En.tensor_tensor(o, i0, i1, ALU.add)),
               waits=[(sPE, sPE.n), (sXld[par], sXld[par].n), (sACT, p3["stf"][par][1])], inc=sDVE)
            zv = sDVE.n
            p3["zadd"][par] = zv
            p3["zv"][par] = zv

        def p3_stats(tt):
            par = tt % 2
            zv = p3["zv"][par]
            on("scalar", (lambda En, o=sq_sb[par][:, :], i=z_sb[par][:, :], a=st3_sb[:, par * 8: par * 8 + 1]:
                          En.activation(o, i, AF.Copy, accum_out=a)),
               waits=[(sDVE, zv)], inc=sACT)
            on("scalar", (lambda En, o=sq_sb[par][:, :], i=z_sb[par][:, :], a=st3_sb[:, par * 8 + 1: par * 8 + 2]:
                          En.activation(o, i, AF.Square, accum_out=a)),
               waits=[(sACT, sACT.n)], inc=sACT)
            stv = sACT.n
            b8 = par * 8
            on("vector", (lambda En, o=st3_sb[:, b8 + 2: b8 + 3], i=st3_sb[:, b8: b8 + 1]:
                          En.tensor_scalar_mul(o, i, 1.0 / D)),
               waits=[(sACT, stv), (sDVE, sDVE.n)], inc=sDVE)
            on("vector", (lambda En, o=st3_sb[:, b8 + 3: b8 + 4], i=st3_sb[:, b8 + 2: b8 + 3], s=st3_sb[:, b8 + 2: b8 + 3]:
                          En.tensor_scalar(o, i, s, -LNEPS, ALU.mult, ALU.add)),
               waits=[(sDVE, sDVE.n)], inc=sDVE)
            on("vector", (lambda En, o=st3_sb[:, b8 + 4: b8 + 5], i=st3_sb[:, b8 + 1: b8 + 2], s=st3_sb[:, b8 + 3: b8 + 4]:
                          En.tensor_scalar(o, i, 1.0 / D, s, ALU.mult, ALU.subtract)),
               waits=[(sDVE, sDVE.n)], inc=sDVE)
            p3["mic"][par] = sDVE.n

        def p3_fin(tt):
            par = tt % 2
            b8 = par * 8
            on("scalar", (lambda En, o=st3_sb[:, b8 + 5: b8 + 6], i=st3_sb[:, b8 + 4: b8 + 5]:
                          En.activation(o, i, AF.Sqrt)),
               waits=[(sDVE, p3["mic"][par])], inc=sACT)
            on("vector", (lambda En, o=st3_sb[:, b8 + 6: b8 + 7], i=st3_sb[:, b8 + 5: b8 + 6]:
                          En.reciprocal(o, i)),
               waits=[(sACT, sACT.n)], inc=sDVE)
            p3["stf"][par] = (sDVE.n, sACT.n)
            on("vector", (lambda En, o=x1f_sb[par][:, :], i=z_sb[par][:, :], m=st3_sb[:, b8 + 2: b8 + 3],
                          rr=st3_sb[:, b8 + 6: b8 + 7]:
                          En.tensor_scalar(o, i, m, rr, ALU.subtract, ALU.mult)),
               waits=[(sDVE, sDVE.n), (sX1st, sX1st.n), (sPE, p3["x1use"][par])], inc=sDVE)
            x1v = sDVE.n
            on("vector", (lambda En, o=x1b_sb[par][:, :], i=x1f_sb[par][:, :]: En.tensor_copy(o, i)),
               waits=[(sDVE, x1v), (sX1st, sX1st.n)], inc=sDVE)
            dma("sync", x1_dram[tt * 128:(tt + 1) * 128, :], x1b_sb[par][:, :], sX1st,
                waits=[(sDVE, sDVE.n)])
            dma("sync", acc_dram[tt * 128:(tt + 1) * 128, :], x1f_sb[par][:, :], sX1st)
            p3["x1v"][par] = x1v

        def p3_rA(tt):
            par = tt % 2
            x1v = p3["x1v"][par]
            for dt in range(DT):
                tp = dt % 2
                on("tensor", (lambda En, o=psB[:, tp * 512: tp * 512 + 128],
                              i=x1f_sb[par][:, dt * 128:(dt + 1) * 128], idn=ident_sb[:, :]:
                              En.transpose(o, i, idn)),
                   waits=[(sDVE, x1v if dt == 0 else 0), (sC, nconst), (sDVE, p3["trv"][tp])] +
                         ([(sDVE, p3["lgread"][0]), (sACT, p3["lgread"][1])] if dt == 0 else []),
                   inc=sPE)
                on("vector", (lambda En, o=x1T_sb[tp][:, :], i=psB[:, tp * 512: tp * 512 + 128]:
                              En.tensor_copy(o, i)),
                   waits=[(sPE, sPE.n)], inc=sDVE)
                p3["trv"][tp] = sDVE.n
                on("tensor", (lambda En, o=psB[:, 1536:1536 + E], l=x1T_sb[tp][:, :],
                              rr=ws_sb[:, dt * E:(dt + 1) * E], st=(dt == 0):
                              En.matmul(o, l, rr, start=st, stop=False)),
                   waits=[(sDVE, sDVE.n)], inc=None)
            on("tensor", (lambda En, o=psB[:, 1536:1536 + E], l=ones_row_f[0:1, 0:128], rr=bs_sb[0:1, 0:E]:
                          En.matmul(o, l, rr, start=False, stop=True)),
               waits=[], inc=sPE)
            p3["lgv"][par] = sPE.n
            p3["x1use"][par] = sPE.n

        def p3_rB(tt):
            par = tt % 2
            b8 = par * 8
            lgv = p3["lgv"][par]
            on("vector", (lambda En, o=st3_sb[:, b8 + 7: b8 + 8], i=psB[:, 1536:1536 + E]:
                          En.reduce_max(o, i, axis=AX.X, negate=True)),
               waits=[(sPE, lgv)], inc=sDVE)
            on("scalar", (lambda En, o=e8_sb[par][:, :], i=psB[:, 1536:1536 + E], b=st3_sb[:, b8 + 7: b8 + 8]:
                          En.activation(o, i, AF.Exp, bias=b)),
               waits=[(sDVE, sDVE.n), (sACT, sACT.n)], inc=sACT)
            p3["lgread"] = (sDVE.n, sACT.n)
            on("vector", (lambda En, o=st3_sb[:, b8: b8 + 1], i=e8_sb[par][:, :]: En.reduce_sum(o, i, axis=AX.X)),
               waits=[(sACT, sACT.n)], inc=sDVE)
            on("vector", (lambda En, o=st3_sb[:, b8 + 1: b8 + 2], i=st3_sb[:, b8: b8 + 1]: En.reciprocal(o, i)),
               waits=[(sDVE, sDVE.n)], inc=sDVE)
            on("vector", (lambda En, o=probs_sb[:, tt * E:(tt + 1) * E], i=e8_sb[par][:, :], s=st3_sb[:, b8 + 1: b8 + 2]:
                          En.tensor_scalar_mul(o, i, s)),
               waits=[(sDVE, sDVE.n)], inc=sDVE)
            p3["prv"][tt] = sDVE.n

        def p3_prtr(tt):
            on("tensor", (lambda En, o=psB[0:E, 1024:1024 + 128], i=probs_sb[:, tt * E:(tt + 1) * E],
                          idn=ident_sb[:, :]: En.transpose(o, i, idn)),
               waits=[(sDVE, p3["prv"][tt]), (sDVE, p3["prtr"])], inc=sPE)
            on("vector", (lambda En, o=probsT_sb[0:E, tt * 128:(tt + 1) * 128], i=psB[0:E, 1024:1024 + 128]:
                          En.tensor_copy(o, i)),
               waits=[(sPE, sPE.n)], inc=sDVE)
            p3["prtr"] = sDVE.n

        for tt in range(NT):
            p3_head(tt)
            if tt >= 1:
                p3_rA(tt - 1)
            p3_stats(tt)
            p3_fin(tt)
            if tt >= 1:
                p3_rB(tt - 1)
            if tt >= 2:
                p3_prtr(tt - 2)
        p3_rA(NT - 1)
        p3_rB(NT - 1)
        p3_prtr(NT - 2)
        p3_prtr(NT - 1)
        probs_done = sDVE.n
        x1_all = sX1st.n

        dma("sync", g_dram[0:S].rearrange("(t p) e -> p t e", p=128),
            probs_sb[:, :].rearrange("p (t e) -> p t e", e=E), sPr,
            waits=[(sDVE, probs_done)])
        dma("sync", g_dram[S:S + C, :], zer_sb[:, 0:C * E // 128], sPr)
        for j in range(CT):
            dma("sync", acc_dram[S + j * 128:S + (j + 1) * 128, :], zer_sb[:, 0:D], sPr,
                waits=[(sX1st, x1_all)] if j == 0 else [])
        dma("sync", pin_dram, probsT_sb[:, :], sPr, waits=[(sDVE, probs_done)])
        pr_all = sPr.n
        self.barrier()
        s3.close()
        scopeA.close()

        s56 = ExitStack()
        w1_sb = [T(s56, f"w1_{i}", [128, DT * WC], BF16) for i in range(2)]
        w2f_sb = T(s56, "w2f_sb", [128, FT * D], BF16)
        b2_sb = T(s56, "b2_sb", [1, D], BF16)
        wst_sb = [T(s56, f"wst{i}", [128, 2048], F32) for i in range(2)]
        sWst = [self.SM("sWst0"), self.SM("sWst1")]
        sB2 = self.SM("sB2")
        NWC = F // WC
        st6 = {"w1_pend": [[], []], "w1_used": [0, 0], "w2_used": 0, "wst_cp": [[], []],
               "sttv": {}, "o2ev": [0, 0],
               "wst_rot": 0,
               "xg_free": 0, "xgT_used": [0, 0], "g_used": [0, 0],
               "xg_ld": {}, "g_ld": {}, "sld": {}, "xgT_dve": {}}

        def emit_w1_load(e, fc):
            wpar = fc % 2
            pend = []
            nq = 2 if DT >= 4 else 1
            cols = DT * WC // nq
            qdt = DT // nq
            for q in range(nq):
                s_ = st6["wst_rot"] % 2
                st6["wst_rot"] += 1
                dma("sync", wst_sb[s_][:, 0:cols].rearrange("p (dt f) -> p dt f", f=WC),
                    w1_in[e].rearrange("(dt p) f -> p dt f", p=128)[:, qdt * q:qdt * (q + 1),
                                                                   fc * WC:(fc + 1) * WC],
                    sWst[s_], waits=list(st6["wst_cp"][s_]))
                cw = [(sWst[s_], sWst[s_].n)]
                if q == 0:
                    cw.append((sPE, st6["w1_used"][wpar]))
                if q % 2 == 0:
                    on("vector", (lambda En, o=w1_sb[wpar][:, q * cols:(q + 1) * cols],
                                  i=wst_sb[s_][:, 0:cols]: En.tensor_copy(o, i)),
                       waits=cw, inc=sDVE)
                    st6["wst_cp"][s_] = [(sDVE, sDVE.n)]
                    pend.append((sDVE, sDVE.n))
                else:
                    on("scalar", (lambda En, o=w1_sb[wpar][:, q * cols:(q + 1) * cols],
                                  i=wst_sb[s_][:, 0:cols]: En.copy(o, i)),
                       waits=cw, inc=sACT)
                    st6["wst_cp"][s_] = [(sACT, sACT.n)]
                    pend.append((sACT, sACT.n))
            st6["w1_pend"][wpar] = pend[-2:]


        def emit_w2_load(e):
            dma("gpsimd", w2f_sb[:, :].rearrange("p (ft d) -> p ft d", d=D),
                w2_in[e].rearrange("(ft p) d -> p ft d", p=128), sW2,
                waits=[(sPE, st6["w2_used"])])
            dma("gpsimd", b2_sb[0:1, :], b2_row_in[e], sB2,
                waits=[(sPE, st6["w2_used"])])
            return sW2.n, sB2.n

        s4 = ExitStack()
        pexp_sb = T(s4, "pexp_sb", [128, S], F32)
        NB = c.N // 128
        pb_sb = T(s4, "pb_sb", [128, NB], F32)
        cmp_sb = T(s4, "cmp_sb", [128, NB], F32)
        lo_sb = [T(s4, f"lo{i}", [128, 1], F32) for i in range(2)]
        hi_sb = [T(s4, f"hi{i}", [128, 1], F32) for i in range(2)]
        mid_sb = T(s4, "mid_sb", [128, 1], F32)
        pc_sb = T(s4, "pc_sb", [128, 1], F32)
        cnt_sb = T(s4, "cnt_sb", [1, 1], F32)
        cntb_sb = T(s4, "cntb_sb", [128, 1], F32)
        sel_sb = T(s4, "sel_sb", [128, 1], U32)
        thr8_sb = T(s4, "thr8_sb", [1, E], F32)
        thrb_sb = T(s4, "thrb_sb", [128, E], F32)
        ranks_sb = T(s4, "ranks_sb", [128, NT * E], F32)
        cs_sb = T(s4, "cs_sb", [1, NT * E], F32)
        base_sb = [T(s4, f"base{i}", [1, NT * E], F32) for i in range(2)]
        idxf_sb = T(s4, "idxf_sb", [128, NT * E], F32)
        masku_sb = T(s4, "masku_sb", [128, NT * E], U32)

        on("gpsimd", (lambda En: En.collective_compute(
            "AllGather", ALU.bypass, replica_groups=[list(range(c.NC))],
            ins=[pin_dram], outs=[pout_dram])),
           waits=[(sPr, pr_all)], inc=sCC)
        for fc_ in range(min(2, NWC)):
            emit_w1_load(0, fc_)
        w2_pend0, b2_ld0 = emit_w2_load(0)
        on("gpsimd", (lambda En: En.indirect_dma_start(
            out=pexp_sb[:, :], out_offset=None, in_=pout_dram,
            in_offset=IndirectOffsetOnAxis(sidx_sb[:, E * CT:E * CT + 1], 0),
            bounds_check=regs["R1"], oob_is_err=False)),
           waits=[(sCC, sCC.n), (sC, nconst)], inc=sBis, inc_by=16)
        on("gpsimd", (lambda En: En.dma_start(out=pe_dram, in_=pexp_sb[0:c.NC, :])),
           waits=[(sBis, sBis.n)], inc=sBis, inc_by=16)
        on("gpsimd", (lambda En: En.dma_start(
            out=pb_sb[:, :], in_=pe_dram.rearrange("r (j f) -> r j f", j=128 // c.NC))),
           waits=[(sBis, sBis.n)], inc=sBis, inc_by=16)
        bis_ld = sBis.n
        on("vector", lambda En: En.memset(lo_sb[0][:, :], 0.0), inc=sDVE)
        on("vector", lambda En: En.memset(hi_sb[0][:, :], 1.0), inc=sDVE)
        for it in range(c.BIS):
            pi, po = it % 2, (it + 1) % 2
            on("vector", (lambda En, o=mid_sb[:, :], i=lo_sb[pi][:, :], hh=hi_sb[pi][:, :]:
                          En.tensor_scalar(o, i, hh, 0.5, ALU.add, ALU.mult)),
               waits=[(sDVE, sDVE.n)], inc=sDVE)
            on("vector", (lambda En, o=cmp_sb[:, :], i=pb_sb[:, :], s=mid_sb[:, :]:
                          En.tensor_single_scalar(o, i, s, ALU.is_gt)),
               waits=[(sDVE, sDVE.n)] + ([(sBis, bis_ld)] if it == 0 else []), inc=sDVE)
            on("vector", (lambda En, o=pc_sb[:, :], i=cmp_sb[:, :]: En.reduce_sum(o, i, axis=AX.X)),
               waits=[(sDVE, sDVE.n)], inc=sDVE)
            on("tensor", (lambda En, o=psB[0:1, 1664:1665], l=pc_sb[:, :], rr=ones_col_f[:, :]:
                          En.matmul(o, l, rr, start=True, stop=True)),
               waits=[(sDVE, sDVE.n)], inc=sPE)
            on("vector", (lambda En, o=cnt_sb[:, :], i=psB[0:1, 1664:1665]: En.tensor_copy(o, i)),
               waits=[(sPE, sPE.n)], inc=sDVE)
            on("tensor", (lambda En, o=psB[:, 1665:1666], l=ones_row_f[0:1, 0:128], rr=cnt_sb[:, :]:
                          En.matmul(o, l, rr, start=True, stop=True)),
               waits=[(sDVE, sDVE.n)], inc=sPE)
            on("vector", (lambda En, o=cntb_sb[:, :], i=psB[:, 1665:1666]: En.tensor_copy(o, i)),
               waits=[(sPE, sPE.n)], inc=sDVE)
            on("vector", (lambda En, o=sel_sb[:, :], i=cntb_sb[:, :]:
                          En.tensor_single_scalar(o, i, float(c.KSEL), ALU.is_ge)),
               waits=[(sDVE, sDVE.n)], inc=sDVE)
            on("vector", (lambda En, o=lo_sb[po][:, :], m=sel_sb[:, :], t=mid_sb[:, :], f=lo_sb[pi][:, :]:
                          En.select(o, m, t, f, add_drain=True)),
               waits=[(sDVE, sDVE.n)], inc=sDVE)
            on("vector", (lambda En, o=hi_sb[po][:, :], m=sel_sb[:, :], t=hi_sb[pi][:, :], f=mid_sb[:, :]:
                          En.select(o, m, t, f, add_drain=True)),
               waits=[(sDVE, sDVE.n)], inc=sDVE)
        fin = c.BIS % 2
        on("gpsimd", (lambda En: En.dma_start(out=tin_dram, in_=lo_sb[fin][0:1, 0:1])),
           waits=[(sDVE, sDVE.n)], inc=sBis, inc_by=16)
        on("gpsimd", (lambda En: En.collective_compute(
            "AllGather", ALU.bypass, replica_groups=[list(range(c.NC))],
            ins=[tin_dram], outs=[tout_dram])),
           waits=[(sBis, sBis.n)], inc=sCC)
        on("gpsimd", (lambda En: En.dma_start(out=thr8_sb[0:1, 0:E],
                                              in_=tout_dram.rearrange("r one -> (one r)")[None, :])),
           waits=[(sCC, sCC.n)], inc=sBis, inc_by=16)
        on("tensor", (lambda En, o=psB[:, 1666:1666 + E], l=ones_row_f[0:1, 0:128], rr=thr8_sb[0:1, 0:E]:
                      En.matmul(o, l, rr, start=True, stop=True)),
           waits=[(sBis, sBis.n)], inc=sPE)
        on("vector", (lambda En, o=thrb_sb[:, :], i=psB[:, 1666:1666 + E]: En.tensor_copy(o, i)),
           waits=[(sPE, sPE.n)], inc=sDVE)
        thr_dve = sDVE.n

        for tt in range(NT):
            on("vector", (lambda En, o=mask_sb[:, tt * E:(tt + 1) * E],
                          i0=probs_sb[:, tt * E:(tt + 1) * E], i1=thrb_sb[:, :]:
                          En.tensor_tensor(o, i0, i1, ALU.is_gt)),
               waits=[(sDVE, thr_dve)] if tt == 0 else [], inc=(sDVE if tt == NT - 1 else None))
        mask_dve = sDVE.n
        for tt in range(NT):
            on("tensor", (lambda En, o=psA[:, tt * E:(tt + 1) * E], l=triu_sb[:, :],
                          rr=mask_sb[:, tt * E:(tt + 1) * E], st=(tt == 0):
                          En.matmul(o, l, rr, start=st, stop=False)),
               waits=([(sDVE, mask_dve), (sC, nconst)] if tt == 0 else []), inc=None)
        on("tensor", (lambda En, o=psB[0:1, 0:NT * E], l=ones_col_f[:, :], rr=mask_sb[:, :]:
                      En.matmul(o, l, rr, start=True, stop=True)),
           waits=[], inc=sPE)
        on("vector", (lambda En, o=cs_sb[:, :], i=psB[0:1, 0:NT * E]: En.tensor_copy(o, i)),
           waits=[(sPE, sPE.n)], inc=sDVE)
        on("vector", (lambda En, o=base_sb[0][0:1, 0:E]: En.memset(o, 0.0)),
           waits=[(sDVE, sDVE.n)], inc=sDVE)
        on("vector", (lambda En, o=base_sb[0][0:1, E:NT * E], i=cs_sb[0:1, 0:(NT - 1) * E]: En.tensor_copy(o, i)),
           waits=[(sDVE, sDVE.n)], inc=sDVE)
        shift, cur = 1, 0
        while shift < NT:
            on("vector", (lambda En, o=base_sb[1 - cur][0:1, shift * E:NT * E],
                          i0=base_sb[cur][0:1, shift * E:NT * E], i1=base_sb[cur][0:1, 0:(NT - shift) * E]:
                          En.tensor_tensor(o, i0, i1, ALU.add)),
               waits=[(sDVE, sDVE.n)], inc=sDVE)
            on("vector", (lambda En, o=base_sb[1 - cur][0:1, 0:shift * E], i=base_sb[cur][0:1, 0:shift * E]:
                          En.tensor_copy(o, i)),
               waits=[(sDVE, sDVE.n)], inc=sDVE)
            cur = 1 - cur
            shift *= 2
        basev = sDVE.n
        for tt in range(NT):
            on("tensor", (lambda En, o=psA[:, tt * E:(tt + 1) * E], l=ones_row_f[0:1, 0:128],
                          rr=base_sb[cur][0:1, tt * E:(tt + 1) * E], sp=(tt == NT - 1):
                          En.matmul(o, l, rr, start=False, stop=sp)),
               waits=([(sDVE, basev)] if tt == 0 else []), inc=(sPE if tt == NT - 1 else None))
        on("vector", (lambda En, o=ranks_sb[:, :], i=psA[:, 0:NT * E]: En.tensor_copy(o, i)),
           waits=[(sPE, sPE.n)], inc=sDVE)
        on("vector", (lambda En, o=masku_sb[:, :], i=mask_sb[:, :]: En.tensor_copy(o, i)),
           waits=[(sDVE, sDVE.n)], inc=sDVE)
        on("vector", (lambda En, o=idxf_sb[:, :], m=masku_sb[:, :], t=ranks_sb[:, :], f=tokbigE_sb[:, :]:
                      En.select(o, m, t, f, add_drain=True)),
           waits=[(sDVE, sDVE.n), (sC, nconst)], inc=sDVE)
        on("vector", (lambda En, o=idxu_sb[:, :], i=idxf_sb[:, :]: En.tensor_copy(o, i)),
           waits=[(sDVE, sDVE.n)], inc=sDVE)
        idx_dve = sDVE.n

        def emit_slot_scatters(e):
            for tt in range(NT):
                on("gpsimd", (lambda En, oi=idxu_sb[:, tt * E + e: tt * E + e + 1],
                              ii=tokid_sb[:, tt:tt + 1], eo=e * C:
                              En.indirect_dma_start(out=slot_dram, out_offset=IndirectOffsetOnAxis(oi, 0),
                                                    in_=ii, in_offset=None, element_offset=eo,
                                                    bounds_check=regs["C1"], oob_is_err=False)),
                   waits=([(sDVE, idx_dve), (sG, sG.n), (sC, nconst)] if tt == 0 else []),
                   inc=sSc[e], inc_by=16)

        emit_slot_scatters(0)
        self.barrier()
        s4.close()

        s6 = ExitStack()
        xg_sb = T(s6, "xg_sb", [128, CT * D], F32)
        xgT_sb = [T(s6, f"xgT{i}", [128, DT * C], BF16) for i in range(2)]
        g_sb = T(s6, "g_sb", [128, 2 * CT], F32)
        h1T_sb = T(s6, "h1T_sb", [128, FT * C], BF16)
        xbh_sb = [T(s6, f"xbh{i}", [128, C], F32) for i in range(2)]
        gt1_sb = [T(s6, f"gt1_{i}", [128, C], F32) for i in range(2)]
        o2_sb = [T(s6, f"o2_{i}", [128, D], F32) for i in range(2)]

        on("vector", lambda En: En.memset(xg_sb[:, :], 0.0), inc=sDVE)
        xg_init = sDVE.n

        def emit_slot_loads(e):
            for j in range(CT):
                dma("sync", sidx_sb[:, e * CT + j: e * CT + j + 1],
                    slot_dram[e * C + j * 128:e * C + (j + 1) * 128, 0:1],
                    sSld[e], waits=([(sSc[e], sSc[e].n)] if j == 0 else []))
            st6["sld"][e] = sSld[e].n

        def emit_gathers(e):
            epar = e % 2
            for j in range(CT):
                on("gpsimd", (lambda En, o=xg_sb[:, j * D:(j + 1) * D],
                              oi=sidx_sb[:, e * CT + j: e * CT + j + 1]:
                              En.indirect_dma_start(out=o, out_offset=None, in_=x1_dram,
                                                    in_offset=IndirectOffsetOnAxis(oi, 0),
                                                    bounds_check=regs["S1"], oob_is_err=False)),
                   waits=([(sSld[e], st6["sld"][e]), (sX1st, x1_all),
                           (sPE, st6["xg_free"]), (sDVE, xg_init)] if j == 0 else []),
                   inc=sXg[epar], inc_by=16)
                on("gpsimd", (lambda En, o=g_sb[:, epar * CT + j: epar * CT + j + 1],
                              oi=sidx_sb[:, e * CT + j: e * CT + j + 1], eo=e:
                              En.indirect_dma_start(out=o, out_offset=None, in_=g_dram,
                                                    in_offset=IndirectOffsetOnAxis(oi, 0), element_offset=eo)),
                   waits=([(sPr, pr_all), (sDVE, st6["g_used"][epar])] if j == 0 else []), inc=sGg[epar], inc_by=16)
            st6["xg_ld"][e] = sXg[epar].n
            st6["g_ld"][e] = sGg[epar].n

        def emit_transposes(e):
            epar = e % 2
            tr_evac = [0, 0, 0, 0]
            for j in range(CT):
                for dt in range(DT):
                    tp = (j * DT + dt) % 4
                    tpo = tp * 512
                    w = [(sACT, tr_evac[tp])]
                    if j == 0 and dt == 0:
                        w += [(sXg[epar], st6["xg_ld"][e]), (sPE, st6["xgT_used"][epar]),
                              (sDVE, sDVE.n), (sACT, sACT.n)]
                    on("tensor", (lambda En, o=psB[:, tpo:tpo + 128],
                                  i=xg_sb[:, j * D + dt * 128: j * D + (dt + 1) * 128], idn=ident_sb[:, :]:
                                  En.transpose(o, i, idn)),
                       waits=w, inc=sPE)
                    on("scalar", (lambda En, o=xgT_sb[epar][:, dt * C + j * 128: dt * C + (j + 1) * 128],
                                  i=psB[:, tpo:tpo + 128]: En.copy(o, i)),
                       waits=[(sPE, sPE.n)], inc=sACT)
                    tr_evac[tp] = sACT.n
            st6["xg_free"] = sPE.n
            st6["xgT_dve"][e] = sACT.n

        def emit_h1(e):
            epar = e % 2
            ps_free = [0, 0]
            gpar_free = [(0, 0), (0, 0)]
            for fc in range(NWC):
                wpar = fc % 2
                for f8 in range(WC // 128):
                    ft = fc * (WC // 128) + f8
                    hp = ft % 2
                    gp = ft % 2
                    ps = psA[:, hp * 1024: hp * 1024 + C]
                    for dt in range(DT):
                        for c0, c1 in chunks(0, C):
                            w = []
                            if dt == 0 and c0 == 0:
                                w = list(st6["w1_pend"][wpar]) + [(sDVE, ps_free[hp])]
                                if ft == 0:
                                    w.append((sACT, st6["xgT_dve"][e]))
                                    w.append((sDVE, sDVE.n))
                            on("tensor", (lambda En, o=ps[:, c0:c1],
                                          l=w1_sb[wpar][:, dt * WC + f8 * 128: dt * WC + (f8 + 1) * 128],
                                          rr=xgT_sb[epar][:, dt * C + c0: dt * C + c1],
                                          st=(dt == 0), sp=(dt == DT - 1):
                                          En.matmul(o, l, rr, start=st, stop=sp)),
                               waits=w, inc=(sPE if (dt == DT - 1 and c1 == C) else None))
                    st6["w1_used"][wpar] = sPE.n
                    dv0, ac0 = gpar_free[gp]
                    on("vector", (lambda En, o=xbh_sb[gp][:, :], i=ps, b=b1T_sb[:, e * FT + ft: e * FT + ft + 1]:
                                  En.tensor_scalar(o, i, b, 0.5, ALU.add, ALU.mult)),
                       waits=[(sPE, sPE.n), (sG, sG.n), (sDVE, dv0), (sACT, ac0)], inc=sDVE)
                    ps_free[hp] = sDVE.n
                    on("scalar", (lambda En, o=gt1_sb[gp][:, :], i=xbh_sb[gp][:, :]:
                                  En.activation(o, i, AF.Square, scale=2.0)),
                       waits=[(sDVE, sDVE.n)], inc=sACT)
                    on("vector", (lambda En, o=gt1_sb[gp][:, :], i=gt1_sb[gp][:, :]:
                                  En.tensor_scalar(o, i, GELU_C, 1.0, ALU.mult, ALU.add)),
                       waits=[(sACT, sACT.n)], inc=sDVE)
                    on("vector", (lambda En, o=gt1_sb[gp][:, :], i0=xbh_sb[gp][:, :], i1=gt1_sb[gp][:, :]:
                                  En.tensor_mul(o, i0, i1)),
                       waits=[(sDVE, sDVE.n)], inc=sDVE)
                    on("scalar", (lambda En, o=gt1_sb[gp][:, :], i=gt1_sb[gp][:, :]:
                                  En.activation(o, i, AF.Tanh, scale=GELU_S2)),
                       waits=[(sDVE, sDVE.n)], inc=sACT)
                    on("vector", (lambda En, o=h1T_sb[:, ft * C:(ft + 1) * C], i0=gt1_sb[gp][:, :], i1=xbh_sb[gp][:, :]:
                                  En.scalar_tensor_tensor(o, i0, 1.0, i1, ALU.add, ALU.mult)),
                       waits=[(sACT, sACT.n)], inc=sDVE)
                    st6["sttv"][ft] = sDVE.n
                    gpar_free[gp] = (sDVE.n, sACT.n)
                if fc + 2 < NWC:
                    emit_w1_load(e, fc + 2)
            st6["xgT_used"][epar] = sPE.n
            return sDVE.n



        def emit_h2(e, h1_dve, w2_pend, b2_ld):
            epar = e % 2
            for ct in range(CT):
                cp = ct % 2
                ps = psB[:, cp * 1024: cp * 1024 + D]
                for ft in range(FT):
                    for c0, c1 in chunks(0, D):
                        w = []
                        if ft == 0 and c0 == 0:
                            w = [(sW2, w2_pend), (sDVE, st6["o2ev"][cp])]
                        if ct == 0 and c0 == 0:
                            w.append((sDVE, st6["sttv"][ft]))
                        on("tensor", (lambda En, o=ps[:, c0:c1],
                                      l=h1T_sb[:, ft * C + ct * 128: ft * C + ct * 128 + 128],
                                      rr=w2f_sb[:, ft * D + c0: ft * D + c1],
                                      st=(ft == 0), sp=False:
                                      En.matmul(o, l, rr, start=st, stop=sp)),
                           waits=w, inc=None)
                for c0, c1 in chunks(0, D):
                    on("tensor", (lambda En, o=ps[:, c0:c1], l=ones_row_b[0:1, 0:128],
                                  rr=b2_sb[0:1, c0:c1]:
                                  En.matmul(o, l, rr, start=False, stop=True)),
                       waits=([(sB2, b2_ld)] if (ct == 0 and c0 == 0) else []),
                       inc=(sPE if c1 == D else None))
                on("vector", (lambda En, o=o2_sb[cp][:, :], i=ps, s=g_sb[:, epar * CT + ct: epar * CT + ct + 1]:
                              En.tensor_scalar_mul(o, i, s)),
                   waits=[(sPE, sPE.n), (sGg[epar], st6["g_ld"][e]), (sSA, sSA.n)], inc=sDVE)
                st6["o2ev"][cp] = sDVE.n
                on("gpsimd", (lambda En, ii=o2_sb[cp][:, :],
                              oi=sidx_sb[:, e * CT + ct: e * CT + ct + 1]:
                              En.indirect_dma_start(out=acc_dram, out_offset=IndirectOffsetOnAxis(oi, 0),
                                                    in_=ii, in_offset=None, compute_op=ALU.add)),
                   waits=[(sDVE, sDVE.n), (sSA, sSA.n), (sPr, pr_all)], inc=sSA, inc_by=16)
            st6["w2_used"] = sPE.n
            st6["g_used"][epar] = sDVE.n

        emit_slot_loads(0)
        emit_gathers(0)
        w2_pend, b2_ld = w2_pend0, b2_ld0
        nxt = {}
        for e in range(E):
            emit_transposes(e)
            if e > 0:
                w2_pend, b2_ld = emit_w2_load(e)
            h1_dve = emit_h1(e)
            if e + 1 < E:
                emit_slot_scatters(e + 1)
                emit_slot_loads(e + 1)
                emit_gathers(e + 1)
                for fc_ in range(min(2, NWC)):
                    emit_w1_load(e + 1, fc_)
            emit_h2(e, h1_dve, w2_pend, b2_ld)
        sa_all = sSA.n
        self.barrier()
        s6.close()
        s56.close()

        s7 = ExitStack()
        a_sb = [T(s7, f"a_{i}", [128, D], F32) for i in range(4)]
        sq7_sb = [T(s7, f"sq7_{i}", [128, D], F32) for i in range(2)]
        out_sb = [T(s7, f"out_{i}", [128, D], F32) for i in range(4)]
        st7_sb = T(s7, "st7_sb", [128, 32], F32)
        sAld4 = [self.SM(f"sAld4_{i}") for i in range(4)]
        sOut2 = [self.SM(f"sOut2_{i}") for i in range(4)]
        p7 = {"apl": [0, 0, 0, 0], "stdve": [0, 0, 0, 0], "stact": [0, 0],
              "outst": [0, 0, 0, 0], "mic": [0, 0, 0, 0]}

        def p7_a(tt):
            p4_ = tt % 4
            p2_ = tt % 2
            b8 = p4_ * 8
            dma("sync" if p4_ % 2 == 0 else "gpsimd",
                a_sb[p4_][:, :], acc_dram[tt * 128:(tt + 1) * 128, :], sAld4[p4_],
                waits=([(sSA, sa_all)] if tt == 0 else []) + [(sDVE, p7["apl"][p4_])])
            aldv = sAld4[p4_].n
            on("scalar", (lambda En, o=sq7_sb[p2_][:, :], i=a_sb[p4_][:, :], a=st7_sb[:, b8: b8 + 1]:
                          En.activation(o, i, AF.Copy, accum_out=a)),
               waits=[(sAld4[p4_], aldv), (sDVE, p7["stdve"][p4_]),
                      (sACT, p7["stact"][p2_])], inc=sACT)
            on("scalar", (lambda En, o=sq7_sb[p2_][:, :], i=a_sb[p4_][:, :], a=st7_sb[:, b8 + 1: b8 + 2]:
                          En.activation(o, i, AF.Square, accum_out=a)),
               waits=[(sACT, sACT.n)], inc=sACT)
            p7["stact"][p2_] = sACT.n
            on("vector", (lambda En, o=st7_sb[:, b8 + 2: b8 + 3], i=st7_sb[:, b8: b8 + 1]:
                          En.tensor_scalar_mul(o, i, 1.0 / D)),
               waits=[(sACT, sACT.n)], inc=sDVE)
            on("vector", (lambda En, o=st7_sb[:, b8 + 3: b8 + 4], i=st7_sb[:, b8 + 2: b8 + 3], s=st7_sb[:, b8 + 2: b8 + 3]:
                          En.tensor_scalar(o, i, s, -LNEPS, ALU.mult, ALU.add)),
               waits=[(sDVE, sDVE.n)], inc=sDVE)
            on("vector", (lambda En, o=st7_sb[:, b8 + 4: b8 + 5], i=st7_sb[:, b8 + 1: b8 + 2], s=st7_sb[:, b8 + 3: b8 + 4]:
                          En.tensor_scalar(o, i, 1.0 / D, s, ALU.mult, ALU.subtract)),
               waits=[(sDVE, sDVE.n)], inc=sDVE)
            p7["mic"][p4_] = sDVE.n

        def p7_b(tt):
            p4_ = tt % 4
            p2_ = tt % 2
            b8 = p4_ * 8
            on("scalar", (lambda En, o=st7_sb[:, b8 + 5: b8 + 6], i=st7_sb[:, b8 + 4: b8 + 5]:
                          En.activation(o, i, AF.Sqrt)),
               waits=[(sDVE, p7["mic"][p4_])], inc=sACT)
            on("vector", (lambda En, o=st7_sb[:, b8 + 6: b8 + 7], i=st7_sb[:, b8 + 5: b8 + 6]:
                          En.reciprocal(o, i)),
               waits=[(sACT, sACT.n)], inc=sDVE)
            p7["stdve"][p4_] = sDVE.n
            on("vector", (lambda En, o=out_sb[p4_][:, :], i=a_sb[p4_][:, :], m=st7_sb[:, b8 + 2: b8 + 3],
                          rr=st7_sb[:, b8 + 6: b8 + 7]:
                          En.tensor_scalar(o, i, m, rr, ALU.subtract, ALU.mult)),
               waits=[(sDVE, sDVE.n), (sOut2[p4_], p7["outst"][p4_])], inc=sDVE)
            p7["apl"][p4_] = sDVE.n
            dma("gpsimd" if p4_ % 2 == 0 else "sync",
                y_out[tt * 128:(tt + 1) * 128, :], out_sb[p4_][:, :], sOut2[p4_],
                waits=[(sDVE, sDVE.n)])
            p7["outst"][p4_] = sOut2[p4_].n

        for g in range(NT // 4):
            for i_ in range(4):
                p7_a(4 * g + i_)
            for i_ in range(4):
                p7_b(4 * g + i_)
        self.on("sync", (lambda En: None),
                waits=[(sOut2[i_], sOut2[i_].n) for i_ in range(4)])

        with nc.Block() as block:
            for eng in self.ENGS:
                getattr(block, eng)(self._runner(eng))
        s7.close()
        ctx.close()
        return nc

    def ln_stats(self, zt, sqt, st, zv, sPE, sACT, sDVE, in_ready=None):
        on = self.on
        D = self.cfg.D
        w0 = in_ready if in_ready is not None else [(sDVE, zv)]
        on("scalar", (lambda En, o=sqt[:, :], i=zt[:, :]: En.activation(o, i, AF.Square)),
           waits=w0 + [(sDVE, sDVE.n)], inc=sACT)
        on("vector", (lambda En, o=st[:, 0:1], i=zt[:, :]: En.reduce_sum(o, i, axis=AX.X)),
           waits=w0 + [(sDVE, sDVE.n)], inc=sDVE)
        on("vector", (lambda En, o=st[:, 1:2], i=sqt[:, :]: En.reduce_sum(o, i, axis=AX.X)),
           waits=[(sACT, sACT.n), (sDVE, sDVE.n)], inc=sDVE)
        on("vector", (lambda En, o=st[:, 2:3], i=st[:, 0:1]: En.tensor_scalar_mul(o, i, 1.0 / D)),
           waits=[(sDVE, sDVE.n)], inc=sDVE)
        on("vector", (lambda En, o=st[:, 3:4], i=st[:, 2:3]:
                      En.tensor_scalar(o, i, st[:, 2:3], -LNEPS, ALU.mult, ALU.add)),
           waits=[(sDVE, sDVE.n)], inc=sDVE)
        on("vector", (lambda En, o=st[:, 4:5], i=st[:, 1:2], s1=st[:, 3:4]:
                      En.tensor_scalar(o, i, 1.0 / D, s1, ALU.mult, ALU.subtract)),
           waits=[(sDVE, sDVE.n)], inc=sDVE)
        on("scalar", (lambda En, o=st[:, 5:6], i=st[:, 4:5]: En.activation(o, i, AF.Sqrt)),
           waits=[(sDVE, sDVE.n)], inc=sACT)
        on("vector", (lambda En, o=st[:, 6:7], i=st[:, 5:6]: En.reciprocal(o, i)),
           waits=[(sACT, sACT.n)], inc=sDVE)

    def _runner(self, eng):
        thunks = self.q[eng]
        def run(E):
            for t in thunks:
                t(E)
        return run


def host_inputs(cfg: Cfg, full):
    c = cfg
    E, C, S, D, NT = c.E, c.C, c.S, c.D, c.NT
    x = np.ascontiguousarray(np.asarray(full["L0_attn_x"], np.float32))
    wqkvT = np.ascontiguousarray(np.asarray(full["L0_attn_Wqkv_w"], np.float32).T)
    bqkv = np.ascontiguousarray(np.asarray(full["L0_attn_Wqkv_b"], np.float32))
    bqkvT = np.ascontiguousarray(bqkv.reshape(3 * D // 128, 128).T)
    woutT = np.ascontiguousarray(np.asarray(full["L0_attn_out_w"], np.float32).T)
    bout = np.ascontiguousarray(np.asarray(full["L0_attn_out_b"], np.float32).reshape(1, D))
    ws = np.ascontiguousarray(np.asarray(full["L0_ffn_Ws"], np.float32))
    bs = np.ascontiguousarray(np.asarray(full["L0_ffn_bs"], np.float32).reshape(1, E))
    w1 = np.ascontiguousarray(np.asarray(full["L0_ffn_W1"], np.float32))
    b1 = np.asarray(full["L0_ffn_b1"], np.float32)
    b1T = np.ascontiguousarray(b1.reshape(E, c.F // 128, 128).transpose(0, 2, 1))
    w2 = np.ascontiguousarray(np.asarray(full["L0_ffn_W2"], np.float32))
    b2 = np.ascontiguousarray(np.asarray(full["L0_ffn_b2"], np.float32).reshape(E, 1, D))
    ident = np.eye(128, dtype=np.float32)
    triu = np.triu(np.ones((128, 128), np.float32), 1)
    import ml_dtypes
    tri01 = np.triu(np.ones((128, 128), np.float32), 0).astype(ml_dtypes.bfloat16)
    tokid = (np.arange(NT)[None, :] * 128 + np.arange(128)[:, None]).astype(np.uint32)
    tokbigE = np.empty((128, NT * E), np.float32)
    for tt in range(NT):
        for e in range(E):
            tokbigE[:, tt * E + e] = BIG + tt * 128 + np.arange(128)
    slotinit = np.empty((E * C, 1), np.uint32)
    for e in range(E):
        slotinit[e * C:(e + 1) * C, 0] = S + np.arange(C)
    maps = []
    for cid in range(c.NC):
        rowidx = np.full((128, 1), 0xFFFFF0, np.uint32)
        rowidx[:c.NC, 0] = np.arange(c.NC) * E + cid
        maps.append({
            "x": x[cid], "xT": np.ascontiguousarray(x[cid].T),
            "wqkvT": wqkvT, "bqkvT": bqkvT, "bqkv_row": bqkv.reshape(1, 3 * D),
            "woutT": woutT, "bout_row": bout, "ws": ws, "bs_row": bs,
            "w1": w1, "b1T": b1T, "w2": w2, "b2_row": b2,
            "ident": ident, "triu": triu,
            "tri01": tri01, "tokid": tokid, "tokbigE": tokbigE, "slotinit": slotinit,
            "rowidx": rowidx,
        })
    return maps


_NC_CACHE = {}

def _get_nc():
    if "nc" not in _NC_CACHE:
        cfg = Cfg(S=4096, D=1024, H=16, F=4096, E=8, NC=8, TB=1024, C=640, BIS=30)
        mk = MK(cfg)
        _NC_CACHE["cfg"] = cfg
        _NC_CACHE["nc"] = mk.build()
    return _NC_CACHE["cfg"], _NC_CACHE["nc"]


def kernel(L0_attn_x, L0_attn_Wqkv_w, L0_attn_Wqkv_b, L0_attn_out_w, L0_attn_out_b,
           L0_ffn_Ws, L0_ffn_bs, L0_ffn_W1, L0_ffn_b1, L0_ffn_W2, L0_ffn_b2):
    from concourse.bass_utils import run_bass_kernel_spmd
    cfg, nc = _get_nc()
    full = {
        "L0_attn_x": L0_attn_x, "L0_attn_Wqkv_w": L0_attn_Wqkv_w,
        "L0_attn_Wqkv_b": L0_attn_Wqkv_b, "L0_attn_out_w": L0_attn_out_w,
        "L0_attn_out_b": L0_attn_out_b, "L0_ffn_Ws": L0_ffn_Ws,
        "L0_ffn_bs": L0_ffn_bs, "L0_ffn_W1": L0_ffn_W1, "L0_ffn_b1": L0_ffn_b1,
        "L0_ffn_W2": L0_ffn_W2, "L0_ffn_b2": L0_ffn_b2,
    }
    maps = host_inputs(cfg, full)
    res = run_bass_kernel_spmd(nc, maps, core_ids=list(range(cfg.NC)))
    out = np.stack([np.asarray(res.results[c]["y"], dtype=np.float32) for c in range(cfg.NC)], axis=0)
    return out


# version 45
# speedup vs baseline: 1.0022x; 1.0022x over previous
import numpy as np
from contextlib import ExitStack
import concourse.bass as bass
import concourse.mybir as mybir
from concourse.bass import IndirectOffsetOnAxis

F32 = mybir.dt.float32
BF16 = mybir.dt.bfloat16
U32 = mybir.dt.uint32
AF = mybir.ActivationFunctionType
ALU = mybir.AluOpType
AX = mybir.AxisListType

GELU_C = 0.044715
GELU_S2 = 1.5957691216057308
LNEPS = 1e-5
BIG = float(1 << 20)


class Cfg:
    def __init__(self, S=4096, D=1024, H=16, F=4096, E=8, NC=8, TB=1024, C=640, BIS=30):
        self.S, self.D, self.H, self.F, self.E, self.NC = S, D, H, F, E, NC
        self.TB, self.C, self.BIS = TB, C, BIS
        self.HD = 64
        assert D == H * self.HD
        self.DT = D // 128
        self.NT = S // 128
        self.FT = F // 128
        self.ST = S // 128
        self.CT = C // 128
        self.NTB = S // TB
        self.N = NC * S
        self.KSEL = self.N // E
        self.TQ = min(1024, S)
        self.WC = min(512, F)
        assert C % 128 == 0 and TB % 512 == 0 and S % TB == 0 and TB <= 1024
        assert self.N % 128 == 0 and (self.N // 128) >= 1
        assert 128 % NC == 0


class Sem:
    def __init__(self, ctx, nc, name):
        self.h = ctx.enter_context(nc.semaphore(name))
        self.n = 0


def chunks(lo, hi, step=512):
    c = lo
    while c < hi:
        c1 = min(hi, (c // step) * step + step)
        yield c, c1
        c = c1


class MK:
    ENGS = ("tensor", "scalar", "vector", "gpsimd", "sync")

    def __init__(self, cfg: Cfg):
        self.cfg = cfg
        self.q = {e: [] for e in self.ENGS}
        self.sems = []

    def on(self, eng, build, waits=(), inc=None, inc_by=1):
        waits = tuple((s.h, v) for (s, v) in waits if v > 0)
        def thunk(E, build=build, waits=waits, inc=inc, inc_by=inc_by):
            for h, v in waits:
                E.wait_ge(h, v)
            inst = build(E)
            if inc is not None and inst is not None:
                inst.then_inc(inc.h, inc_by)
        self.q[eng].append(thunk)
        if inc is not None:
            inc.n += inc_by
        return inc.n if inc else None

    def barrier(self):
        vals = [(s, s.n) for s in self.sems if s.n > 0]
        for eng in self.ENGS:
            self.on(eng, (lambda En: None), waits=vals)

    def SM(self, name):
        s = Sem(self.ctx, self.nc, name)
        self.sems.append(s)
        return s

    def dma(self, eng, out, in_, sem, waits=()):
        self.on(eng, (lambda En, o=out, i=in_: En.dma_start(out=o, in_=i)),
                waits=waits, inc=sem, inc_by=16)

    def build(self):
        c = self.cfg
        S, D, H, F, E, TB, C = c.S, c.D, c.H, c.F, c.E, c.TB, c.C
        DT, NT, FT, ST, CT, NTB, TQ, WC = c.DT, c.NT, c.FT, c.ST, c.CT, c.NTB, c.TQ, c.WC
        nc = bass.Bass("TRN2", target_bir_lowering=False)
        self.nc = nc
        self.ctx = ExitStack()
        ctx = self.ctx

        ap = lambda t: t.ap()
        x_in = ap(nc.dram_tensor("x", [S, D], F32, kind="ExternalInput"))
        xT_in = ap(nc.dram_tensor("xT", [D, S], F32, kind="ExternalInput"))
        wqkvT_in = ap(nc.dram_tensor("wqkvT", [D, 3 * D], F32, kind="ExternalInput"))
        bqkvT_in = ap(nc.dram_tensor("bqkvT", [128, 3 * D // 128], F32, kind="ExternalInput"))
        bqkv_row_in = ap(nc.dram_tensor("bqkv_row", [1, 3 * D], F32, kind="ExternalInput"))
        woutT_in = ap(nc.dram_tensor("woutT", [D, D], F32, kind="ExternalInput"))
        bout_row_in = ap(nc.dram_tensor("bout_row", [1, D], F32, kind="ExternalInput"))
        ws_in = ap(nc.dram_tensor("ws", [D, E], F32, kind="ExternalInput"))
        bs_row_in = ap(nc.dram_tensor("bs_row", [1, E], F32, kind="ExternalInput"))
        w1_in = ap(nc.dram_tensor("w1", [E, D, F], F32, kind="ExternalInput"))
        b1T_in = ap(nc.dram_tensor("b1T", [E, 128, F // 128], F32, kind="ExternalInput"))
        w2_in = ap(nc.dram_tensor("w2", [E, F, D], F32, kind="ExternalInput"))
        b2_row_in = ap(nc.dram_tensor("b2_row", [E, 1, D], F32, kind="ExternalInput"))
        ident_in = ap(nc.dram_tensor("ident", [128, 128], F32, kind="ExternalInput"))
        tri01_in = ap(nc.dram_tensor("tri01", [128, 128], BF16, kind="ExternalInput"))
        triu_in = ap(nc.dram_tensor("triu", [128, 128], F32, kind="ExternalInput"))
        tokid_in = ap(nc.dram_tensor("tokid", [128, NT], U32, kind="ExternalInput"))
        tokbigE_in = ap(nc.dram_tensor("tokbigE", [128, NT * E], F32, kind="ExternalInput"))
        slotinit_in = ap(nc.dram_tensor("slotinit", [E * C, 1], U32, kind="ExternalInput"))
        rowidx_in = ap(nc.dram_tensor("rowidx", [128, 1], U32, kind="ExternalInput"))
        y_out = ap(nc.dram_tensor("y", [S, D], F32, kind="ExternalOutput"))

        q_dram = ap(nc.dram_tensor("q_dram", [D, S], BF16))
        k_dram = ap(nc.dram_tensor("k_dram", [D, S], BF16))
        vp_dram = ap(nc.dram_tensor("vp_dram", [ST, 128, H * 128], BF16))
        x1_dram = ap(nc.dram_tensor("x1_dram", [S, D], BF16))
        acc_dram = ap(nc.dram_tensor("acc_dram", [S + C, D], F32))
        g_dram = ap(nc.dram_tensor("g_dram", [S + C, E], F32))
        slot_dram = ap(nc.dram_tensor("slot_dram", [E * C, 1], U32))
        pin_dram = ap(nc.dram_tensor("pin_dram", [E, S], F32))
        pout_dram = ap(nc.dram_tensor("pout_dram", [c.NC * E, S], F32))
        pe_dram = ap(nc.dram_tensor("pe_dram", [c.NC, S], F32))
        tin_dram = ap(nc.dram_tensor("tin_dram", [1, 1], F32))
        nst1_dram = [ap(nc.dram_tensor(f"nst1_{i}", [1, TB], F32)) for i in range(4)]
        nst2_dram = [ap(nc.dram_tensor(f"nst2_{i}", [1, TB], F32)) for i in range(4)]
        tout_dram = ap(nc.dram_tensor("tout_dram", [c.NC, 1], F32))

        T = lambda scope, name, shape, dt: scope.enter_context(nc.sbuf_tensor(name, shape, dt))

        psA = ctx.enter_context(nc.psum_tensor("psA", [128, 2048], F32))
        psB = ctx.enter_context(nc.psum_tensor("psB", [128, 2048], F32))

        P = ctx
        ident_sb = T(P, "ident_sb", [128, 128], F32)
        tri01_sb = T(P, "tri01_sb", [128, 128], BF16)
        triu_sb = T(P, "triu_sb", [128, 128], F32)
        tokid_sb = T(P, "tokid_sb", [128, NT], U32)
        tokbigE_sb = T(P, "tokbigE_sb", [128, NT * E], F32)
        bqkvT_sb = T(P, "bqkvT_sb", [128, 3 * D // 128], F32)
        bqkvTk_sb = T(P, "bqkvTk_sb", [128, D // 128], F32)
        bv_sb = T(P, "bv_sb", [1, D], BF16)
        bout_sb = T(P, "bout_sb", [1, D], BF16)
        bs_sb = T(P, "bs_sb", [1, E], F32)
        b1T_sb = T(P, "b1T_sb", [128, E * FT], F32)
        ones_col_f = T(P, "ones_col_f", [128, 1], F32)
        ones_row_f = T(P, "ones_row_f", [1, 128], F32)
        ones_row_b = T(P, "ones_row_b", [1, 128], BF16)
        ws_sb = T(P, "ws_sb", [128, DT * E], F32)
        probs_sb = T(P, "probs_sb", [128, NT * E], F32)
        mask_sb = T(P, "mask_sb", [128, NT * E], F32)
        idxu_sb = T(P, "idxu_sb", [128, NT * E], U32)
        sidx_sb = T(P, "sidx_sb", [128, E * CT + 1], U32)
        st1_sb = T(P, "st1_sb", [128, 8], F32)

        sC = self.SM("sC")
        sG = self.SM("sG")
        sPE = self.SM("sPE")
        sACT = self.SM("sACT")
        sDVE = self.SM("sDVE")
        sQKst = [self.SM("sQKst0"), self.SM("sQKst1")]
        sVst = [self.SM("sVst0"), self.SM("sVst1")]
        sKld = [self.SM("sKld0"), self.SM("sKld1")]
        sXld = [self.SM("sXld0"), self.SM("sXld1")]
        sX1st = self.SM("sX1st")
        sCC = self.SM("sCC")
        sPr = self.SM("sPr")
        sBis = self.SM("sBis")
        sSc = [self.SM(f"sSc{e}") for e in range(E)]
        sSld = [self.SM(f"sSld{e}") for e in range(E)]
        sXg = [self.SM("sXg0"), self.SM("sXg1")]
        sGg = [self.SM("sGg0"), self.SM("sGg1")]
        sW1 = [self.SM("sW10"), self.SM("sW11")]
        sW2 = self.SM("sW2")
        sSA = self.SM("sSA")
        sAld = [self.SM("sAld0"), self.SM("sAld1")]
        sOut = self.SM("sOut")
        on, dma = self.on, self.dma
        regs = {}

        def mkregs(En):
            regs["C1"] = En.to_reg(C - 1)
            regs["S1"] = En.to_reg(S - 1)
            regs["R1"] = En.to_reg(c.NC * E - 1)
        on("gpsimd", (lambda En: mkregs(En)))

        dma("sync", ident_sb[:, :], ident_in, sC)
        dma("sync", tri01_sb[:, :], tri01_in, sC)
        dma("sync", triu_sb[:, :], triu_in, sC)
        dma("sync", tokid_sb[:, :], tokid_in, sC)
        dma("sync", tokbigE_sb[:, :], tokbigE_in, sC)
        dma("sync", bqkvT_sb[:, :], bqkvT_in, sC)
        dma("sync", bs_sb[:, :], bs_row_in, sC)
        dma("sync", ws_sb[:, :].rearrange("p (dt e) -> p dt e", e=E),
            ws_in.rearrange("(dt p) e -> p dt e", p=128), sC)
        dma("sync", sidx_sb[:, E * CT:E * CT + 1], rowidx_in, sC)
        nconst = sC.n

        dma("gpsimd", bv_sb[:, :], bqkv_row_in[:, 2 * D:3 * D], sG)
        dma("gpsimd", bout_sb[:, :], bout_row_in, sG)
        dma("gpsimd", b1T_sb[:, :].rearrange("p (e f) -> p e f", e=E),
            b1T_in.rearrange("e p f -> p e f"), sG)
        dma("gpsimd", slot_dram, slotinit_in, sG)
        nG0 = sG.n

        on("vector", lambda En: En.memset(ones_col_f[:, :], 1.0), inc=sDVE)
        on("vector", lambda En: En.memset(ones_row_f[:, :], 1.0), inc=sDVE)
        on("vector", lambda En: En.memset(ones_row_b[:, :], 1.0), inc=sDVE)
        on("vector", (lambda En: En.tensor_scalar_mul(bqkvTk_sb[:, :], bqkvT_sb[:, D // 128:2 * D // 128], 0.125)),
           waits=[(sC, nconst)], inc=sDVE)
        nset = sDVE.n

        scopeA = ExitStack()
        ctx3_sb = T(scopeA, "ctx3_sb", [128, (H // 2) * S], BF16)
        k_sb0 = T(scopeA, "k_sb0", [128, S], BF16)
        sVp0 = self.SM("sVp0")

        s1 = ExitStack()
        wqkv_sb = T(s1, "wqkv_sb", [128, DT * 3 * D], BF16)
        xTb_sb = T(s1, "xTb_sb", [128, DT * S], BF16)
        stg_sb = [T(s1, f"stg{i}", [128, TQ], BF16) for i in range(2)]
        vst_sb = [T(s1, f"vst{i}", [128, H * 128], BF16) for i in range(2)]

        sGp = [self.SM(f"sGp{i}") for i in range(DT)]
        for dt_ in range(DT):
            dma("gpsimd", wqkv_sb[:, :].rearrange("p (dt o) -> p dt o", o=3 * D)[:, dt_:dt_ + 1, :],
                wqkvT_in.rearrange("(dt p) o -> p dt o", p=128)[:, dt_:dt_ + 1, :], sGp[dt_])
            dma("gpsimd", xTb_sb[:, :].rearrange("p (dt t) -> p dt t", t=S)[:, dt_:dt_ + 1, :],
                xT_in.rearrange("(dt p) t -> p dt t", p=128)[:, dt_:dt_ + 1, :], sGp[dt_])
        nG1 = sG.n

        OT2 = 2 * D // 128
        ot_order = list(range(D // 128, OT2)) + list(range(D // 128))
        p1m = {"it": 0}

        def emit_qk_ot(ot):
            is_k = ot >= D // 128
            for tq in range(S // TQ):
                par = p1m["it"] % 2
                ps = psA[:, par * 1024: par * 1024 + TQ]
                for dt in range(DT):
                    for c0, c1 in chunks(0, TQ):
                        w = []
                        if p1m["it"] == 0 and c0 == 0:
                            w = [(sGp[dt], 32), (sDVE, sDVE.n)]
                        elif dt == 0 and c0 == 0:
                            w = [(sGp[DT - 1], 32), (sDVE, sDVE.n), (sACT, sACT.n)]
                        on("tensor", (lambda En, o=ps[:, c0:c1],
                                      l=wqkv_sb[:, dt * 3 * D + ot * 128: dt * 3 * D + (ot + 1) * 128],
                                      r=xTb_sb[:, dt * S + tq * TQ + c0: dt * S + tq * TQ + c1],
                                      st=(dt == 0), sp=(dt == DT - 1):
                                      En.matmul(o, l, r, start=st, stop=sp)),
                           waits=w, inc=(sPE if (dt == DT - 1 and c1 == TQ) else None))
                bt = bqkvTk_sb[:, ot - D // 128: ot - D // 128 + 1] if is_k else bqkvT_sb[:, ot:ot + 1]
                sc = 0.125 if is_k else 1.0
                on("vector", (lambda En, o=stg_sb[par][:, :], i=ps, b=bt, s=sc:
                              En.tensor_scalar(o, i, b, s, ALU.add, ALU.mult)),
                   waits=[(sPE, sPE.n), (sC, nconst), (sDVE, nset), (sQKst[par], sQKst[par].n)],
                   inc=sDVE)
                dst = (k_dram if is_k else q_dram)
                row0 = (ot - D // 128 if is_k else ot) * 128
                dma("sync" if par == 0 else "gpsimd",
                    dst[row0:row0 + 128, tq * TQ:(tq + 1) * TQ], stg_sb[par][:, :],
                    sQKst[par], waits=[(sDVE, sDVE.n)])
                p1m["it"] += 1

        for ot in ot_order[:D // 128]:
            emit_qk_ot(ot)
            if ot == D // 128:
                p1m["k0"] = [(sQKst[0], sQKst[0].n), (sQKst[1], sQKst[1].n)]
        for i in range(2):
            on("vector", (lambda En, o=vst_sb[i][:, :]: En.memset(o, 0.0)), inc=sDVE)
            on("vector", (lambda En, o=vst_sb[i].ap().rearrange("p (h x) -> p h x", x=128)[:, :, 64:65]:
                          En.memset(o, 1.0)),
               waits=[(sDVE, sDVE.n)], inc=sDVE)
        vst_init = sDVE.n
        for st_i in range(ST):
            par = st_i % 2
            ps = psA[:, par * 1024: par * 1024 + D]
            for dt in range(DT):
                for c0, c1 in chunks(0, D):
                    w = []
                    if dt == 0 and c0 == 0:
                        w = [(sDVE, sDVE.n), (sACT, sACT.n)]
                    on("tensor", (lambda En, o=ps[:, c0:c1],
                                  l=xTb_sb[:, dt * S + st_i * 128: dt * S + (st_i + 1) * 128],
                                  r=wqkv_sb[:, dt * 3 * D + 2 * D + c0: dt * 3 * D + 2 * D + c1],
                                  st=(dt == 0): En.matmul(o, l, r, start=st, stop=False)),
                       waits=w, inc=None)
            for c0, c1 in chunks(0, D):
                on("tensor", (lambda En, o=ps[:, c0:c1], l=ones_row_b[0:1, 0:128], r=bv_sb[0:1, c0:c1]:
                              En.matmul(o, l, r, start=False, stop=True)),
                   waits=([(sDVE, nset), (sG, sG.n)] if (st_i == 0 and c0 == 0) else []),
                   inc=(sPE if c1 == D else None))
            on("scalar", (lambda En, o=vst_sb[par][:, :].rearrange("p (h x) -> p h x", x=128)[:, :, 0:64],
                          i=ps.rearrange("p (h x) -> p h x", x=64):
                          En.activation(o, i, AF.Copy)),
               waits=[(sPE, sPE.n), (sVst[par], sVst[par].n), (sDVE, vst_init)], inc=sACT)
            dma("sync" if par == 0 else "gpsimd",
                vp_dram[st_i], vst_sb[par][:, :], sVst[par], waits=[(sACT, sACT.n)])
        v_all = [(sVst[0], sVst[0].n), (sVst[1], sVst[1].n)]
        for ot in ot_order[D // 128:]:
            emit_qk_ot(ot)
            if ot == 0:
                dma("sync", k_sb0[0:64, :], k_dram[0:64, :], sKld[0], waits=p1m["k0"])
        qk_all = [(sQKst[0], sQKst[0].n), (sQKst[1], sQKst[1].n)]
        self.barrier()
        s1.close()

        s2 = ExitStack()
        k_sb = [k_sb0, T(s2, "k_sb1", [128, S], BF16)]
        vp_sb = [T(s2, f"vp_sb{i}", [128, ST * 128], BF16) for i in range(2)]
        qh_sb = [T(s2, f"qh_sb{i}", [128, S], BF16) for i in range(2)]
        qw_sb = [T(s2, f"qw_sb{i}", [128, S], BF16) for i in range(2)]
        e_sb = [T(s2, f"e_sb{i}", [128, 2 * TB], BF16) for i in range(3)]
        ctxu_sb = [T(s2, f"ctxu{i}", [65, TB], F32) for i in range(4)]
        denp_sb = [T(s2, f"denp{i}", [128, TB // 128], F32) for i in range(4)]
        recp_sb = [T(s2, f"recp{i}", [128, TB // 128], F32) for i in range(4)]
        bden_sb = [T(s2, f"bden{i}", [64, TB], F32) for i in range(4)]
        sNrm4 = [self.SM(f"sNrm{i}") for i in range(4)]
        sPool = self.SM("sPool")

        for tt_ in (k_sb[0], k_sb[1], qh_sb[0], qh_sb[1], qw_sb[0], qw_sb[1]):
            on("vector", (lambda En, o=tt_[64:128, :]: En.memset(o, 0.0)), inc=sDVE)
        pad_init = sDVE.n
        last_head_done = [0, 0]
        ectx_done = {0: 0, 1: 0, 2: 0}

        def head_loads(h, par):
            w0 = qk_all + ([(sPE, last_head_done[par])] if last_head_done[par] else [])
            dma("sync", k_sb[par][0:64, :], k_dram[h * 64:(h + 1) * 64, :], sKld[par],
                waits=w0 + [(sDVE, pad_init)])
            dma("sync", vp_sb[par][:, :],
                vp_dram[:, :, h * 128:(h + 1) * 128].rearrange("st p x -> p st x"),
                sKld[par], waits=v_all)
            dma("sync", qh_sb[par][0:64, :], q_dram[h * 64:(h + 1) * 64, :], sKld[par])

        dma("sync", qh_sb[0][0:64, :], q_dram[0:64, :], sKld[0])
        dma("sync", vp_sb[0][:, :],
            vp_dram[:, :, 0:128].rearrange("st p x -> p st x"), sVp0, waits=v_all)
        nrm = {"gtb": 0, "chains": [], "multv": {}, "mult4": [0, 0, 0, 0],
               "ctxevac": [0, 0], "recipv": [0, 0, 0, 0], "d3v": [0, 0, 0, 0]}

        def chain_advance_all(flush=False):
            if flush:
                while nrm["chains"]:
                    for chn in list(nrm["chains"]):
                        _adv_one(chn)
            else:
                for chn in list(nrm["chains"]):
                    _adv_one(chn)

        def _adv_one(chn):
            st = chn["step"]
            p4 = chn["p4"]
            sN = sNrm4[p4]
            if st == 0:
                dma("sync", nst1_dram[p4], ctxu_sb[p4][64:65, 0:TB], sN,
                    waits=[(sDVE, chn["evacv"])])
            elif st == 1:
                dma("sync", denp_sb[p4][:, :],
                    nst1_dram[p4].rearrange("one (p f) -> (one p) f", p=128), sN,
                    waits=[(sN, chn["v"])])
            elif st == 2:
                on("vector", (lambda En, o=recp_sb[p4][:, :], i=denp_sb[p4][:, :]:
                              En.reciprocal(o, i)),
                   waits=[(sN, chn["v"])], inc=sDVE)
                nrm["recipv"][p4] = sDVE.n
                dma("sync", nst2_dram[p4], recp_sb[p4][:, :], sN,
                    waits=[(sDVE, sDVE.n)])
            elif st == 3:
                dma("sync", bden_sb[p4][:, :], nst2_dram[p4].to_broadcast([64, TB]), sN,
                    waits=[(sN, chn["v"]), (sDVE, nrm["mult4"][p4])])
            elif st == 4:
                on("vector", (lambda En, o=chn["dst"], i0=ctxu_sb[p4][0:64, 0:TB], i1=bden_sb[p4][:, :]:
                              En.tensor_tensor(o, i0, i1, ALU.mult)),
                   waits=[(sN, chn["v"]), (sDVE, sDVE.n)], inc=sDVE)
                nrm["multv"][chn["key"]] = sDVE.n
                nrm["mult4"][p4] = sDVE.n
                nrm["chains"].remove(chn)
                chn["step"] = 5
                return
            chn["step"] += 1
            chn["v"] = sN.n

        exp_done = [0, 0]
        nrm["pend"] = []

        def emit_ctx(pc):
            cslot = pc["cslot"]
            for c0, c1 in chunks(pc["a0"], TB):
                bank_last = pc["t0b"] // 128 + (c0 // 512) * 4 + 3
                on("tensor", (lambda En, o=psB[0:128, cslot * 1024 + c0: cslot * 1024 + c1],
                              l=vp_sb[pc["par"]][:, pc["si"] * 128:(pc["si"] + 1) * 128],
                              rr=e_sb[pc["ep"]][:, c0:c1], st=pc["start"],
                              sp=(pc["si"] == bank_last):
                              En.matmul(o, l, rr, start=st, stop=sp)),
                   waits=(pc["eready"] + [(sDVE, nrm["ctxevac"][cslot])]) if c0 == pc["a0"] else [],
                   inc=(sPE if c1 == TB else None))
            ectx_done[pc["ep"]] = sPE.n
            if pc["stop"]:
                p4 = pc["p4"]
                on("vector", (lambda En, o=ctxu_sb[p4][:, 0:TB], i=psB[0:65, cslot * 1024: cslot * 1024 + TB]:
                              En.tensor_copy(o, i)),
                   waits=[(sPE, sPE.n), (sDVE, nrm["mult4"][p4])],
                   inc=sDVE)
                nrm["ctxevac"][cslot] = sDVE.n
                chn = {"step": 0, "p4": p4, "v": sNrm4[p4].n, "evacv": sDVE.n,
                       "dst": pc["dst"], "key": pc["key"]}
                nrm["chains"].append(chn)
                _adv_one(chn)

        for h in range(H):
            par = h % 2
            if h + 1 < H:
                head_loads(h + 1, (h + 1) % 2)
            for r in range(1, 4):
                qsrc = qh_sb[par] if r == 1 else qw_sb[r % 2]
                for tb in range(NTB):
                    chain_advance_all()
                    chain_advance_all()
                    t0b, t1b = tb * TB, (tb + 1) * TB
                    s_last = t1b // 128
                    gtb = nrm["gtb"]
                    cslot = gtb % 2
                    p4 = gtb % 4
                    if r < 3:
                        dst = qw_sb[(r + 1) % 2][0:64, t0b:t1b]
                    else:
                        dst = ctx3_sb[(h % 2) * 64:(h % 2) * 64 + 64, (h // 2) * S + t0b:(h // 2) * S + t1b]

                    for si in range(s_last):
                        t0 = max(t0b, si * 128)
                        a0 = t0 - t0b
                        ep = si % 3
                        sslot = si % 2
                        first = True
                        for c0, c1 in chunks(a0, TB):
                            w = []
                            if first:
                                if r > 1 and (h, r - 1, tb) not in nrm["multv"]:
                                    while any(pc["key"] == (h, r - 1, tb) for pc in nrm["pend"]):
                                        emit_ctx(nrm["pend"].pop(0))
                                    for chn in list(nrm["chains"]):
                                        if chn["key"] == (h, r - 1, tb):
                                            while chn["step"] < 5:
                                                _adv_one(chn)
                                w = [(sACT, exp_done[sslot]), (sKld[par], sKld[par].n)]
                                if h == 0 and r == 1 and tb == 0 and si == 0:
                                    w.append((sDVE, pad_init))
                                if r > 1:
                                    w.append((sDVE, nrm["multv"][(h, r - 1, tb)]))
                            on("tensor", (lambda En, o=psA[:, sslot * 1024 + c0: sslot * 1024 + c1],
                                          l=k_sb[par][0:128, si * 128:(si + 1) * 128],
                                          rr=qsrc[0:128, t0b + c0: t0b + c1]:
                                          En.matmul(o, l, rr, start=True, stop=True)),
                               waits=w, inc=(sPE if c1 == TB else None))
                            first = False
                        spe_scores = sPE.n
                        while any(pc["ep"] == ep for pc in nrm["pend"]):
                            emit_ctx(nrm["pend"].pop(0))
                        on("scalar", (lambda En, o=e_sb[ep][:, a0:TB], i=psA[:, sslot * 1024 + a0: sslot * 1024 + TB]:
                                      En.activation(o, i, AF.Exp)),
                           waits=[(sPE, spe_scores), (sPE, ectx_done[ep])],
                           inc=sACT)
                        exp_done[sslot] = sACT.n
                        has_diag = si * 128 >= t0b
                        if has_diag:
                            on("gpsimd", (lambda En, o=e_sb[ep][:, a0:a0 + 128], m=tri01_sb[:, :]:
                                          En.tensor_tensor(o, o, m, ALU.mult)),
                               waits=[(sACT, sACT.n), (sC, nconst)], inc=sPool)
                        eready = [(sPool, sPool.n)] if has_diag else [(sACT, sACT.n)]
                        if h == 0 and r == 1 and tb == 0 and si == 0:
                            eready = eready + [(sVp0, 16)]
                        nrm["pend"].append({"si": si, "a0": a0, "ep": ep, "eready": eready,
                                            "cslot": cslot, "par": par, "p4": p4, "t0b": t0b,
                                            "start": si == 0, "stop": si == s_last - 1,
                                            "dst": dst, "key": (h, r, tb)})
                        if len(nrm["pend"]) > 2:
                            emit_ctx(nrm["pend"].pop(0))
                    nrm["gtb"] += 1
            while any(pc["key"][0] == h for pc in nrm["pend"]):
                emit_ctx(nrm["pend"].pop(0))
            last_head_done[par] = sPE.n
        for pc in nrm["pend"]:
            emit_ctx(pc)
        nrm["pend"] = []
        chain_advance_all(flush=True)
        self.barrier()
        s2.close()

        s3 = ExitStack()
        wout_sb = T(s3, "wout_sb", [128, DT * D], BF16)
        x_sb = [T(s3, f"x_sb{i}", [128, D], F32) for i in range(2)]
        z_sb = [T(s3, f"z_sb{i}", [128, D], F32) for i in range(2)]
        sq_sb = [T(s3, f"sq_sb{i}", [128, D], F32) for i in range(2)]
        st3_sb = T(s3, "st3_sb", [128, 16], F32)
        x1f_sb = [T(s3, f"x1f{i}", [128, D], F32) for i in range(2)]
        x1b_sb = [T(s3, f"x1b{i}", [128, D], BF16) for i in range(2)]
        x1T_sb = [T(s3, f"x1T{i}", [128, 128], F32) for i in range(2)]
        e8_sb = [T(s3, f"e8_{i}", [128, E], F32) for i in range(2)]
        lgc_sb = [T(s3, f"lgc_{i}", [128, E], F32) for i in range(2)]
        probsT_sb = T(s3, "probsT_sb", [8, S], F32)
        zer_sb = T(s3, "zer_sb", [128, D], F32)

        dma("gpsimd", wout_sb[:, :].rearrange("p (dt o) -> p dt o", o=D),
            woutT_in.rearrange("(dt p) o -> p dt o", p=128), sG)
        on("vector", lambda En: En.memset(zer_sb[:, :], 0.0), inc=sDVE)

        p3 = {"zadd": [0, 0], "stf": [(0, 0), (0, 0)], "x1use": [0, 0],
              "trv": [0, 0], "lgread": (0, 0), "prtr": 0, "x1v": [0, 0],
              "prv": {}, "zv": [0, 0], "lgv": [0, 0], "mic": [0, 0],
              "rmaxv": [0, 0], "e8rd": [0, 0], "e8v": [0, 0]}
        x1_store_cnt = 0

        def p3_head(tt):
            par = tt % 2
            dma("gpsimd", x_sb[par][:, :], x_in[tt * 128:(tt + 1) * 128, :], sXld[par],
                waits=[(sDVE, p3["zadd"][par])])
            ps = psA[:, par * 1024: par * 1024 + D]
            for pr in range(H // 2):
                for c0, c1 in chunks(0, D):
                    w = []
                    if pr == 0 and c0 == 0:
                        w = [(sG, sG.n), (sDVE, p3["zadd"][par])]
                        if tt < 2:
                            w.append((sDVE, sDVE.n))
                    on("tensor", (lambda En, o=ps[:, c0:c1],
                                  l=ctx3_sb[:, pr * S + tt * 128: pr * S + (tt + 1) * 128],
                                  rr=wout_sb[:, pr * D + c0: pr * D + c1],
                                  st=(pr == 0): En.matmul(o, l, rr, start=st, stop=False)),
                       waits=w, inc=None)
            for c0, c1 in chunks(0, D):
                on("tensor", (lambda En, o=ps[:, c0:c1], l=ones_row_b[0:1, 0:128], rr=bout_sb[0:1, c0:c1]:
                              En.matmul(o, l, rr, start=False, stop=True)),
                   waits=([(sG, sG.n)] if tt == 0 and c0 == 0 else []),
                   inc=(sPE if c1 == D else None))
            on("vector", (lambda En, o=z_sb[par][:, :], i0=ps, i1=x_sb[par][:, :]:
                          En.tensor_tensor(o, i0, i1, ALU.add)),
               waits=[(sPE, sPE.n), (sXld[par], sXld[par].n), (sACT, p3["stf"][par][1])], inc=sDVE)
            zv = sDVE.n
            p3["zadd"][par] = zv
            p3["zv"][par] = zv

        def p3_stats(tt):
            par = tt % 2
            zv = p3["zv"][par]
            on("scalar", (lambda En, o=sq_sb[par][:, :], i=z_sb[par][:, :], a=st3_sb[:, par * 8: par * 8 + 1]:
                          En.activation(o, i, AF.Copy, accum_out=a)),
               waits=[(sDVE, zv)], inc=sACT)
            on("scalar", (lambda En, o=sq_sb[par][:, :], i=z_sb[par][:, :], a=st3_sb[:, par * 8 + 1: par * 8 + 2]:
                          En.activation(o, i, AF.Square, accum_out=a)),
               waits=[(sACT, sACT.n)], inc=sACT)
            stv = sACT.n
            b8 = par * 8
            on("vector", (lambda En, o=st3_sb[:, b8 + 2: b8 + 3], i=st3_sb[:, b8: b8 + 1]:
                          En.tensor_scalar_mul(o, i, 1.0 / D)),
               waits=[(sACT, stv), (sDVE, sDVE.n)], inc=sDVE)
            on("vector", (lambda En, o=st3_sb[:, b8 + 3: b8 + 4], i=st3_sb[:, b8 + 2: b8 + 3], s=st3_sb[:, b8 + 2: b8 + 3]:
                          En.tensor_scalar(o, i, s, -LNEPS, ALU.mult, ALU.add)),
               waits=[(sDVE, sDVE.n)], inc=sDVE)
            on("vector", (lambda En, o=st3_sb[:, b8 + 4: b8 + 5], i=st3_sb[:, b8 + 1: b8 + 2], s=st3_sb[:, b8 + 3: b8 + 4]:
                          En.tensor_scalar(o, i, 1.0 / D, s, ALU.mult, ALU.subtract)),
               waits=[(sDVE, sDVE.n)], inc=sDVE)
            p3["mic"][par] = sDVE.n

        def p3_fin(tt):
            par = tt % 2
            b8 = par * 8
            on("scalar", (lambda En, o=st3_sb[:, b8 + 5: b8 + 6], i=st3_sb[:, b8 + 4: b8 + 5]:
                          En.activation(o, i, AF.Sqrt)),
               waits=[(sDVE, p3["mic"][par])], inc=sACT)
            on("vector", (lambda En, o=st3_sb[:, b8 + 6: b8 + 7], i=st3_sb[:, b8 + 5: b8 + 6]:
                          En.reciprocal(o, i)),
               waits=[(sACT, sACT.n)], inc=sDVE)
            p3["stf"][par] = (sDVE.n, sACT.n)
            on("vector", (lambda En, o=x1f_sb[par][:, :], i=z_sb[par][:, :], m=st3_sb[:, b8 + 2: b8 + 3],
                          rr=st3_sb[:, b8 + 6: b8 + 7]:
                          En.tensor_scalar(o, i, m, rr, ALU.subtract, ALU.mult)),
               waits=[(sDVE, sDVE.n), (sX1st, sX1st.n), (sPE, p3["x1use"][par])], inc=sDVE)
            x1v = sDVE.n
            on("vector", (lambda En, o=x1b_sb[par][:, :], i=x1f_sb[par][:, :]: En.tensor_copy(o, i)),
               waits=[(sDVE, x1v), (sX1st, sX1st.n)], inc=sDVE)
            dma("sync", x1_dram[tt * 128:(tt + 1) * 128, :], x1b_sb[par][:, :], sX1st,
                waits=[(sDVE, sDVE.n)])
            dma("sync", acc_dram[tt * 128:(tt + 1) * 128, :], x1f_sb[par][:, :], sX1st)
            p3["x1v"][par] = x1v

        def p3_rA(tt):
            par = tt % 2
            x1v = p3["x1v"][par]
            for dt in range(DT):
                tp = dt % 2
                on("tensor", (lambda En, o=psB[:, tp * 512: tp * 512 + 128],
                              i=x1f_sb[par][:, dt * 128:(dt + 1) * 128], idn=ident_sb[:, :]:
                              En.transpose(o, i, idn)),
                   waits=[(sDVE, x1v if dt == 0 else 0), (sC, nconst), (sDVE, p3["trv"][tp])] +
                         ([(sDVE, p3["lgread"][0]), (sACT, p3["lgread"][1])] if dt == 0 else []),
                   inc=sPE)
                on("vector", (lambda En, o=x1T_sb[tp][:, :], i=psB[:, tp * 512: tp * 512 + 128]:
                              En.tensor_copy(o, i)),
                   waits=[(sPE, sPE.n)], inc=sDVE)
                p3["trv"][tp] = sDVE.n
                on("tensor", (lambda En, o=psB[:, 1536:1536 + E], l=x1T_sb[tp][:, :],
                              rr=ws_sb[:, dt * E:(dt + 1) * E], st=(dt == 0):
                              En.matmul(o, l, rr, start=st, stop=False)),
                   waits=[(sDVE, sDVE.n)], inc=None)
            on("tensor", (lambda En, o=psB[:, 1536:1536 + E], l=ones_row_f[0:1, 0:128], rr=bs_sb[0:1, 0:E]:
                          En.matmul(o, l, rr, start=False, stop=True)),
               waits=[], inc=sPE)
            p3["lgv"][par] = sPE.n
            p3["x1use"][par] = sPE.n

        def p3_rB(tt):
            par = tt % 2
            b8 = par * 8
            lgv = p3["lgv"][par]
            on("vector", (lambda En, o=st3_sb[:, b8 + 7: b8 + 8], i=psB[:, 1536:1536 + E]:
                          En.reduce_max(o, i, axis=AX.X, negate=True)),
               waits=[(sPE, lgv)], inc=sDVE)
            on("scalar", (lambda En, o=e8_sb[par][:, :], i=psB[:, 1536:1536 + E], b=st3_sb[:, b8 + 7: b8 + 8]:
                          En.activation(o, i, AF.Exp, bias=b)),
               waits=[(sDVE, sDVE.n), (sACT, sACT.n)], inc=sACT)
            p3["lgread"] = (sDVE.n, sACT.n)
            on("vector", (lambda En, o=st3_sb[:, b8: b8 + 1], i=e8_sb[par][:, :]: En.reduce_sum(o, i, axis=AX.X)),
               waits=[(sACT, sACT.n)], inc=sDVE)
            on("vector", (lambda En, o=st3_sb[:, b8 + 1: b8 + 2], i=st3_sb[:, b8: b8 + 1]: En.reciprocal(o, i)),
               waits=[(sDVE, sDVE.n)], inc=sDVE)
            on("vector", (lambda En, o=probs_sb[:, tt * E:(tt + 1) * E], i=e8_sb[par][:, :], s=st3_sb[:, b8 + 1: b8 + 2]:
                          En.tensor_scalar_mul(o, i, s)),
               waits=[(sDVE, sDVE.n)], inc=sDVE)
            p3["prv"][tt] = sDVE.n

        def p3_prtr(tt):
            on("tensor", (lambda En, o=psB[0:E, 1024:1024 + 128], i=probs_sb[:, tt * E:(tt + 1) * E],
                          idn=ident_sb[:, :]: En.transpose(o, i, idn)),
               waits=[(sDVE, p3["prv"][tt]), (sDVE, p3["prtr"])], inc=sPE)
            on("vector", (lambda En, o=probsT_sb[0:E, tt * 128:(tt + 1) * 128], i=psB[0:E, 1024:1024 + 128]:
                          En.tensor_copy(o, i)),
               waits=[(sPE, sPE.n)], inc=sDVE)
            p3["prtr"] = sDVE.n

        for tt in range(NT):
            p3_head(tt)
            if tt >= 1:
                p3_rA(tt - 1)
            p3_stats(tt)
            p3_fin(tt)
            if tt >= 1:
                p3_rB(tt - 1)
            if tt >= 2:
                p3_prtr(tt - 2)
        p3_rA(NT - 1)
        p3_rB(NT - 1)
        p3_prtr(NT - 2)
        p3_prtr(NT - 1)
        probs_done = sDVE.n
        x1_all = sX1st.n

        dma("sync", g_dram[0:S].rearrange("(t p) e -> p t e", p=128),
            probs_sb[:, :].rearrange("p (t e) -> p t e", e=E), sPr,
            waits=[(sDVE, probs_done)])
        dma("sync", g_dram[S:S + C, :], zer_sb[:, 0:C * E // 128], sPr)
        for j in range(CT):
            dma("sync", acc_dram[S + j * 128:S + (j + 1) * 128, :], zer_sb[:, 0:D], sPr,
                waits=[(sX1st, x1_all)] if j == 0 else [])
        dma("sync", pin_dram, probsT_sb[:, :], sPr, waits=[(sDVE, probs_done)])
        pr_all = sPr.n
        self.barrier()
        s3.close()
        scopeA.close()

        s56 = ExitStack()
        w1_sb = [T(s56, f"w1_{i}", [128, DT * WC], BF16) for i in range(2)]
        w2f_sb = T(s56, "w2f_sb", [128, FT * D], BF16)
        b2_sb = T(s56, "b2_sb", [1, D], BF16)
        wst_sb = [T(s56, f"wst{i}", [128, 2048], F32) for i in range(2)]
        sWst = [self.SM("sWst0"), self.SM("sWst1")]
        sB2 = self.SM("sB2")
        NWC = F // WC
        st6 = {"w1_pend": [[], []], "w1_used": [0, 0], "w2_used": 0, "wst_cp": [[], []],
               "sttv": {}, "o2ev": [0, 0],
               "wst_rot": 0,
               "xg_free": 0, "xgT_used": [0, 0], "g_used": [0, 0],
               "xg_ld": {}, "g_ld": {}, "sld": {}, "xgT_dve": {}}

        def emit_w1_load(e, fc):
            wpar = fc % 2
            pend = []
            nq = 2 if DT >= 4 else 1
            cols = DT * WC // nq
            qdt = DT // nq
            for q in range(nq):
                s_ = st6["wst_rot"] % 2
                st6["wst_rot"] += 1
                dma("sync", wst_sb[s_][:, 0:cols].rearrange("p (dt f) -> p dt f", f=WC),
                    w1_in[e].rearrange("(dt p) f -> p dt f", p=128)[:, qdt * q:qdt * (q + 1),
                                                                   fc * WC:(fc + 1) * WC],
                    sWst[s_], waits=list(st6["wst_cp"][s_]))
                cw = [(sWst[s_], sWst[s_].n)]
                if q == 0:
                    cw.append((sPE, st6["w1_used"][wpar]))
                if q % 2 == 0:
                    on("vector", (lambda En, o=w1_sb[wpar][:, q * cols:(q + 1) * cols],
                                  i=wst_sb[s_][:, 0:cols]: En.tensor_copy(o, i)),
                       waits=cw, inc=sDVE)
                    st6["wst_cp"][s_] = [(sDVE, sDVE.n)]
                    pend.append((sDVE, sDVE.n))
                else:
                    on("scalar", (lambda En, o=w1_sb[wpar][:, q * cols:(q + 1) * cols],
                                  i=wst_sb[s_][:, 0:cols]: En.copy(o, i)),
                       waits=cw, inc=sACT)
                    st6["wst_cp"][s_] = [(sACT, sACT.n)]
                    pend.append((sACT, sACT.n))
            st6["w1_pend"][wpar] = pend[-2:]


        def emit_w2_load(e):
            dma("gpsimd", w2f_sb[:, :].rearrange("p (ft d) -> p ft d", d=D),
                w2_in[e].rearrange("(ft p) d -> p ft d", p=128), sW2,
                waits=[(sPE, st6["w2_used"])])
            dma("gpsimd", b2_sb[0:1, :], b2_row_in[e], sB2,
                waits=[(sPE, st6["w2_used"])])
            return sW2.n, sB2.n

        s4 = ExitStack()
        pexp_sb = T(s4, "pexp_sb", [128, S], F32)
        NB = c.N // 128
        pb_sb = T(s4, "pb_sb", [128, NB], F32)
        cmp_sb = T(s4, "cmp_sb", [128, NB], F32)
        lo_sb = [T(s4, f"lo{i}", [128, 1], F32) for i in range(2)]
        hi_sb = [T(s4, f"hi{i}", [128, 1], F32) for i in range(2)]
        mid_sb = T(s4, "mid_sb", [128, 1], F32)
        pc_sb = T(s4, "pc_sb", [128, 1], F32)
        cnt_sb = T(s4, "cnt_sb", [1, 1], F32)
        cntb_sb = T(s4, "cntb_sb", [128, 1], F32)
        sel_sb = T(s4, "sel_sb", [128, 1], U32)
        thr8_sb = T(s4, "thr8_sb", [1, E], F32)
        thrb_sb = T(s4, "thrb_sb", [128, E], F32)
        ranks_sb = T(s4, "ranks_sb", [128, NT * E], F32)
        cs_sb = T(s4, "cs_sb", [1, NT * E], F32)
        base_sb = [T(s4, f"base{i}", [1, NT * E], F32) for i in range(2)]
        idxf_sb = T(s4, "idxf_sb", [128, NT * E], F32)
        masku_sb = T(s4, "masku_sb", [128, NT * E], U32)

        on("gpsimd", (lambda En: En.collective_compute(
            "AllGather", ALU.bypass, replica_groups=[list(range(c.NC))],
            ins=[pin_dram], outs=[pout_dram])),
           waits=[(sPr, pr_all)], inc=sCC)
        for fc_ in range(min(2, NWC)):
            emit_w1_load(0, fc_)
        w2_pend0, b2_ld0 = emit_w2_load(0)
        on("gpsimd", (lambda En: En.indirect_dma_start(
            out=pexp_sb[:, :], out_offset=None, in_=pout_dram,
            in_offset=IndirectOffsetOnAxis(sidx_sb[:, E * CT:E * CT + 1], 0),
            bounds_check=regs["R1"], oob_is_err=False)),
           waits=[(sCC, sCC.n), (sC, nconst)], inc=sBis, inc_by=16)
        on("gpsimd", (lambda En: En.dma_start(out=pe_dram, in_=pexp_sb[0:c.NC, :])),
           waits=[(sBis, sBis.n)], inc=sBis, inc_by=16)
        on("gpsimd", (lambda En: En.dma_start(
            out=pb_sb[:, :], in_=pe_dram.rearrange("r (j f) -> r j f", j=128 // c.NC))),
           waits=[(sBis, sBis.n)], inc=sBis, inc_by=16)
        bis_ld = sBis.n
        on("vector", lambda En: En.memset(lo_sb[0][:, :], 0.0), inc=sDVE)
        on("vector", lambda En: En.memset(hi_sb[0][:, :], 1.0), inc=sDVE)
        for it in range(c.BIS):
            pi, po = it % 2, (it + 1) % 2
            on("vector", (lambda En, o=mid_sb[:, :], i=lo_sb[pi][:, :], hh=hi_sb[pi][:, :]:
                          En.tensor_scalar(o, i, hh, 0.5, ALU.add, ALU.mult)),
               waits=[(sDVE, sDVE.n)], inc=sDVE)
            on("vector", (lambda En, o=cmp_sb[:, :], i=pb_sb[:, :], s=mid_sb[:, :]:
                          En.tensor_single_scalar(o, i, s, ALU.is_gt)),
               waits=[(sDVE, sDVE.n)] + ([(sBis, bis_ld)] if it == 0 else []), inc=sDVE)
            on("vector", (lambda En, o=pc_sb[:, :], i=cmp_sb[:, :]: En.reduce_sum(o, i, axis=AX.X)),
               waits=[(sDVE, sDVE.n)], inc=sDVE)
            on("tensor", (lambda En, o=psB[0:1, 1664:1665], l=pc_sb[:, :], rr=ones_col_f[:, :]:
                          En.matmul(o, l, rr, start=True, stop=True)),
               waits=[(sDVE, sDVE.n)], inc=sPE)
            on("vector", (lambda En, o=cnt_sb[:, :], i=psB[0:1, 1664:1665]: En.tensor_copy(o, i)),
               waits=[(sPE, sPE.n)], inc=sDVE)
            on("tensor", (lambda En, o=psB[:, 1665:1666], l=ones_row_f[0:1, 0:128], rr=cnt_sb[:, :]:
                          En.matmul(o, l, rr, start=True, stop=True)),
               waits=[(sDVE, sDVE.n)], inc=sPE)
            on("vector", (lambda En, o=cntb_sb[:, :], i=psB[:, 1665:1666]: En.tensor_copy(o, i)),
               waits=[(sPE, sPE.n)], inc=sDVE)
            on("vector", (lambda En, o=sel_sb[:, :], i=cntb_sb[:, :]:
                          En.tensor_single_scalar(o, i, float(c.KSEL), ALU.is_ge)),
               waits=[(sDVE, sDVE.n)], inc=sDVE)
            on("vector", (lambda En, o=lo_sb[po][:, :], m=sel_sb[:, :], t=mid_sb[:, :], f=lo_sb[pi][:, :]:
                          En.select(o, m, t, f, add_drain=True)),
               waits=[(sDVE, sDVE.n)], inc=sDVE)
            on("vector", (lambda En, o=hi_sb[po][:, :], m=sel_sb[:, :], t=hi_sb[pi][:, :], f=mid_sb[:, :]:
                          En.select(o, m, t, f, add_drain=True)),
               waits=[(sDVE, sDVE.n)], inc=sDVE)
        fin = c.BIS % 2
        on("gpsimd", (lambda En: En.dma_start(out=tin_dram, in_=lo_sb[fin][0:1, 0:1])),
           waits=[(sDVE, sDVE.n)], inc=sBis, inc_by=16)
        on("gpsimd", (lambda En: En.collective_compute(
            "AllGather", ALU.bypass, replica_groups=[list(range(c.NC))],
            ins=[tin_dram], outs=[tout_dram])),
           waits=[(sBis, sBis.n)], inc=sCC)
        on("gpsimd", (lambda En: En.dma_start(out=thr8_sb[0:1, 0:E],
                                              in_=tout_dram.rearrange("r one -> (one r)")[None, :])),
           waits=[(sCC, sCC.n)], inc=sBis, inc_by=16)
        on("tensor", (lambda En, o=psB[:, 1666:1666 + E], l=ones_row_f[0:1, 0:128], rr=thr8_sb[0:1, 0:E]:
                      En.matmul(o, l, rr, start=True, stop=True)),
           waits=[(sBis, sBis.n)], inc=sPE)
        on("vector", (lambda En, o=thrb_sb[:, :], i=psB[:, 1666:1666 + E]: En.tensor_copy(o, i)),
           waits=[(sPE, sPE.n)], inc=sDVE)
        thr_dve = sDVE.n

        for tt in range(NT):
            on("vector", (lambda En, o=mask_sb[:, tt * E:(tt + 1) * E],
                          i0=probs_sb[:, tt * E:(tt + 1) * E], i1=thrb_sb[:, :]:
                          En.tensor_tensor(o, i0, i1, ALU.is_gt)),
               waits=[(sDVE, thr_dve)] if tt == 0 else [], inc=(sDVE if tt == NT - 1 else None))
        mask_dve = sDVE.n
        for tt in range(NT):
            on("tensor", (lambda En, o=psA[:, tt * E:(tt + 1) * E], l=triu_sb[:, :],
                          rr=mask_sb[:, tt * E:(tt + 1) * E], st=(tt == 0):
                          En.matmul(o, l, rr, start=st, stop=False)),
               waits=([(sDVE, mask_dve), (sC, nconst)] if tt == 0 else []), inc=None)
        on("tensor", (lambda En, o=psB[0:1, 0:NT * E], l=ones_col_f[:, :], rr=mask_sb[:, :]:
                      En.matmul(o, l, rr, start=True, stop=True)),
           waits=[], inc=sPE)
        on("vector", (lambda En, o=cs_sb[:, :], i=psB[0:1, 0:NT * E]: En.tensor_copy(o, i)),
           waits=[(sPE, sPE.n)], inc=sDVE)
        on("vector", (lambda En, o=base_sb[0][0:1, 0:E]: En.memset(o, 0.0)),
           waits=[(sDVE, sDVE.n)], inc=sDVE)
        on("vector", (lambda En, o=base_sb[0][0:1, E:NT * E], i=cs_sb[0:1, 0:(NT - 1) * E]: En.tensor_copy(o, i)),
           waits=[(sDVE, sDVE.n)], inc=sDVE)
        shift, cur = 1, 0
        while shift < NT:
            on("vector", (lambda En, o=base_sb[1 - cur][0:1, shift * E:NT * E],
                          i0=base_sb[cur][0:1, shift * E:NT * E], i1=base_sb[cur][0:1, 0:(NT - shift) * E]:
                          En.tensor_tensor(o, i0, i1, ALU.add)),
               waits=[(sDVE, sDVE.n)], inc=sDVE)
            on("vector", (lambda En, o=base_sb[1 - cur][0:1, 0:shift * E], i=base_sb[cur][0:1, 0:shift * E]:
                          En.tensor_copy(o, i)),
               waits=[(sDVE, sDVE.n)], inc=sDVE)
            cur = 1 - cur
            shift *= 2
        basev = sDVE.n
        for tt in range(NT):
            on("tensor", (lambda En, o=psA[:, tt * E:(tt + 1) * E], l=ones_row_f[0:1, 0:128],
                          rr=base_sb[cur][0:1, tt * E:(tt + 1) * E], sp=(tt == NT - 1):
                          En.matmul(o, l, rr, start=False, stop=sp)),
               waits=([(sDVE, basev)] if tt == 0 else []), inc=(sPE if tt == NT - 1 else None))
        on("vector", (lambda En, o=ranks_sb[:, :], i=psA[:, 0:NT * E]: En.tensor_copy(o, i)),
           waits=[(sPE, sPE.n)], inc=sDVE)
        on("vector", (lambda En, o=masku_sb[:, :], i=mask_sb[:, :]: En.tensor_copy(o, i)),
           waits=[(sDVE, sDVE.n)], inc=sDVE)
        on("vector", (lambda En, o=idxf_sb[:, :], m=masku_sb[:, :], t=ranks_sb[:, :], f=tokbigE_sb[:, :]:
                      En.select(o, m, t, f, add_drain=True)),
           waits=[(sDVE, sDVE.n), (sC, nconst)], inc=sDVE)
        on("vector", (lambda En, o=idxu_sb[:, :], i=idxf_sb[:, :]: En.tensor_copy(o, i)),
           waits=[(sDVE, sDVE.n)], inc=sDVE)
        idx_dve = sDVE.n

        def emit_slot_scatters(e):
            for tt in range(NT):
                on("gpsimd", (lambda En, oi=idxu_sb[:, tt * E + e: tt * E + e + 1],
                              ii=tokid_sb[:, tt:tt + 1], eo=e * C:
                              En.indirect_dma_start(out=slot_dram, out_offset=IndirectOffsetOnAxis(oi, 0),
                                                    in_=ii, in_offset=None, element_offset=eo,
                                                    bounds_check=regs["C1"], oob_is_err=False)),
                   waits=([(sDVE, idx_dve), (sG, sG.n), (sC, nconst)] if tt == 0 else []),
                   inc=sSc[e], inc_by=16)

        emit_slot_scatters(0)
        self.barrier()
        s4.close()

        s6 = ExitStack()
        xg_sb = T(s6, "xg_sb", [128, CT * D], F32)
        xgT_sb = [T(s6, f"xgT{i}", [128, DT * C], BF16) for i in range(2)]
        g_sb = T(s6, "g_sb", [128, 2 * CT], F32)
        h1T_sb = T(s6, "h1T_sb", [128, FT * C], BF16)
        xbh_sb = [T(s6, f"xbh{i}", [128, C], F32) for i in range(2)]
        gt1_sb = [T(s6, f"gt1_{i}", [128, C], F32) for i in range(2)]
        o2_sb = [T(s6, f"o2_{i}", [128, D], F32) for i in range(2)]

        on("vector", lambda En: En.memset(xg_sb[:, :], 0.0), inc=sDVE)
        xg_init = sDVE.n

        def emit_slot_loads(e):
            for j in range(CT):
                dma("sync", sidx_sb[:, e * CT + j: e * CT + j + 1],
                    slot_dram[e * C + j * 128:e * C + (j + 1) * 128, 0:1],
                    sSld[e], waits=([(sSc[e], sSc[e].n)] if j == 0 else []))
            st6["sld"][e] = sSld[e].n

        def emit_gathers(e):
            epar = e % 2
            for j in range(CT):
                on("gpsimd", (lambda En, o=xg_sb[:, j * D:(j + 1) * D],
                              oi=sidx_sb[:, e * CT + j: e * CT + j + 1]:
                              En.indirect_dma_start(out=o, out_offset=None, in_=x1_dram,
                                                    in_offset=IndirectOffsetOnAxis(oi, 0),
                                                    bounds_check=regs["S1"], oob_is_err=False)),
                   waits=([(sSld[e], st6["sld"][e]), (sX1st, x1_all),
                           (sPE, st6["xg_free"]), (sDVE, xg_init)] if j == 0 else []),
                   inc=sXg[epar], inc_by=16)
                on("gpsimd", (lambda En, o=g_sb[:, epar * CT + j: epar * CT + j + 1],
                              oi=sidx_sb[:, e * CT + j: e * CT + j + 1], eo=e:
                              En.indirect_dma_start(out=o, out_offset=None, in_=g_dram,
                                                    in_offset=IndirectOffsetOnAxis(oi, 0), element_offset=eo)),
                   waits=([(sPr, pr_all), (sDVE, st6["g_used"][epar])] if j == 0 else []), inc=sGg[epar], inc_by=16)
            st6["xg_ld"][e] = sXg[epar].n
            st6["g_ld"][e] = sGg[epar].n

        def emit_transposes(e):
            epar = e % 2
            tr_evac = [0, 0, 0, 0]
            for j in range(CT):
                for dt in range(DT):
                    tp = (j * DT + dt) % 4
                    tpo = tp * 512
                    w = [(sACT, tr_evac[tp])]
                    if j == 0 and dt == 0:
                        w += [(sXg[epar], st6["xg_ld"][e]), (sPE, st6["xgT_used"][epar]),
                              (sDVE, sDVE.n), (sACT, sACT.n)]
                    on("tensor", (lambda En, o=psB[:, tpo:tpo + 128],
                                  i=xg_sb[:, j * D + dt * 128: j * D + (dt + 1) * 128], idn=ident_sb[:, :]:
                                  En.transpose(o, i, idn)),
                       waits=w, inc=sPE)
                    on("scalar", (lambda En, o=xgT_sb[epar][:, dt * C + j * 128: dt * C + (j + 1) * 128],
                                  i=psB[:, tpo:tpo + 128]: En.copy(o, i)),
                       waits=[(sPE, sPE.n)], inc=sACT)
                    tr_evac[tp] = sACT.n
            st6["xg_free"] = sPE.n
            st6["xgT_dve"][e] = sACT.n

        def emit_h1(e):
            epar = e % 2
            ps_free = [0, 0]
            gpar_free = [(0, 0), (0, 0)]
            for fc in range(NWC):
                wpar = fc % 2
                for f8 in range(WC // 128):
                    ft = fc * (WC // 128) + f8
                    hp = ft % 2
                    gp = ft % 2
                    ps = psA[:, hp * 1024: hp * 1024 + C]
                    for dt in range(DT):
                        for c0, c1 in chunks(0, C):
                            w = []
                            if dt == 0 and c0 == 0:
                                w = list(st6["w1_pend"][wpar]) + [(sDVE, ps_free[hp])]
                                if ft == 0:
                                    w.append((sACT, st6["xgT_dve"][e]))
                                    w.append((sDVE, sDVE.n))
                            on("tensor", (lambda En, o=ps[:, c0:c1],
                                          l=w1_sb[wpar][:, dt * WC + f8 * 128: dt * WC + (f8 + 1) * 128],
                                          rr=xgT_sb[epar][:, dt * C + c0: dt * C + c1],
                                          st=(dt == 0), sp=(dt == DT - 1):
                                          En.matmul(o, l, rr, start=st, stop=sp)),
                               waits=w, inc=(sPE if (dt == DT - 1 and c1 == C) else None))
                    st6["w1_used"][wpar] = sPE.n
                    dv0, ac0 = gpar_free[gp]
                    on("vector", (lambda En, o=xbh_sb[gp][:, :], i=ps, b=b1T_sb[:, e * FT + ft: e * FT + ft + 1]:
                                  En.tensor_scalar(o, i, b, 0.5, ALU.add, ALU.mult)),
                       waits=[(sPE, sPE.n), (sG, sG.n), (sDVE, dv0), (sACT, ac0)], inc=sDVE)
                    ps_free[hp] = sDVE.n
                    on("scalar", (lambda En, o=gt1_sb[gp][:, :], i=xbh_sb[gp][:, :]:
                                  En.activation(o, i, AF.Square, scale=2.0)),
                       waits=[(sDVE, sDVE.n)], inc=sACT)
                    on("vector", (lambda En, o=gt1_sb[gp][:, :], i=gt1_sb[gp][:, :]:
                                  En.tensor_scalar(o, i, GELU_C, 1.0, ALU.mult, ALU.add)),
                       waits=[(sACT, sACT.n)], inc=sDVE)
                    on("vector", (lambda En, o=gt1_sb[gp][:, :], i0=xbh_sb[gp][:, :], i1=gt1_sb[gp][:, :]:
                                  En.tensor_mul(o, i0, i1)),
                       waits=[(sDVE, sDVE.n)], inc=sDVE)
                    on("scalar", (lambda En, o=gt1_sb[gp][:, :], i=gt1_sb[gp][:, :]:
                                  En.activation(o, i, AF.Tanh, scale=GELU_S2)),
                       waits=[(sDVE, sDVE.n)], inc=sACT)
                    on("vector", (lambda En, o=h1T_sb[:, ft * C:(ft + 1) * C], i0=gt1_sb[gp][:, :], i1=xbh_sb[gp][:, :]:
                                  En.scalar_tensor_tensor(o, i0, 1.0, i1, ALU.add, ALU.mult)),
                       waits=[(sACT, sACT.n)], inc=sDVE)
                    st6["sttv"][ft] = sDVE.n
                    gpar_free[gp] = (sDVE.n, sACT.n)
                if fc + 2 < NWC:
                    emit_w1_load(e, fc + 2)
            st6["xgT_used"][epar] = sPE.n
            return sDVE.n



        def emit_h2(e, h1_dve, w2_pend, b2_ld):
            epar = e % 2
            for ct in range(CT):
                cp = ct % 2
                ps = psB[:, cp * 1024: cp * 1024 + D]
                for ft in range(FT):
                    for c0, c1 in chunks(0, D):
                        w = []
                        if ft == 0 and c0 == 0:
                            w = [(sW2, w2_pend), (sDVE, st6["o2ev"][cp])]
                        if ct == 0 and c0 == 0:
                            w.append((sDVE, st6["sttv"][ft]))
                        on("tensor", (lambda En, o=ps[:, c0:c1],
                                      l=h1T_sb[:, ft * C + ct * 128: ft * C + ct * 128 + 128],
                                      rr=w2f_sb[:, ft * D + c0: ft * D + c1],
                                      st=(ft == 0), sp=False:
                                      En.matmul(o, l, rr, start=st, stop=sp)),
                           waits=w, inc=None)
                for c0, c1 in chunks(0, D):
                    on("tensor", (lambda En, o=ps[:, c0:c1], l=ones_row_b[0:1, 0:128],
                                  rr=b2_sb[0:1, c0:c1]:
                                  En.matmul(o, l, rr, start=False, stop=True)),
                       waits=([(sB2, b2_ld)] if (ct == 0 and c0 == 0) else []),
                       inc=(sPE if c1 == D else None))
                on("vector", (lambda En, o=o2_sb[cp][:, :], i=ps, s=g_sb[:, epar * CT + ct: epar * CT + ct + 1]:
                              En.tensor_scalar_mul(o, i, s)),
                   waits=[(sPE, sPE.n), (sGg[epar], st6["g_ld"][e]), (sSA, sSA.n)], inc=sDVE)
                st6["o2ev"][cp] = sDVE.n
                on("gpsimd", (lambda En, ii=o2_sb[cp][:, :],
                              oi=sidx_sb[:, e * CT + ct: e * CT + ct + 1]:
                              En.indirect_dma_start(out=acc_dram, out_offset=IndirectOffsetOnAxis(oi, 0),
                                                    in_=ii, in_offset=None, compute_op=ALU.add)),
                   waits=[(sDVE, sDVE.n), (sSA, sSA.n), (sPr, pr_all)], inc=sSA, inc_by=16)
            st6["w2_used"] = sPE.n
            st6["g_used"][epar] = sDVE.n

        emit_slot_loads(0)
        emit_gathers(0)
        w2_pend, b2_ld = w2_pend0, b2_ld0
        nxt = {}
        for e in range(E):
            emit_transposes(e)
            if e > 0:
                w2_pend, b2_ld = emit_w2_load(e)
            h1_dve = emit_h1(e)
            if e + 1 < E:
                emit_slot_scatters(e + 1)
                emit_slot_loads(e + 1)
                emit_gathers(e + 1)
                for fc_ in range(min(2, NWC)):
                    emit_w1_load(e + 1, fc_)
            emit_h2(e, h1_dve, w2_pend, b2_ld)
        sa_all = sSA.n
        self.barrier()
        s6.close()
        s56.close()

        s7 = ExitStack()
        a_sb = [T(s7, f"a_{i}", [128, D], F32) for i in range(4)]
        sq7_sb = [T(s7, f"sq7_{i}", [128, D], F32) for i in range(2)]
        out_sb = [T(s7, f"out_{i}", [128, D], F32) for i in range(4)]
        st7_sb = T(s7, "st7_sb", [128, 32], F32)
        sAld4 = [self.SM(f"sAld4_{i}") for i in range(4)]
        sOut2 = [self.SM(f"sOut2_{i}") for i in range(4)]
        p7 = {"apl": [0, 0, 0, 0], "stdve": [0, 0, 0, 0], "stact": [0, 0],
              "outst": [0, 0, 0, 0], "mic": [0, 0, 0, 0]}

        def p7_a(tt):
            p4_ = tt % 4
            p2_ = tt % 2
            b8 = p4_ * 8
            dma("sync" if p4_ % 2 == 0 else "gpsimd",
                a_sb[p4_][:, :], acc_dram[tt * 128:(tt + 1) * 128, :], sAld4[p4_],
                waits=([(sSA, sa_all)] if tt == 0 else []) + [(sDVE, p7["apl"][p4_])])
            aldv = sAld4[p4_].n
            on("vector", (lambda En, o=st7_sb[:, b8: b8 + 1], i=a_sb[p4_][:, :]:
                          En.reduce_sum(o, i, axis=AX.X)),
               waits=[(sAld4[p4_], aldv), (sDVE, p7["stdve"][p4_])], inc=sDVE)
            on("scalar", (lambda En, o=sq7_sb[p2_][:, :], i=a_sb[p4_][:, :], a=st7_sb[:, b8 + 1: b8 + 2]:
                          En.activation(o, i, AF.Square, accum_out=a)),
               waits=[(sAld4[p4_], aldv), (sACT, p7["stact"][p2_]),
                      (sDVE, p7["stdve"][p4_])], inc=sACT)
            p7["stact"][p2_] = sACT.n
            on("vector", (lambda En, o=st7_sb[:, b8 + 2: b8 + 3], i=st7_sb[:, b8: b8 + 1]:
                          En.tensor_scalar_mul(o, i, 1.0 / D)),
               waits=[(sDVE, sDVE.n)], inc=sDVE)
            on("vector", (lambda En, o=st7_sb[:, b8 + 3: b8 + 4], i=st7_sb[:, b8 + 2: b8 + 3], s=st7_sb[:, b8 + 2: b8 + 3]:
                          En.tensor_scalar(o, i, s, -LNEPS, ALU.mult, ALU.add)),
               waits=[(sDVE, sDVE.n)], inc=sDVE)
            on("vector", (lambda En, o=st7_sb[:, b8 + 4: b8 + 5], i=st7_sb[:, b8 + 1: b8 + 2], s=st7_sb[:, b8 + 3: b8 + 4]:
                          En.tensor_scalar(o, i, 1.0 / D, s, ALU.mult, ALU.subtract)),
               waits=[(sDVE, sDVE.n), (sACT, p7["stact"][p2_])], inc=sDVE)
            p7["mic"][p4_] = sDVE.n

        def p7_b(tt):
            p4_ = tt % 4
            p2_ = tt % 2
            b8 = p4_ * 8
            on("scalar", (lambda En, o=st7_sb[:, b8 + 5: b8 + 6], i=st7_sb[:, b8 + 4: b8 + 5]:
                          En.activation(o, i, AF.Sqrt)),
               waits=[(sDVE, p7["mic"][p4_])], inc=sACT)
            on("vector", (lambda En, o=st7_sb[:, b8 + 6: b8 + 7], i=st7_sb[:, b8 + 5: b8 + 6]:
                          En.reciprocal(o, i)),
               waits=[(sACT, sACT.n)], inc=sDVE)
            p7["stdve"][p4_] = sDVE.n
            on("vector", (lambda En, o=out_sb[p4_][:, :], i=a_sb[p4_][:, :], m=st7_sb[:, b8 + 2: b8 + 3],
                          rr=st7_sb[:, b8 + 6: b8 + 7]:
                          En.tensor_scalar(o, i, m, rr, ALU.subtract, ALU.mult)),
               waits=[(sDVE, sDVE.n), (sOut2[p4_], p7["outst"][p4_])], inc=sDVE)
            p7["apl"][p4_] = sDVE.n
            dma("gpsimd" if p4_ % 2 == 0 else "sync",
                y_out[tt * 128:(tt + 1) * 128, :], out_sb[p4_][:, :], sOut2[p4_],
                waits=[(sDVE, sDVE.n)])
            p7["outst"][p4_] = sOut2[p4_].n

        for g in range(NT // 4):
            for i_ in range(4):
                p7_a(4 * g + i_)
            for i_ in range(4):
                p7_b(4 * g + i_)
        self.on("sync", (lambda En: None),
                waits=[(sOut2[i_], sOut2[i_].n) for i_ in range(4)])

        with nc.Block() as block:
            for eng in self.ENGS:
                getattr(block, eng)(self._runner(eng))
        s7.close()
        ctx.close()
        return nc

    def ln_stats(self, zt, sqt, st, zv, sPE, sACT, sDVE, in_ready=None):
        on = self.on
        D = self.cfg.D
        w0 = in_ready if in_ready is not None else [(sDVE, zv)]
        on("scalar", (lambda En, o=sqt[:, :], i=zt[:, :]: En.activation(o, i, AF.Square)),
           waits=w0 + [(sDVE, sDVE.n)], inc=sACT)
        on("vector", (lambda En, o=st[:, 0:1], i=zt[:, :]: En.reduce_sum(o, i, axis=AX.X)),
           waits=w0 + [(sDVE, sDVE.n)], inc=sDVE)
        on("vector", (lambda En, o=st[:, 1:2], i=sqt[:, :]: En.reduce_sum(o, i, axis=AX.X)),
           waits=[(sACT, sACT.n), (sDVE, sDVE.n)], inc=sDVE)
        on("vector", (lambda En, o=st[:, 2:3], i=st[:, 0:1]: En.tensor_scalar_mul(o, i, 1.0 / D)),
           waits=[(sDVE, sDVE.n)], inc=sDVE)
        on("vector", (lambda En, o=st[:, 3:4], i=st[:, 2:3]:
                      En.tensor_scalar(o, i, st[:, 2:3], -LNEPS, ALU.mult, ALU.add)),
           waits=[(sDVE, sDVE.n)], inc=sDVE)
        on("vector", (lambda En, o=st[:, 4:5], i=st[:, 1:2], s1=st[:, 3:4]:
                      En.tensor_scalar(o, i, 1.0 / D, s1, ALU.mult, ALU.subtract)),
           waits=[(sDVE, sDVE.n)], inc=sDVE)
        on("scalar", (lambda En, o=st[:, 5:6], i=st[:, 4:5]: En.activation(o, i, AF.Sqrt)),
           waits=[(sDVE, sDVE.n)], inc=sACT)
        on("vector", (lambda En, o=st[:, 6:7], i=st[:, 5:6]: En.reciprocal(o, i)),
           waits=[(sACT, sACT.n)], inc=sDVE)

    def _runner(self, eng):
        thunks = self.q[eng]
        def run(E):
            for t in thunks:
                t(E)
        return run


def host_inputs(cfg: Cfg, full):
    c = cfg
    E, C, S, D, NT = c.E, c.C, c.S, c.D, c.NT
    x = np.ascontiguousarray(np.asarray(full["L0_attn_x"], np.float32))
    wqkvT = np.ascontiguousarray(np.asarray(full["L0_attn_Wqkv_w"], np.float32).T)
    bqkv = np.ascontiguousarray(np.asarray(full["L0_attn_Wqkv_b"], np.float32))
    bqkvT = np.ascontiguousarray(bqkv.reshape(3 * D // 128, 128).T)
    woutT = np.ascontiguousarray(np.asarray(full["L0_attn_out_w"], np.float32).T)
    bout = np.ascontiguousarray(np.asarray(full["L0_attn_out_b"], np.float32).reshape(1, D))
    ws = np.ascontiguousarray(np.asarray(full["L0_ffn_Ws"], np.float32))
    bs = np.ascontiguousarray(np.asarray(full["L0_ffn_bs"], np.float32).reshape(1, E))
    w1 = np.ascontiguousarray(np.asarray(full["L0_ffn_W1"], np.float32))
    b1 = np.asarray(full["L0_ffn_b1"], np.float32)
    b1T = np.ascontiguousarray(b1.reshape(E, c.F // 128, 128).transpose(0, 2, 1))
    w2 = np.ascontiguousarray(np.asarray(full["L0_ffn_W2"], np.float32))
    b2 = np.ascontiguousarray(np.asarray(full["L0_ffn_b2"], np.float32).reshape(E, 1, D))
    ident = np.eye(128, dtype=np.float32)
    triu = np.triu(np.ones((128, 128), np.float32), 1)
    import ml_dtypes
    tri01 = np.triu(np.ones((128, 128), np.float32), 0).astype(ml_dtypes.bfloat16)
    tokid = (np.arange(NT)[None, :] * 128 + np.arange(128)[:, None]).astype(np.uint32)
    tokbigE = np.empty((128, NT * E), np.float32)
    for tt in range(NT):
        for e in range(E):
            tokbigE[:, tt * E + e] = BIG + tt * 128 + np.arange(128)
    slotinit = np.empty((E * C, 1), np.uint32)
    for e in range(E):
        slotinit[e * C:(e + 1) * C, 0] = S + np.arange(C)
    maps = []
    for cid in range(c.NC):
        rowidx = np.full((128, 1), 0xFFFFF0, np.uint32)
        rowidx[:c.NC, 0] = np.arange(c.NC) * E + cid
        maps.append({
            "x": x[cid], "xT": np.ascontiguousarray(x[cid].T),
            "wqkvT": wqkvT, "bqkvT": bqkvT, "bqkv_row": bqkv.reshape(1, 3 * D),
            "woutT": woutT, "bout_row": bout, "ws": ws, "bs_row": bs,
            "w1": w1, "b1T": b1T, "w2": w2, "b2_row": b2,
            "ident": ident, "triu": triu,
            "tri01": tri01, "tokid": tokid, "tokbigE": tokbigE, "slotinit": slotinit,
            "rowidx": rowidx,
        })
    return maps


_NC_CACHE = {}

def _get_nc():
    if "nc" not in _NC_CACHE:
        cfg = Cfg(S=4096, D=1024, H=16, F=4096, E=8, NC=8, TB=1024, C=640, BIS=30)
        mk = MK(cfg)
        _NC_CACHE["cfg"] = cfg
        _NC_CACHE["nc"] = mk.build()
    return _NC_CACHE["cfg"], _NC_CACHE["nc"]


def kernel(L0_attn_x, L0_attn_Wqkv_w, L0_attn_Wqkv_b, L0_attn_out_w, L0_attn_out_b,
           L0_ffn_Ws, L0_ffn_bs, L0_ffn_W1, L0_ffn_b1, L0_ffn_W2, L0_ffn_b2):
    from concourse.bass_utils import run_bass_kernel_spmd
    cfg, nc = _get_nc()
    full = {
        "L0_attn_x": L0_attn_x, "L0_attn_Wqkv_w": L0_attn_Wqkv_w,
        "L0_attn_Wqkv_b": L0_attn_Wqkv_b, "L0_attn_out_w": L0_attn_out_w,
        "L0_attn_out_b": L0_attn_out_b, "L0_ffn_Ws": L0_ffn_Ws,
        "L0_ffn_bs": L0_ffn_bs, "L0_ffn_W1": L0_ffn_W1, "L0_ffn_b1": L0_ffn_b1,
        "L0_ffn_W2": L0_ffn_W2, "L0_ffn_b2": L0_ffn_b2,
    }
    maps = host_inputs(cfg, full)
    res = run_bass_kernel_spmd(nc, maps, core_ids=list(range(cfg.NC)))
    out = np.stack([np.asarray(res.results[c]["y"], dtype=np.float32) for c in range(cfg.NC)], axis=0)
    return out


# version 46 (final)
# speedup vs baseline: 1.0164x; 1.0142x over previous
import numpy as np
from contextlib import ExitStack
import concourse.bass as bass
import concourse.mybir as mybir
from concourse.bass import IndirectOffsetOnAxis

F32 = mybir.dt.float32
BF16 = mybir.dt.bfloat16
U32 = mybir.dt.uint32
AF = mybir.ActivationFunctionType
ALU = mybir.AluOpType
AX = mybir.AxisListType

GELU_C = 0.044715
GELU_S2 = 1.5957691216057308
LNEPS = 1e-5
BIG = float(1 << 20)


class Cfg:
    def __init__(self, S=4096, D=1024, H=16, F=4096, E=8, NC=8, TB=1024, C=640, BIS=30):
        self.S, self.D, self.H, self.F, self.E, self.NC = S, D, H, F, E, NC
        self.TB, self.C, self.BIS = TB, C, BIS
        self.HD = 64
        assert D == H * self.HD
        self.DT = D // 128
        self.NT = S // 128
        self.FT = F // 128
        self.ST = S // 128
        self.CT = C // 128
        self.NTB = S // TB
        self.N = NC * S
        self.KSEL = self.N // E
        self.TQ = min(1024, S)
        self.WC = min(512, F)
        assert C % 128 == 0 and TB % 512 == 0 and S % TB == 0 and TB <= 1024
        assert self.N % 128 == 0 and (self.N // 128) >= 1
        assert 128 % NC == 0


class Sem:
    def __init__(self, ctx, nc, name):
        self.h = ctx.enter_context(nc.semaphore(name))
        self.n = 0


def chunks(lo, hi, step=512):
    c = lo
    while c < hi:
        c1 = min(hi, (c // step) * step + step)
        yield c, c1
        c = c1


class MK:
    ENGS = ("tensor", "scalar", "vector", "gpsimd", "sync")

    def __init__(self, cfg: Cfg):
        self.cfg = cfg
        self.q = {e: [] for e in self.ENGS}
        self.sems = []

    def on(self, eng, build, waits=(), inc=None, inc_by=1):
        waits = tuple((s.h, v) for (s, v) in waits if v > 0)
        def thunk(E, build=build, waits=waits, inc=inc, inc_by=inc_by):
            for h, v in waits:
                E.wait_ge(h, v)
            inst = build(E)
            if inc is not None and inst is not None:
                inst.then_inc(inc.h, inc_by)
        self.q[eng].append(thunk)
        if inc is not None:
            inc.n += inc_by
        return inc.n if inc else None

    def barrier(self):
        vals = [(s, s.n) for s in self.sems if s.n > 0]
        for eng in self.ENGS:
            self.on(eng, (lambda En: None), waits=vals)

    def SM(self, name):
        s = Sem(self.ctx, self.nc, name)
        self.sems.append(s)
        return s

    def dma(self, eng, out, in_, sem, waits=()):
        self.on(eng, (lambda En, o=out, i=in_: En.dma_start(out=o, in_=i)),
                waits=waits, inc=sem, inc_by=16)

    def build(self):
        c = self.cfg
        S, D, H, F, E, TB, C = c.S, c.D, c.H, c.F, c.E, c.TB, c.C
        DT, NT, FT, ST, CT, NTB, TQ, WC = c.DT, c.NT, c.FT, c.ST, c.CT, c.NTB, c.TQ, c.WC
        nc = bass.Bass("TRN2", target_bir_lowering=False)
        self.nc = nc
        self.ctx = ExitStack()
        ctx = self.ctx

        ap = lambda t: t.ap()
        x_in = ap(nc.dram_tensor("x", [S, D], F32, kind="ExternalInput"))
        xT_in = ap(nc.dram_tensor("xT", [D, S], F32, kind="ExternalInput"))
        wqkvT_in = ap(nc.dram_tensor("wqkvT", [D, 3 * D], F32, kind="ExternalInput"))
        bqkvT_in = ap(nc.dram_tensor("bqkvT", [128, 3 * D // 128], F32, kind="ExternalInput"))
        bqkv_row_in = ap(nc.dram_tensor("bqkv_row", [1, 3 * D], F32, kind="ExternalInput"))
        woutT_in = ap(nc.dram_tensor("woutT", [D, D], F32, kind="ExternalInput"))
        bout_row_in = ap(nc.dram_tensor("bout_row", [1, D], F32, kind="ExternalInput"))
        ws_in = ap(nc.dram_tensor("ws", [D, E], F32, kind="ExternalInput"))
        bs_row_in = ap(nc.dram_tensor("bs_row", [1, E], F32, kind="ExternalInput"))
        w1_in = ap(nc.dram_tensor("w1", [E, D, F], F32, kind="ExternalInput"))
        b1T_in = ap(nc.dram_tensor("b1T", [E, 128, F // 128], F32, kind="ExternalInput"))
        w2_in = ap(nc.dram_tensor("w2", [E, F, D], F32, kind="ExternalInput"))
        b2_row_in = ap(nc.dram_tensor("b2_row", [E, 1, D], F32, kind="ExternalInput"))
        ident_in = ap(nc.dram_tensor("ident", [128, 128], F32, kind="ExternalInput"))
        tri01_in = ap(nc.dram_tensor("tri01", [128, 128], BF16, kind="ExternalInput"))
        triu_in = ap(nc.dram_tensor("triu", [128, 128], F32, kind="ExternalInput"))
        tokid_in = ap(nc.dram_tensor("tokid", [128, NT], U32, kind="ExternalInput"))
        tokbigE_in = ap(nc.dram_tensor("tokbigE", [128, NT * E], F32, kind="ExternalInput"))
        slotinit_in = ap(nc.dram_tensor("slotinit", [E * C, 1], U32, kind="ExternalInput"))
        rowidx_in = ap(nc.dram_tensor("rowidx", [128, 1], U32, kind="ExternalInput"))
        y_out = ap(nc.dram_tensor("y", [S, D], F32, kind="ExternalOutput"))

        q_dram = ap(nc.dram_tensor("q_dram", [D, S], BF16))
        k_dram = ap(nc.dram_tensor("k_dram", [D, S], BF16))
        vp_dram = ap(nc.dram_tensor("vp_dram", [ST, 128, H * 128], BF16))
        x1_dram = ap(nc.dram_tensor("x1_dram", [S, D], BF16))
        acc_dram = ap(nc.dram_tensor("acc_dram", [S + C, D], F32))
        g_dram = ap(nc.dram_tensor("g_dram", [S + C, E], F32))
        slot_dram = ap(nc.dram_tensor("slot_dram", [E * C, 1], U32))
        pin_dram = ap(nc.dram_tensor("pin_dram", [E, S], F32))
        pout_dram = ap(nc.dram_tensor("pout_dram", [c.NC * E, S], F32))
        pe_dram = ap(nc.dram_tensor("pe_dram", [c.NC, S], F32))
        tin_dram = ap(nc.dram_tensor("tin_dram", [1, 1], F32))
        nst1_dram = [ap(nc.dram_tensor(f"nst1_{i}", [1, TB], F32)) for i in range(4)]
        nst2_dram = [ap(nc.dram_tensor(f"nst2_{i}", [1, TB], F32)) for i in range(4)]
        tout_dram = ap(nc.dram_tensor("tout_dram", [c.NC, 1], F32))

        T = lambda scope, name, shape, dt: scope.enter_context(nc.sbuf_tensor(name, shape, dt))

        psA = ctx.enter_context(nc.psum_tensor("psA", [128, 2048], F32))
        psB = ctx.enter_context(nc.psum_tensor("psB", [128, 2048], F32))

        P = ctx
        ident_sb = T(P, "ident_sb", [128, 128], F32)
        tri01_sb = T(P, "tri01_sb", [128, 128], BF16)
        triu_sb = T(P, "triu_sb", [128, 128], F32)
        tokid_sb = T(P, "tokid_sb", [128, NT], U32)
        tokbigE_sb = T(P, "tokbigE_sb", [128, NT * E], F32)
        bqkvT_sb = T(P, "bqkvT_sb", [128, 3 * D // 128], F32)
        bqkvTk_sb = T(P, "bqkvTk_sb", [128, D // 128], F32)
        bv_sb = T(P, "bv_sb", [1, D], BF16)
        bout_sb = T(P, "bout_sb", [1, D], BF16)
        bs_sb = T(P, "bs_sb", [1, E], F32)
        b1T_sb = T(P, "b1T_sb", [128, E * FT], F32)
        ones_col_f = T(P, "ones_col_f", [128, 1], F32)
        ones_row_f = T(P, "ones_row_f", [1, 128], F32)
        ones_row_b = T(P, "ones_row_b", [1, 128], BF16)
        ws_sb = T(P, "ws_sb", [128, DT * E], F32)
        probs_sb = T(P, "probs_sb", [128, NT * E], F32)
        mask_sb = T(P, "mask_sb", [128, NT * E], F32)
        idxu_sb = T(P, "idxu_sb", [128, NT * E], U32)
        sidx_sb = T(P, "sidx_sb", [128, E * CT + 1], U32)
        st1_sb = T(P, "st1_sb", [128, 8], F32)

        sC = self.SM("sC")
        sG = self.SM("sG")
        sPE = self.SM("sPE")
        sACT = self.SM("sACT")
        sDVE = self.SM("sDVE")
        sQKst = [self.SM("sQKst0"), self.SM("sQKst1")]
        sVst = [self.SM("sVst0"), self.SM("sVst1")]
        sKld = [self.SM("sKld0"), self.SM("sKld1")]
        sXld = [self.SM("sXld0"), self.SM("sXld1")]
        sX1st = self.SM("sX1st")
        sCC = self.SM("sCC")
        sPr = self.SM("sPr")
        sBis = self.SM("sBis")
        sSc = [self.SM(f"sSc{e}") for e in range(E)]
        sSld = [self.SM(f"sSld{e}") for e in range(E)]
        sXg = [self.SM("sXg0"), self.SM("sXg1")]
        sGg = [self.SM("sGg0"), self.SM("sGg1")]
        sW1 = [self.SM("sW10"), self.SM("sW11")]
        sW2 = self.SM("sW2")
        sSA = self.SM("sSA")
        sAld = [self.SM("sAld0"), self.SM("sAld1")]
        sOut = self.SM("sOut")
        on, dma = self.on, self.dma
        regs = {}

        def mkregs(En):
            regs["C1"] = En.to_reg(C - 1)
            regs["S1"] = En.to_reg(S - 1)
            regs["R1"] = En.to_reg(c.NC * E - 1)
        on("gpsimd", (lambda En: mkregs(En)))

        dma("sync", ident_sb[:, :], ident_in, sC)
        dma("sync", tri01_sb[:, :], tri01_in, sC)
        dma("sync", triu_sb[:, :], triu_in, sC)
        dma("sync", tokid_sb[:, :], tokid_in, sC)
        dma("sync", tokbigE_sb[:, :], tokbigE_in, sC)
        dma("sync", bqkvT_sb[:, :], bqkvT_in, sC)
        dma("sync", bs_sb[:, :], bs_row_in, sC)
        dma("sync", ws_sb[:, :].rearrange("p (dt e) -> p dt e", e=E),
            ws_in.rearrange("(dt p) e -> p dt e", p=128), sC)
        dma("sync", sidx_sb[:, E * CT:E * CT + 1], rowidx_in, sC)
        nconst = sC.n

        dma("gpsimd", bv_sb[:, :], bqkv_row_in[:, 2 * D:3 * D], sG)
        dma("gpsimd", bout_sb[:, :], bout_row_in, sG)
        dma("gpsimd", b1T_sb[:, :].rearrange("p (e f) -> p e f", e=E),
            b1T_in.rearrange("e p f -> p e f"), sG)
        dma("gpsimd", slot_dram, slotinit_in, sG)
        nG0 = sG.n

        on("vector", lambda En: En.memset(ones_col_f[:, :], 1.0), inc=sDVE)
        on("vector", lambda En: En.memset(ones_row_f[:, :], 1.0), inc=sDVE)
        on("vector", lambda En: En.memset(ones_row_b[:, :], 1.0), inc=sDVE)
        on("vector", (lambda En: En.tensor_scalar_mul(bqkvTk_sb[:, :], bqkvT_sb[:, D // 128:2 * D // 128], 0.125)),
           waits=[(sC, nconst)], inc=sDVE)
        nset = sDVE.n

        scopeA = ExitStack()
        ctx3_sb = T(scopeA, "ctx3_sb", [128, (H // 2) * S], BF16)
        k_sb0 = T(scopeA, "k_sb0", [128, S], BF16)
        sVp0 = self.SM("sVp0")

        s1 = ExitStack()
        wqkv_sb = T(s1, "wqkv_sb", [128, DT * 3 * D], BF16)
        xTb_sb = T(s1, "xTb_sb", [128, DT * S], BF16)
        stg_sb = [T(s1, f"stg{i}", [128, TQ], BF16) for i in range(2)]
        vst_sb = [T(s1, f"vst{i}", [128, H * 128], BF16) for i in range(2)]

        sGp = [self.SM(f"sGp{i}") for i in range(DT)]
        for dt_ in range(DT):
            dma("gpsimd", wqkv_sb[:, :].rearrange("p (dt o) -> p dt o", o=3 * D)[:, dt_:dt_ + 1, :],
                wqkvT_in.rearrange("(dt p) o -> p dt o", p=128)[:, dt_:dt_ + 1, :], sGp[dt_])
            dma("gpsimd", xTb_sb[:, :].rearrange("p (dt t) -> p dt t", t=S)[:, dt_:dt_ + 1, :],
                xT_in.rearrange("(dt p) t -> p dt t", p=128)[:, dt_:dt_ + 1, :], sGp[dt_])
        nG1 = sG.n

        OT2 = 2 * D // 128
        ot_order = list(range(D // 128, OT2)) + list(range(D // 128))
        p1m = {"it": 0}

        def emit_qk_ot(ot):
            is_k = ot >= D // 128
            for tq in range(S // TQ):
                par = p1m["it"] % 2
                ps = psA[:, par * 1024: par * 1024 + TQ]
                for dt in range(DT):
                    for c0, c1 in chunks(0, TQ):
                        w = []
                        if p1m["it"] == 0 and c0 == 0:
                            w = [(sGp[dt], 32), (sDVE, sDVE.n)]
                        elif dt == 0 and c0 == 0:
                            w = [(sGp[DT - 1], 32), (sDVE, sDVE.n), (sACT, sACT.n)]
                        on("tensor", (lambda En, o=ps[:, c0:c1],
                                      l=wqkv_sb[:, dt * 3 * D + ot * 128: dt * 3 * D + (ot + 1) * 128],
                                      r=xTb_sb[:, dt * S + tq * TQ + c0: dt * S + tq * TQ + c1],
                                      st=(dt == 0), sp=(dt == DT - 1):
                                      En.matmul(o, l, r, start=st, stop=sp)),
                           waits=w, inc=(sPE if (dt == DT - 1 and c1 == TQ) else None))
                bt = bqkvTk_sb[:, ot - D // 128: ot - D // 128 + 1] if is_k else bqkvT_sb[:, ot:ot + 1]
                sc = 0.125 if is_k else 1.0
                on("vector", (lambda En, o=stg_sb[par][:, :], i=ps, b=bt, s=sc:
                              En.tensor_scalar(o, i, b, s, ALU.add, ALU.mult)),
                   waits=[(sPE, sPE.n), (sC, nconst), (sDVE, nset), (sQKst[par], sQKst[par].n)],
                   inc=sDVE)
                dst = (k_dram if is_k else q_dram)
                row0 = (ot - D // 128 if is_k else ot) * 128
                dma("sync" if par == 0 else "gpsimd",
                    dst[row0:row0 + 128, tq * TQ:(tq + 1) * TQ], stg_sb[par][:, :],
                    sQKst[par], waits=[(sDVE, sDVE.n)])
                p1m["it"] += 1

        for ot in ot_order[:D // 128]:
            emit_qk_ot(ot)
            if ot == D // 128:
                p1m["k0"] = [(sQKst[0], sQKst[0].n), (sQKst[1], sQKst[1].n)]
        for i in range(2):
            on("vector", (lambda En, o=vst_sb[i][:, :]: En.memset(o, 0.0)), inc=sDVE)
            on("vector", (lambda En, o=vst_sb[i].ap().rearrange("p (h x) -> p h x", x=128)[:, :, 64:65]:
                          En.memset(o, 1.0)),
               waits=[(sDVE, sDVE.n)], inc=sDVE)
        vst_init = sDVE.n
        for st_i in range(ST):
            par = st_i % 2
            ps = psA[:, par * 1024: par * 1024 + D]
            for dt in range(DT):
                for c0, c1 in chunks(0, D):
                    w = []
                    if dt == 0 and c0 == 0:
                        w = [(sDVE, sDVE.n), (sACT, sACT.n)]
                    on("tensor", (lambda En, o=ps[:, c0:c1],
                                  l=xTb_sb[:, dt * S + st_i * 128: dt * S + (st_i + 1) * 128],
                                  r=wqkv_sb[:, dt * 3 * D + 2 * D + c0: dt * 3 * D + 2 * D + c1],
                                  st=(dt == 0): En.matmul(o, l, r, start=st, stop=False)),
                       waits=w, inc=None)
            for c0, c1 in chunks(0, D):
                on("tensor", (lambda En, o=ps[:, c0:c1], l=ones_row_b[0:1, 0:128], r=bv_sb[0:1, c0:c1]:
                              En.matmul(o, l, r, start=False, stop=True)),
                   waits=([(sDVE, nset), (sG, sG.n)] if (st_i == 0 and c0 == 0) else []),
                   inc=(sPE if c1 == D else None))
            on("scalar", (lambda En, o=vst_sb[par][:, :].rearrange("p (h x) -> p h x", x=128)[:, :, 0:64],
                          i=ps.rearrange("p (h x) -> p h x", x=64):
                          En.activation(o, i, AF.Copy)),
               waits=[(sPE, sPE.n), (sVst[par], sVst[par].n), (sDVE, vst_init)], inc=sACT)
            dma("sync" if par == 0 else "gpsimd",
                vp_dram[st_i], vst_sb[par][:, :], sVst[par], waits=[(sACT, sACT.n)])
        v_all = [(sVst[0], sVst[0].n), (sVst[1], sVst[1].n)]
        for ot in ot_order[D // 128:]:
            emit_qk_ot(ot)
            if ot == 0:
                dma("sync", k_sb0[0:64, :], k_dram[0:64, :], sKld[0], waits=p1m["k0"])
        qk_all = [(sQKst[0], sQKst[0].n), (sQKst[1], sQKst[1].n)]
        self.barrier()
        s1.close()

        s2 = ExitStack()
        k_sb = [k_sb0, T(s2, "k_sb1", [128, S], BF16)]
        vp_sb = [T(s2, f"vp_sb{i}", [128, ST * 128], BF16) for i in range(2)]
        qh_sb = [T(s2, f"qh_sb{i}", [128, S], BF16) for i in range(2)]
        qw_sb = [T(s2, f"qw_sb{i}", [128, S], BF16) for i in range(2)]
        e_sb = [T(s2, f"e_sb{i}", [128, 2 * TB], BF16) for i in range(4)]
        ctxu_sb = [T(s2, f"ctxu{i}", [65, TB], F32) for i in range(4)]
        denp_sb = [T(s2, f"denp{i}", [128, TB // 128], F32) for i in range(4)]
        recp_sb = [T(s2, f"recp{i}", [128, TB // 128], F32) for i in range(4)]
        bden_sb = [T(s2, f"bden{i}", [64, TB], F32) for i in range(4)]
        sNrm4 = [self.SM(f"sNrm{i}") for i in range(4)]
        sPool = self.SM("sPool")

        for tt_ in (k_sb[0], k_sb[1], qh_sb[0], qh_sb[1], qw_sb[0], qw_sb[1]):
            on("vector", (lambda En, o=tt_[64:128, :]: En.memset(o, 0.0)), inc=sDVE)
        pad_init = sDVE.n
        last_head_done = [0, 0]
        ectx_done = {0: 0, 1: 0, 2: 0, 3: 0}

        def head_loads(h, par):
            w0 = qk_all + ([(sPE, last_head_done[par])] if last_head_done[par] else [])
            dma("sync", k_sb[par][0:64, :], k_dram[h * 64:(h + 1) * 64, :], sKld[par],
                waits=w0 + [(sDVE, pad_init)])
            dma("sync", vp_sb[par][:, :],
                vp_dram[:, :, h * 128:(h + 1) * 128].rearrange("st p x -> p st x"),
                sKld[par], waits=v_all)
            dma("sync", qh_sb[par][0:64, :], q_dram[h * 64:(h + 1) * 64, :], sKld[par])

        dma("sync", qh_sb[0][0:64, :], q_dram[0:64, :], sKld[0])
        dma("sync", vp_sb[0][:, :],
            vp_dram[:, :, 0:128].rearrange("st p x -> p st x"), sVp0, waits=v_all)
        nrm = {"gtb": 0, "chains": [], "multv": {}, "mult4": [0, 0, 0, 0],
               "ctxevac": [0, 0], "recipv": [0, 0, 0, 0], "d3v": [0, 0, 0, 0]}

        def chain_advance_all(flush=False):
            if flush:
                while nrm["chains"]:
                    for chn in list(nrm["chains"]):
                        _adv_one(chn)
            else:
                for chn in list(nrm["chains"]):
                    _adv_one(chn)

        def _adv_one(chn):
            st = chn["step"]
            p4 = chn["p4"]
            sN = sNrm4[p4]
            if st == 0:
                dma("sync", nst1_dram[p4], ctxu_sb[p4][64:65, 0:TB], sN,
                    waits=[(sDVE, chn["evacv"])])
            elif st == 1:
                dma("sync", denp_sb[p4][:, :],
                    nst1_dram[p4].rearrange("one (p f) -> (one p) f", p=128), sN,
                    waits=[(sN, chn["v"])])
            elif st == 2:
                on("vector", (lambda En, o=recp_sb[p4][:, :], i=denp_sb[p4][:, :]:
                              En.reciprocal(o, i)),
                   waits=[(sN, chn["v"])], inc=sDVE)
                nrm["recipv"][p4] = sDVE.n
                dma("sync", nst2_dram[p4], recp_sb[p4][:, :], sN,
                    waits=[(sDVE, sDVE.n)])
            elif st == 3:
                dma("sync", bden_sb[p4][:, :], nst2_dram[p4].to_broadcast([64, TB]), sN,
                    waits=[(sN, chn["v"]), (sDVE, nrm["mult4"][p4])])
            elif st == 4:
                on("vector", (lambda En, o=chn["dst"], i0=ctxu_sb[p4][0:64, 0:TB], i1=bden_sb[p4][:, :]:
                              En.tensor_tensor(o, i0, i1, ALU.mult)),
                   waits=[(sN, chn["v"]), (sDVE, sDVE.n)], inc=sDVE)
                nrm["multv"][chn["key"]] = sDVE.n
                nrm["mult4"][p4] = sDVE.n
                nrm["chains"].remove(chn)
                chn["step"] = 5
                return
            chn["step"] += 1
            chn["v"] = sN.n

        exp_done = [0, 0]
        nrm["pend"] = []

        def emit_ctx(pc):
            cslot = pc["cslot"]
            for c0, c1 in chunks(pc["a0"], TB):
                bank_last = pc["t0b"] // 128 + (c0 // 512) * 4 + 3
                on("tensor", (lambda En, o=psB[0:128, cslot * 1024 + c0: cslot * 1024 + c1],
                              l=vp_sb[pc["par"]][:, pc["si"] * 128:(pc["si"] + 1) * 128],
                              rr=e_sb[pc["ep"]][:, c0:c1], st=pc["start"],
                              sp=(pc["si"] == bank_last):
                              En.matmul(o, l, rr, start=st, stop=sp)),
                   waits=(pc["eready"] + [(sDVE, nrm["ctxevac"][cslot])]) if c0 == pc["a0"] else [],
                   inc=(sPE if c1 == TB else None))
            ectx_done[pc["ep"]] = sPE.n
            if pc["stop"]:
                p4 = pc["p4"]
                on("vector", (lambda En, o=ctxu_sb[p4][:, 0:TB], i=psB[0:65, cslot * 1024: cslot * 1024 + TB]:
                              En.tensor_copy(o, i)),
                   waits=[(sPE, sPE.n), (sDVE, nrm["mult4"][p4])],
                   inc=sDVE)
                nrm["ctxevac"][cslot] = sDVE.n
                chn = {"step": 0, "p4": p4, "v": sNrm4[p4].n, "evacv": sDVE.n,
                       "dst": pc["dst"], "key": pc["key"]}
                nrm["chains"].append(chn)
                _adv_one(chn)

        for h in range(H):
            par = h % 2
            if h + 1 < H:
                head_loads(h + 1, (h + 1) % 2)
            for r in range(1, 4):
                qsrc = qh_sb[par] if r == 1 else qw_sb[r % 2]
                for tb in range(NTB):
                    chain_advance_all()
                    chain_advance_all()
                    t0b, t1b = tb * TB, (tb + 1) * TB
                    s_last = t1b // 128
                    gtb = nrm["gtb"]
                    cslot = gtb % 2
                    p4 = gtb % 4
                    if r < 3:
                        dst = qw_sb[(r + 1) % 2][0:64, t0b:t1b]
                    else:
                        dst = ctx3_sb[(h % 2) * 64:(h % 2) * 64 + 64, (h // 2) * S + t0b:(h // 2) * S + t1b]

                    for si in range(s_last):
                        t0 = max(t0b, si * 128)
                        a0 = t0 - t0b
                        ep = si % 4
                        sslot = si % 2
                        first = True
                        for c0, c1 in chunks(a0, TB):
                            w = []
                            if first:
                                if r > 1 and (h, r - 1, tb) not in nrm["multv"]:
                                    while any(pc["key"] == (h, r - 1, tb) for pc in nrm["pend"]):
                                        emit_ctx(nrm["pend"].pop(0))
                                    for chn in list(nrm["chains"]):
                                        if chn["key"] == (h, r - 1, tb):
                                            while chn["step"] < 5:
                                                _adv_one(chn)
                                w = [(sACT, exp_done[sslot]), (sKld[par], sKld[par].n)]
                                if h == 0 and r == 1 and tb == 0 and si == 0:
                                    w.append((sDVE, pad_init))
                                if r > 1:
                                    w.append((sDVE, nrm["multv"][(h, r - 1, tb)]))
                            on("tensor", (lambda En, o=psA[:, sslot * 1024 + c0: sslot * 1024 + c1],
                                          l=k_sb[par][0:128, si * 128:(si + 1) * 128],
                                          rr=qsrc[0:128, t0b + c0: t0b + c1]:
                                          En.matmul(o, l, rr, start=True, stop=True)),
                               waits=w, inc=(sPE if c1 == TB else None))
                            first = False
                        spe_scores = sPE.n
                        while any(pc["ep"] == ep for pc in nrm["pend"]):
                            emit_ctx(nrm["pend"].pop(0))
                        on("scalar", (lambda En, o=e_sb[ep][:, a0:TB], i=psA[:, sslot * 1024 + a0: sslot * 1024 + TB]:
                                      En.activation(o, i, AF.Exp)),
                           waits=[(sPE, spe_scores), (sPE, ectx_done[ep])],
                           inc=sACT)
                        exp_done[sslot] = sACT.n
                        has_diag = si * 128 >= t0b
                        if has_diag:
                            on("gpsimd", (lambda En, o=e_sb[ep][:, a0:a0 + 128], m=tri01_sb[:, :]:
                                          En.tensor_tensor(o, o, m, ALU.mult)),
                               waits=[(sACT, sACT.n), (sC, nconst)], inc=sPool)
                        eready = [(sPool, sPool.n)] if has_diag else [(sACT, sACT.n)]
                        if h == 0 and r == 1 and tb == 0 and si == 0:
                            eready = eready + [(sVp0, 16)]
                        nrm["pend"].append({"si": si, "a0": a0, "ep": ep, "eready": eready,
                                            "cslot": cslot, "par": par, "p4": p4, "t0b": t0b,
                                            "start": si == 0, "stop": si == s_last - 1,
                                            "dst": dst, "key": (h, r, tb)})
                        if len(nrm["pend"]) > 3:
                            emit_ctx(nrm["pend"].pop(0))
                    nrm["gtb"] += 1
            while any(pc["key"][0] == h for pc in nrm["pend"]):
                emit_ctx(nrm["pend"].pop(0))
            last_head_done[par] = sPE.n
        for pc in nrm["pend"]:
            emit_ctx(pc)
        nrm["pend"] = []
        chain_advance_all(flush=True)
        self.barrier()
        s2.close()

        s3 = ExitStack()
        wout_sb = T(s3, "wout_sb", [128, DT * D], BF16)
        x_sb = [T(s3, f"x_sb{i}", [128, D], F32) for i in range(2)]
        z_sb = [T(s3, f"z_sb{i}", [128, D], F32) for i in range(2)]
        sq_sb = [T(s3, f"sq_sb{i}", [128, D], F32) for i in range(2)]
        st3_sb = T(s3, "st3_sb", [128, 16], F32)
        x1f_sb = [T(s3, f"x1f{i}", [128, D], F32) for i in range(2)]
        x1b_sb = [T(s3, f"x1b{i}", [128, D], BF16) for i in range(2)]
        x1T_sb = [T(s3, f"x1T{i}", [128, 128], F32) for i in range(2)]
        e8_sb = [T(s3, f"e8_{i}", [128, E], F32) for i in range(2)]
        lgc_sb = [T(s3, f"lgc_{i}", [128, E], F32) for i in range(2)]
        probsT_sb = T(s3, "probsT_sb", [8, S], F32)
        zer_sb = T(s3, "zer_sb", [128, D], F32)

        dma("gpsimd", wout_sb[:, :].rearrange("p (dt o) -> p dt o", o=D),
            woutT_in.rearrange("(dt p) o -> p dt o", p=128), sG)
        on("vector", lambda En: En.memset(zer_sb[:, :], 0.0), inc=sDVE)

        p3 = {"zadd": [0, 0], "stf": [(0, 0), (0, 0)], "x1use": [0, 0],
              "trv": [0, 0], "lgread": (0, 0), "prtr": 0, "x1v": [0, 0],
              "prv": {}, "zv": [0, 0], "lgv": [0, 0], "mic": [0, 0],
              "rmaxv": [0, 0], "e8rd": [0, 0], "e8v": [0, 0]}
        x1_store_cnt = 0

        def p3_head(tt):
            par = tt % 2
            dma("gpsimd", x_sb[par][:, :], x_in[tt * 128:(tt + 1) * 128, :], sXld[par],
                waits=[(sDVE, p3["zadd"][par])])
            ps = psA[:, par * 1024: par * 1024 + D]
            for pr in range(H // 2):
                for c0, c1 in chunks(0, D):
                    w = []
                    if pr == 0 and c0 == 0:
                        w = [(sG, sG.n), (sDVE, p3["zadd"][par])]
                        if tt < 2:
                            w.append((sDVE, sDVE.n))
                    on("tensor", (lambda En, o=ps[:, c0:c1],
                                  l=ctx3_sb[:, pr * S + tt * 128: pr * S + (tt + 1) * 128],
                                  rr=wout_sb[:, pr * D + c0: pr * D + c1],
                                  st=(pr == 0): En.matmul(o, l, rr, start=st, stop=False)),
                       waits=w, inc=None)
            for c0, c1 in chunks(0, D):
                on("tensor", (lambda En, o=ps[:, c0:c1], l=ones_row_b[0:1, 0:128], rr=bout_sb[0:1, c0:c1]:
                              En.matmul(o, l, rr, start=False, stop=True)),
                   waits=([(sG, sG.n)] if tt == 0 and c0 == 0 else []),
                   inc=(sPE if c1 == D else None))
            on("vector", (lambda En, o=z_sb[par][:, :], i0=ps, i1=x_sb[par][:, :]:
                          En.tensor_tensor(o, i0, i1, ALU.add)),
               waits=[(sPE, sPE.n), (sXld[par], sXld[par].n), (sACT, p3["stf"][par][1])], inc=sDVE)
            zv = sDVE.n
            p3["zadd"][par] = zv
            p3["zv"][par] = zv

        def p3_stats(tt):
            par = tt % 2
            zv = p3["zv"][par]
            on("scalar", (lambda En, o=sq_sb[par][:, :], i=z_sb[par][:, :], a=st3_sb[:, par * 8: par * 8 + 1]:
                          En.activation(o, i, AF.Copy, accum_out=a)),
               waits=[(sDVE, zv)], inc=sACT)
            on("scalar", (lambda En, o=sq_sb[par][:, :], i=z_sb[par][:, :], a=st3_sb[:, par * 8 + 1: par * 8 + 2]:
                          En.activation(o, i, AF.Square, accum_out=a)),
               waits=[(sACT, sACT.n)], inc=sACT)
            stv = sACT.n
            b8 = par * 8
            on("vector", (lambda En, o=st3_sb[:, b8 + 2: b8 + 3], i=st3_sb[:, b8: b8 + 1]:
                          En.tensor_scalar_mul(o, i, 1.0 / D)),
               waits=[(sACT, stv), (sDVE, sDVE.n)], inc=sDVE)
            on("vector", (lambda En, o=st3_sb[:, b8 + 3: b8 + 4], i=st3_sb[:, b8 + 2: b8 + 3], s=st3_sb[:, b8 + 2: b8 + 3]:
                          En.tensor_scalar(o, i, s, -LNEPS, ALU.mult, ALU.add)),
               waits=[(sDVE, sDVE.n)], inc=sDVE)
            on("vector", (lambda En, o=st3_sb[:, b8 + 4: b8 + 5], i=st3_sb[:, b8 + 1: b8 + 2], s=st3_sb[:, b8 + 3: b8 + 4]:
                          En.tensor_scalar(o, i, 1.0 / D, s, ALU.mult, ALU.subtract)),
               waits=[(sDVE, sDVE.n)], inc=sDVE)
            p3["mic"][par] = sDVE.n

        def p3_fin(tt):
            par = tt % 2
            b8 = par * 8
            on("scalar", (lambda En, o=st3_sb[:, b8 + 5: b8 + 6], i=st3_sb[:, b8 + 4: b8 + 5]:
                          En.activation(o, i, AF.Sqrt)),
               waits=[(sDVE, p3["mic"][par])], inc=sACT)
            on("vector", (lambda En, o=st3_sb[:, b8 + 6: b8 + 7], i=st3_sb[:, b8 + 5: b8 + 6]:
                          En.reciprocal(o, i)),
               waits=[(sACT, sACT.n)], inc=sDVE)
            p3["stf"][par] = (sDVE.n, sACT.n)
            on("vector", (lambda En, o=x1f_sb[par][:, :], i=z_sb[par][:, :], m=st3_sb[:, b8 + 2: b8 + 3],
                          rr=st3_sb[:, b8 + 6: b8 + 7]:
                          En.tensor_scalar(o, i, m, rr, ALU.subtract, ALU.mult)),
               waits=[(sDVE, sDVE.n), (sX1st, sX1st.n), (sPE, p3["x1use"][par])], inc=sDVE)
            x1v = sDVE.n
            on("vector", (lambda En, o=x1b_sb[par][:, :], i=x1f_sb[par][:, :]: En.tensor_copy(o, i)),
               waits=[(sDVE, x1v), (sX1st, sX1st.n)], inc=sDVE)
            dma("sync", x1_dram[tt * 128:(tt + 1) * 128, :], x1b_sb[par][:, :], sX1st,
                waits=[(sDVE, sDVE.n)])
            dma("sync", acc_dram[tt * 128:(tt + 1) * 128, :], x1f_sb[par][:, :], sX1st)
            p3["x1v"][par] = x1v

        def p3_rA(tt):
            par = tt % 2
            x1v = p3["x1v"][par]
            for dt in range(DT):
                tp = dt % 2
                on("tensor", (lambda En, o=psB[:, tp * 512: tp * 512 + 128],
                              i=x1f_sb[par][:, dt * 128:(dt + 1) * 128], idn=ident_sb[:, :]:
                              En.transpose(o, i, idn)),
                   waits=[(sDVE, x1v if dt == 0 else 0), (sC, nconst), (sDVE, p3["trv"][tp])] +
                         ([(sDVE, p3["lgread"][0]), (sACT, p3["lgread"][1])] if dt == 0 else []),
                   inc=sPE)
                on("vector", (lambda En, o=x1T_sb[tp][:, :], i=psB[:, tp * 512: tp * 512 + 128]:
                              En.tensor_copy(o, i)),
                   waits=[(sPE, sPE.n)], inc=sDVE)
                p3["trv"][tp] = sDVE.n
                on("tensor", (lambda En, o=psB[:, 1536:1536 + E], l=x1T_sb[tp][:, :],
                              rr=ws_sb[:, dt * E:(dt + 1) * E], st=(dt == 0):
                              En.matmul(o, l, rr, start=st, stop=False)),
                   waits=[(sDVE, sDVE.n)], inc=None)
            on("tensor", (lambda En, o=psB[:, 1536:1536 + E], l=ones_row_f[0:1, 0:128], rr=bs_sb[0:1, 0:E]:
                          En.matmul(o, l, rr, start=False, stop=True)),
               waits=[], inc=sPE)
            p3["lgv"][par] = sPE.n
            p3["x1use"][par] = sPE.n

        def p3_rB(tt):
            par = tt % 2
            b8 = par * 8
            lgv = p3["lgv"][par]
            on("vector", (lambda En, o=st3_sb[:, b8 + 7: b8 + 8], i=psB[:, 1536:1536 + E]:
                          En.reduce_max(o, i, axis=AX.X, negate=True)),
               waits=[(sPE, lgv)], inc=sDVE)
            on("scalar", (lambda En, o=e8_sb[par][:, :], i=psB[:, 1536:1536 + E], b=st3_sb[:, b8 + 7: b8 + 8]:
                          En.activation(o, i, AF.Exp, bias=b)),
               waits=[(sDVE, sDVE.n), (sACT, sACT.n)], inc=sACT)
            p3["lgread"] = (sDVE.n, sACT.n)
            on("vector", (lambda En, o=st3_sb[:, b8: b8 + 1], i=e8_sb[par][:, :]: En.reduce_sum(o, i, axis=AX.X)),
               waits=[(sACT, sACT.n)], inc=sDVE)
            on("vector", (lambda En, o=st3_sb[:, b8 + 1: b8 + 2], i=st3_sb[:, b8: b8 + 1]: En.reciprocal(o, i)),
               waits=[(sDVE, sDVE.n)], inc=sDVE)
            on("vector", (lambda En, o=probs_sb[:, tt * E:(tt + 1) * E], i=e8_sb[par][:, :], s=st3_sb[:, b8 + 1: b8 + 2]:
                          En.tensor_scalar_mul(o, i, s)),
               waits=[(sDVE, sDVE.n)], inc=sDVE)
            p3["prv"][tt] = sDVE.n

        def p3_prtr(tt):
            on("tensor", (lambda En, o=psB[0:E, 1024:1024 + 128], i=probs_sb[:, tt * E:(tt + 1) * E],
                          idn=ident_sb[:, :]: En.transpose(o, i, idn)),
               waits=[(sDVE, p3["prv"][tt]), (sDVE, p3["prtr"])], inc=sPE)
            on("vector", (lambda En, o=probsT_sb[0:E, tt * 128:(tt + 1) * 128], i=psB[0:E, 1024:1024 + 128]:
                          En.tensor_copy(o, i)),
               waits=[(sPE, sPE.n)], inc=sDVE)
            p3["prtr"] = sDVE.n

        for tt in range(NT):
            p3_head(tt)
            if tt >= 1:
                p3_rA(tt - 1)
            p3_stats(tt)
            p3_fin(tt)
            if tt >= 1:
                p3_rB(tt - 1)
            if tt >= 2:
                p3_prtr(tt - 2)
        p3_rA(NT - 1)
        p3_rB(NT - 1)
        p3_prtr(NT - 2)
        p3_prtr(NT - 1)
        probs_done = sDVE.n
        x1_all = sX1st.n

        dma("sync", g_dram[0:S].rearrange("(t p) e -> p t e", p=128),
            probs_sb[:, :].rearrange("p (t e) -> p t e", e=E), sPr,
            waits=[(sDVE, probs_done)])
        dma("sync", g_dram[S:S + C, :], zer_sb[:, 0:C * E // 128], sPr)
        for j in range(CT):
            dma("sync", acc_dram[S + j * 128:S + (j + 1) * 128, :], zer_sb[:, 0:D], sPr,
                waits=[(sX1st, x1_all)] if j == 0 else [])
        dma("sync", pin_dram, probsT_sb[:, :], sPr, waits=[(sDVE, probs_done)])
        pr_all = sPr.n
        self.barrier()
        s3.close()
        scopeA.close()

        s56 = ExitStack()
        w1_sb = [T(s56, f"w1_{i}", [128, DT * WC], BF16) for i in range(2)]
        w2f_sb = T(s56, "w2f_sb", [128, FT * D], BF16)
        b2_sb = T(s56, "b2_sb", [1, D], BF16)
        wst_sb = [T(s56, f"wst{i}", [128, 2048], F32) for i in range(2)]
        sWst = [self.SM("sWst0"), self.SM("sWst1")]
        sB2 = self.SM("sB2")
        NWC = F // WC
        st6 = {"w1_pend": [[], []], "w1_used": [0, 0], "w2_used": 0, "wst_cp": [[], []],
               "sttv": {}, "o2ev": [0, 0],
               "wst_rot": 0,
               "xg_free": 0, "xgT_used": [0, 0], "g_used": [0, 0],
               "xg_ld": {}, "g_ld": {}, "sld": {}, "xgT_dve": {}}

        def emit_w1_load(e, fc):
            wpar = fc % 2
            pend = []
            nq = 2 if DT >= 4 else 1
            cols = DT * WC // nq
            qdt = DT // nq
            for q in range(nq):
                s_ = st6["wst_rot"] % 2
                st6["wst_rot"] += 1
                dma("sync", wst_sb[s_][:, 0:cols].rearrange("p (dt f) -> p dt f", f=WC),
                    w1_in[e].rearrange("(dt p) f -> p dt f", p=128)[:, qdt * q:qdt * (q + 1),
                                                                   fc * WC:(fc + 1) * WC],
                    sWst[s_], waits=list(st6["wst_cp"][s_]))
                cw = [(sWst[s_], sWst[s_].n)]
                if q == 0:
                    cw.append((sPE, st6["w1_used"][wpar]))
                if q % 2 == 0:
                    on("vector", (lambda En, o=w1_sb[wpar][:, q * cols:(q + 1) * cols],
                                  i=wst_sb[s_][:, 0:cols]: En.tensor_copy(o, i)),
                       waits=cw, inc=sDVE)
                    st6["wst_cp"][s_] = [(sDVE, sDVE.n)]
                    pend.append((sDVE, sDVE.n))
                else:
                    on("scalar", (lambda En, o=w1_sb[wpar][:, q * cols:(q + 1) * cols],
                                  i=wst_sb[s_][:, 0:cols]: En.copy(o, i)),
                       waits=cw, inc=sACT)
                    st6["wst_cp"][s_] = [(sACT, sACT.n)]
                    pend.append((sACT, sACT.n))
            st6["w1_pend"][wpar] = pend[-2:]


        def emit_w2_load(e):
            dma("gpsimd", w2f_sb[:, :].rearrange("p (ft d) -> p ft d", d=D),
                w2_in[e].rearrange("(ft p) d -> p ft d", p=128), sW2,
                waits=[(sPE, st6["w2_used"])])
            dma("gpsimd", b2_sb[0:1, :], b2_row_in[e], sB2,
                waits=[(sPE, st6["w2_used"])])
            return sW2.n, sB2.n

        s4 = ExitStack()
        pexp_sb = T(s4, "pexp_sb", [128, S], F32)
        NB = c.N // 128
        pb_sb = T(s4, "pb_sb", [128, NB], F32)
        cmp_sb = T(s4, "cmp_sb", [128, NB], F32)
        lo_sb = [T(s4, f"lo{i}", [128, 1], F32) for i in range(2)]
        hi_sb = [T(s4, f"hi{i}", [128, 1], F32) for i in range(2)]
        mid_sb = T(s4, "mid_sb", [128, 1], F32)
        pc_sb = T(s4, "pc_sb", [128, 1], F32)
        cnt_sb = T(s4, "cnt_sb", [1, 1], F32)
        cntb_sb = T(s4, "cntb_sb", [128, 1], F32)
        sel_sb = T(s4, "sel_sb", [128, 1], U32)
        thr8_sb = T(s4, "thr8_sb", [1, E], F32)
        thrb_sb = T(s4, "thrb_sb", [128, E], F32)
        ranks_sb = T(s4, "ranks_sb", [128, NT * E], F32)
        cs_sb = T(s4, "cs_sb", [1, NT * E], F32)
        base_sb = [T(s4, f"base{i}", [1, NT * E], F32) for i in range(2)]
        idxf_sb = T(s4, "idxf_sb", [128, NT * E], F32)
        masku_sb = T(s4, "masku_sb", [128, NT * E], U32)

        on("gpsimd", (lambda En: En.collective_compute(
            "AllGather", ALU.bypass, replica_groups=[list(range(c.NC))],
            ins=[pin_dram], outs=[pout_dram])),
           waits=[(sPr, pr_all)], inc=sCC)
        for fc_ in range(min(2, NWC)):
            emit_w1_load(0, fc_)
        w2_pend0, b2_ld0 = emit_w2_load(0)
        on("gpsimd", (lambda En: En.indirect_dma_start(
            out=pexp_sb[:, :], out_offset=None, in_=pout_dram,
            in_offset=IndirectOffsetOnAxis(sidx_sb[:, E * CT:E * CT + 1], 0),
            bounds_check=regs["R1"], oob_is_err=False)),
           waits=[(sCC, sCC.n), (sC, nconst)], inc=sBis, inc_by=16)
        on("gpsimd", (lambda En: En.dma_start(out=pe_dram, in_=pexp_sb[0:c.NC, :])),
           waits=[(sBis, sBis.n)], inc=sBis, inc_by=16)
        on("gpsimd", (lambda En: En.dma_start(
            out=pb_sb[:, :], in_=pe_dram.rearrange("r (j f) -> r j f", j=128 // c.NC))),
           waits=[(sBis, sBis.n)], inc=sBis, inc_by=16)
        bis_ld = sBis.n
        on("vector", lambda En: En.memset(lo_sb[0][:, :], 0.0), inc=sDVE)
        on("vector", lambda En: En.memset(hi_sb[0][:, :], 1.0), inc=sDVE)
        for it in range(c.BIS):
            pi, po = it % 2, (it + 1) % 2
            on("vector", (lambda En, o=mid_sb[:, :], i=lo_sb[pi][:, :], hh=hi_sb[pi][:, :]:
                          En.tensor_scalar(o, i, hh, 0.5, ALU.add, ALU.mult)),
               waits=[(sDVE, sDVE.n)], inc=sDVE)
            on("vector", (lambda En, o=cmp_sb[:, :], i=pb_sb[:, :], s=mid_sb[:, :]:
                          En.tensor_single_scalar(o, i, s, ALU.is_gt)),
               waits=[(sDVE, sDVE.n)] + ([(sBis, bis_ld)] if it == 0 else []), inc=sDVE)
            on("vector", (lambda En, o=pc_sb[:, :], i=cmp_sb[:, :]: En.reduce_sum(o, i, axis=AX.X)),
               waits=[(sDVE, sDVE.n)], inc=sDVE)
            on("tensor", (lambda En, o=psB[0:1, 1664:1665], l=pc_sb[:, :], rr=ones_col_f[:, :]:
                          En.matmul(o, l, rr, start=True, stop=True)),
               waits=[(sDVE, sDVE.n)], inc=sPE)
            on("vector", (lambda En, o=cnt_sb[:, :], i=psB[0:1, 1664:1665]: En.tensor_copy(o, i)),
               waits=[(sPE, sPE.n)], inc=sDVE)
            on("tensor", (lambda En, o=psB[:, 1665:1666], l=ones_row_f[0:1, 0:128], rr=cnt_sb[:, :]:
                          En.matmul(o, l, rr, start=True, stop=True)),
               waits=[(sDVE, sDVE.n)], inc=sPE)
            on("vector", (lambda En, o=cntb_sb[:, :], i=psB[:, 1665:1666]: En.tensor_copy(o, i)),
               waits=[(sPE, sPE.n)], inc=sDVE)
            on("vector", (lambda En, o=sel_sb[:, :], i=cntb_sb[:, :]:
                          En.tensor_single_scalar(o, i, float(c.KSEL), ALU.is_ge)),
               waits=[(sDVE, sDVE.n)], inc=sDVE)
            on("vector", (lambda En, o=lo_sb[po][:, :], m=sel_sb[:, :], t=mid_sb[:, :], f=lo_sb[pi][:, :]:
                          En.select(o, m, t, f, add_drain=True)),
               waits=[(sDVE, sDVE.n)], inc=sDVE)
            on("vector", (lambda En, o=hi_sb[po][:, :], m=sel_sb[:, :], t=hi_sb[pi][:, :], f=mid_sb[:, :]:
                          En.select(o, m, t, f, add_drain=True)),
               waits=[(sDVE, sDVE.n)], inc=sDVE)
        fin = c.BIS % 2
        on("gpsimd", (lambda En: En.dma_start(out=tin_dram, in_=lo_sb[fin][0:1, 0:1])),
           waits=[(sDVE, sDVE.n)], inc=sBis, inc_by=16)
        on("gpsimd", (lambda En: En.collective_compute(
            "AllGather", ALU.bypass, replica_groups=[list(range(c.NC))],
            ins=[tin_dram], outs=[tout_dram])),
           waits=[(sBis, sBis.n)], inc=sCC)
        on("gpsimd", (lambda En: En.dma_start(out=thr8_sb[0:1, 0:E],
                                              in_=tout_dram.rearrange("r one -> (one r)")[None, :])),
           waits=[(sCC, sCC.n)], inc=sBis, inc_by=16)
        on("tensor", (lambda En, o=psB[:, 1666:1666 + E], l=ones_row_f[0:1, 0:128], rr=thr8_sb[0:1, 0:E]:
                      En.matmul(o, l, rr, start=True, stop=True)),
           waits=[(sBis, sBis.n)], inc=sPE)
        on("vector", (lambda En, o=thrb_sb[:, :], i=psB[:, 1666:1666 + E]: En.tensor_copy(o, i)),
           waits=[(sPE, sPE.n)], inc=sDVE)
        thr_dve = sDVE.n

        for tt in range(NT):
            on("vector", (lambda En, o=mask_sb[:, tt * E:(tt + 1) * E],
                          i0=probs_sb[:, tt * E:(tt + 1) * E], i1=thrb_sb[:, :]:
                          En.tensor_tensor(o, i0, i1, ALU.is_gt)),
               waits=[(sDVE, thr_dve)] if tt == 0 else [], inc=(sDVE if tt == NT - 1 else None))
        mask_dve = sDVE.n
        for tt in range(NT):
            on("tensor", (lambda En, o=psA[:, tt * E:(tt + 1) * E], l=triu_sb[:, :],
                          rr=mask_sb[:, tt * E:(tt + 1) * E], st=(tt == 0):
                          En.matmul(o, l, rr, start=st, stop=False)),
               waits=([(sDVE, mask_dve), (sC, nconst)] if tt == 0 else []), inc=None)
        on("tensor", (lambda En, o=psB[0:1, 0:NT * E], l=ones_col_f[:, :], rr=mask_sb[:, :]:
                      En.matmul(o, l, rr, start=True, stop=True)),
           waits=[], inc=sPE)
        on("vector", (lambda En, o=cs_sb[:, :], i=psB[0:1, 0:NT * E]: En.tensor_copy(o, i)),
           waits=[(sPE, sPE.n)], inc=sDVE)
        on("vector", (lambda En, o=base_sb[0][0:1, 0:E]: En.memset(o, 0.0)),
           waits=[(sDVE, sDVE.n)], inc=sDVE)
        on("vector", (lambda En, o=base_sb[0][0:1, E:NT * E], i=cs_sb[0:1, 0:(NT - 1) * E]: En.tensor_copy(o, i)),
           waits=[(sDVE, sDVE.n)], inc=sDVE)
        shift, cur = 1, 0
        while shift < NT:
            on("vector", (lambda En, o=base_sb[1 - cur][0:1, shift * E:NT * E],
                          i0=base_sb[cur][0:1, shift * E:NT * E], i1=base_sb[cur][0:1, 0:(NT - shift) * E]:
                          En.tensor_tensor(o, i0, i1, ALU.add)),
               waits=[(sDVE, sDVE.n)], inc=sDVE)
            on("vector", (lambda En, o=base_sb[1 - cur][0:1, 0:shift * E], i=base_sb[cur][0:1, 0:shift * E]:
                          En.tensor_copy(o, i)),
               waits=[(sDVE, sDVE.n)], inc=sDVE)
            cur = 1 - cur
            shift *= 2
        basev = sDVE.n
        for tt in range(NT):
            on("tensor", (lambda En, o=psA[:, tt * E:(tt + 1) * E], l=ones_row_f[0:1, 0:128],
                          rr=base_sb[cur][0:1, tt * E:(tt + 1) * E], sp=(tt == NT - 1):
                          En.matmul(o, l, rr, start=False, stop=sp)),
               waits=([(sDVE, basev)] if tt == 0 else []), inc=(sPE if tt == NT - 1 else None))
        on("vector", (lambda En, o=ranks_sb[:, :], i=psA[:, 0:NT * E]: En.tensor_copy(o, i)),
           waits=[(sPE, sPE.n)], inc=sDVE)
        on("vector", (lambda En, o=masku_sb[:, :], i=mask_sb[:, :]: En.tensor_copy(o, i)),
           waits=[(sDVE, sDVE.n)], inc=sDVE)
        on("vector", (lambda En, o=idxf_sb[:, :], m=masku_sb[:, :], t=ranks_sb[:, :], f=tokbigE_sb[:, :]:
                      En.select(o, m, t, f, add_drain=True)),
           waits=[(sDVE, sDVE.n), (sC, nconst)], inc=sDVE)
        on("vector", (lambda En, o=idxu_sb[:, :], i=idxf_sb[:, :]: En.tensor_copy(o, i)),
           waits=[(sDVE, sDVE.n)], inc=sDVE)
        idx_dve = sDVE.n

        def emit_slot_scatters(e):
            for tt in range(NT):
                on("gpsimd", (lambda En, oi=idxu_sb[:, tt * E + e: tt * E + e + 1],
                              ii=tokid_sb[:, tt:tt + 1], eo=e * C:
                              En.indirect_dma_start(out=slot_dram, out_offset=IndirectOffsetOnAxis(oi, 0),
                                                    in_=ii, in_offset=None, element_offset=eo,
                                                    bounds_check=regs["C1"], oob_is_err=False)),
                   waits=([(sDVE, idx_dve), (sG, sG.n), (sC, nconst)] if tt == 0 else []),
                   inc=sSc[e], inc_by=16)

        emit_slot_scatters(0)
        self.barrier()
        s4.close()

        s6 = ExitStack()
        xg_sb = T(s6, "xg_sb", [128, CT * D], F32)
        xgT_sb = [T(s6, f"xgT{i}", [128, DT * C], BF16) for i in range(2)]
        g_sb = T(s6, "g_sb", [128, 2 * CT], F32)
        h1T_sb = T(s6, "h1T_sb", [128, FT * C], BF16)
        xbh_sb = [T(s6, f"xbh{i}", [128, C], F32) for i in range(2)]
        gt1_sb = [T(s6, f"gt1_{i}", [128, C], F32) for i in range(2)]
        o2_sb = [T(s6, f"o2_{i}", [128, D], F32) for i in range(2)]

        on("vector", lambda En: En.memset(xg_sb[:, :], 0.0), inc=sDVE)
        xg_init = sDVE.n

        def emit_slot_loads(e):
            for j in range(CT):
                dma("sync", sidx_sb[:, e * CT + j: e * CT + j + 1],
                    slot_dram[e * C + j * 128:e * C + (j + 1) * 128, 0:1],
                    sSld[e], waits=([(sSc[e], sSc[e].n)] if j == 0 else []))
            st6["sld"][e] = sSld[e].n

        def emit_gathers(e):
            epar = e % 2
            for j in range(CT):
                on("gpsimd", (lambda En, o=xg_sb[:, j * D:(j + 1) * D],
                              oi=sidx_sb[:, e * CT + j: e * CT + j + 1]:
                              En.indirect_dma_start(out=o, out_offset=None, in_=x1_dram,
                                                    in_offset=IndirectOffsetOnAxis(oi, 0),
                                                    bounds_check=regs["S1"], oob_is_err=False)),
                   waits=([(sSld[e], st6["sld"][e]), (sX1st, x1_all),
                           (sPE, st6["xg_free"]), (sDVE, xg_init)] if j == 0 else []),
                   inc=sXg[epar], inc_by=16)
                on("gpsimd", (lambda En, o=g_sb[:, epar * CT + j: epar * CT + j + 1],
                              oi=sidx_sb[:, e * CT + j: e * CT + j + 1], eo=e:
                              En.indirect_dma_start(out=o, out_offset=None, in_=g_dram,
                                                    in_offset=IndirectOffsetOnAxis(oi, 0), element_offset=eo)),
                   waits=([(sPr, pr_all), (sDVE, st6["g_used"][epar])] if j == 0 else []), inc=sGg[epar], inc_by=16)
            st6["xg_ld"][e] = sXg[epar].n
            st6["g_ld"][e] = sGg[epar].n

        def emit_transposes(e):
            epar = e % 2
            tr_evac = [0, 0, 0, 0]
            for j in range(CT):
                for dt in range(DT):
                    tp = (j * DT + dt) % 4
                    tpo = tp * 512
                    w = [(sACT, tr_evac[tp])]
                    if j == 0 and dt == 0:
                        w += [(sXg[epar], st6["xg_ld"][e]), (sPE, st6["xgT_used"][epar]),
                              (sDVE, sDVE.n), (sACT, sACT.n)]
                    on("tensor", (lambda En, o=psB[:, tpo:tpo + 128],
                                  i=xg_sb[:, j * D + dt * 128: j * D + (dt + 1) * 128], idn=ident_sb[:, :]:
                                  En.transpose(o, i, idn)),
                       waits=w, inc=sPE)
                    on("scalar", (lambda En, o=xgT_sb[epar][:, dt * C + j * 128: dt * C + (j + 1) * 128],
                                  i=psB[:, tpo:tpo + 128]: En.copy(o, i)),
                       waits=[(sPE, sPE.n)], inc=sACT)
                    tr_evac[tp] = sACT.n
            st6["xg_free"] = sPE.n
            st6["xgT_dve"][e] = sACT.n

        def emit_h1(e):
            epar = e % 2
            ps_free = [0, 0]
            gpar_free = [(0, 0), (0, 0)]
            for fc in range(NWC):
                wpar = fc % 2
                for f8 in range(WC // 128):
                    ft = fc * (WC // 128) + f8
                    hp = ft % 2
                    gp = ft % 2
                    ps = psA[:, hp * 1024: hp * 1024 + C]
                    for dt in range(DT):
                        for c0, c1 in chunks(0, C):
                            w = []
                            if dt == 0 and c0 == 0:
                                w = list(st6["w1_pend"][wpar]) + [(sDVE, ps_free[hp])]
                                if ft == 0:
                                    w.append((sACT, st6["xgT_dve"][e]))
                                    w.append((sDVE, sDVE.n))
                            on("tensor", (lambda En, o=ps[:, c0:c1],
                                          l=w1_sb[wpar][:, dt * WC + f8 * 128: dt * WC + (f8 + 1) * 128],
                                          rr=xgT_sb[epar][:, dt * C + c0: dt * C + c1],
                                          st=(dt == 0), sp=(dt == DT - 1):
                                          En.matmul(o, l, rr, start=st, stop=sp)),
                               waits=w, inc=(sPE if (dt == DT - 1 and c1 == C) else None))
                    st6["w1_used"][wpar] = sPE.n
                    dv0, ac0 = gpar_free[gp]
                    on("vector", (lambda En, o=xbh_sb[gp][:, :], i=ps, b=b1T_sb[:, e * FT + ft: e * FT + ft + 1]:
                                  En.tensor_scalar(o, i, b, 0.5, ALU.add, ALU.mult)),
                       waits=[(sPE, sPE.n), (sG, sG.n), (sDVE, dv0), (sACT, ac0)], inc=sDVE)
                    ps_free[hp] = sDVE.n
                    on("scalar", (lambda En, o=gt1_sb[gp][:, :], i=xbh_sb[gp][:, :]:
                                  En.activation(o, i, AF.Square, scale=2.0)),
                       waits=[(sDVE, sDVE.n)], inc=sACT)
                    on("vector", (lambda En, o=gt1_sb[gp][:, :], i=gt1_sb[gp][:, :]:
                                  En.tensor_scalar(o, i, GELU_C, 1.0, ALU.mult, ALU.add)),
                       waits=[(sACT, sACT.n)], inc=sDVE)
                    on("vector", (lambda En, o=gt1_sb[gp][:, :], i0=xbh_sb[gp][:, :], i1=gt1_sb[gp][:, :]:
                                  En.tensor_mul(o, i0, i1)),
                       waits=[(sDVE, sDVE.n)], inc=sDVE)
                    on("scalar", (lambda En, o=gt1_sb[gp][:, :], i=gt1_sb[gp][:, :]:
                                  En.activation(o, i, AF.Tanh, scale=GELU_S2)),
                       waits=[(sDVE, sDVE.n)], inc=sACT)
                    on("vector", (lambda En, o=h1T_sb[:, ft * C:(ft + 1) * C], i0=gt1_sb[gp][:, :], i1=xbh_sb[gp][:, :]:
                                  En.scalar_tensor_tensor(o, i0, 1.0, i1, ALU.add, ALU.mult)),
                       waits=[(sACT, sACT.n)], inc=sDVE)
                    st6["sttv"][ft] = sDVE.n
                    gpar_free[gp] = (sDVE.n, sACT.n)
                if fc + 2 < NWC:
                    emit_w1_load(e, fc + 2)
            st6["xgT_used"][epar] = sPE.n
            return sDVE.n



        def emit_h2(e, h1_dve, w2_pend, b2_ld):
            epar = e % 2
            for ct in range(CT):
                cp = ct % 2
                ps = psB[:, cp * 1024: cp * 1024 + D]
                for ft in range(FT):
                    for c0, c1 in chunks(0, D):
                        w = []
                        if ft == 0 and c0 == 0:
                            w = [(sW2, w2_pend), (sDVE, st6["o2ev"][cp])]
                        if ct == 0 and c0 == 0:
                            w.append((sDVE, st6["sttv"][ft]))
                        on("tensor", (lambda En, o=ps[:, c0:c1],
                                      l=h1T_sb[:, ft * C + ct * 128: ft * C + ct * 128 + 128],
                                      rr=w2f_sb[:, ft * D + c0: ft * D + c1],
                                      st=(ft == 0), sp=False:
                                      En.matmul(o, l, rr, start=st, stop=sp)),
                           waits=w, inc=None)
                for c0, c1 in chunks(0, D):
                    on("tensor", (lambda En, o=ps[:, c0:c1], l=ones_row_b[0:1, 0:128],
                                  rr=b2_sb[0:1, c0:c1]:
                                  En.matmul(o, l, rr, start=False, stop=True)),
                       waits=([(sB2, b2_ld)] if (ct == 0 and c0 == 0) else []),
                       inc=(sPE if c1 == D else None))
                on("vector", (lambda En, o=o2_sb[cp][:, :], i=ps, s=g_sb[:, epar * CT + ct: epar * CT + ct + 1]:
                              En.tensor_scalar_mul(o, i, s)),
                   waits=[(sPE, sPE.n), (sGg[epar], st6["g_ld"][e]), (sSA, sSA.n)], inc=sDVE)
                st6["o2ev"][cp] = sDVE.n
                on("gpsimd", (lambda En, ii=o2_sb[cp][:, :],
                              oi=sidx_sb[:, e * CT + ct: e * CT + ct + 1]:
                              En.indirect_dma_start(out=acc_dram, out_offset=IndirectOffsetOnAxis(oi, 0),
                                                    in_=ii, in_offset=None, compute_op=ALU.add)),
                   waits=[(sDVE, sDVE.n), (sSA, sSA.n), (sPr, pr_all)], inc=sSA, inc_by=16)
            st6["w2_used"] = sPE.n
            st6["g_used"][epar] = sDVE.n

        emit_slot_loads(0)
        emit_gathers(0)
        w2_pend, b2_ld = w2_pend0, b2_ld0
        nxt = {}
        for e in range(E):
            emit_transposes(e)
            if e > 0:
                w2_pend, b2_ld = emit_w2_load(e)
            h1_dve = emit_h1(e)
            if e + 1 < E:
                emit_slot_scatters(e + 1)
                emit_slot_loads(e + 1)
                emit_gathers(e + 1)
                for fc_ in range(min(2, NWC)):
                    emit_w1_load(e + 1, fc_)
            emit_h2(e, h1_dve, w2_pend, b2_ld)
        sa_all = sSA.n
        self.barrier()
        s6.close()
        s56.close()

        s7 = ExitStack()
        a_sb = [T(s7, f"a_{i}", [128, D], F32) for i in range(4)]
        sq7_sb = [T(s7, f"sq7_{i}", [128, D], F32) for i in range(2)]
        out_sb = [T(s7, f"out_{i}", [128, D], F32) for i in range(4)]
        st7_sb = T(s7, "st7_sb", [128, 32], F32)
        sAld4 = [self.SM(f"sAld4_{i}") for i in range(4)]
        sOut2 = [self.SM(f"sOut2_{i}") for i in range(4)]
        p7 = {"apl": [0, 0, 0, 0], "stdve": [0, 0, 0, 0], "stact": [0, 0],
              "outst": [0, 0, 0, 0], "mic": [0, 0, 0, 0]}

        def p7_a(tt):
            p4_ = tt % 4
            p2_ = tt % 2
            b8 = p4_ * 8
            dma("sync" if p4_ % 2 == 0 else "gpsimd",
                a_sb[p4_][:, :], acc_dram[tt * 128:(tt + 1) * 128, :], sAld4[p4_],
                waits=([(sSA, sa_all)] if tt == 0 else []) + [(sDVE, p7["apl"][p4_])])
            aldv = sAld4[p4_].n
            on("vector", (lambda En, o=st7_sb[:, b8: b8 + 1], i=a_sb[p4_][:, :]:
                          En.reduce_sum(o, i, axis=AX.X)),
               waits=[(sAld4[p4_], aldv), (sDVE, p7["stdve"][p4_])], inc=sDVE)
            on("scalar", (lambda En, o=sq7_sb[p2_][:, :], i=a_sb[p4_][:, :], a=st7_sb[:, b8 + 1: b8 + 2]:
                          En.activation(o, i, AF.Square, accum_out=a)),
               waits=[(sAld4[p4_], aldv), (sACT, p7["stact"][p2_]),
                      (sDVE, p7["stdve"][p4_])], inc=sACT)
            p7["stact"][p2_] = sACT.n
            on("vector", (lambda En, o=st7_sb[:, b8 + 2: b8 + 3], i=st7_sb[:, b8: b8 + 1]:
                          En.tensor_scalar_mul(o, i, 1.0 / D)),
               waits=[(sDVE, sDVE.n)], inc=sDVE)
            on("vector", (lambda En, o=st7_sb[:, b8 + 3: b8 + 4], i=st7_sb[:, b8 + 2: b8 + 3], s=st7_sb[:, b8 + 2: b8 + 3]:
                          En.tensor_scalar(o, i, s, -LNEPS, ALU.mult, ALU.add)),
               waits=[(sDVE, sDVE.n)], inc=sDVE)
            on("vector", (lambda En, o=st7_sb[:, b8 + 4: b8 + 5], i=st7_sb[:, b8 + 1: b8 + 2], s=st7_sb[:, b8 + 3: b8 + 4]:
                          En.tensor_scalar(o, i, 1.0 / D, s, ALU.mult, ALU.subtract)),
               waits=[(sDVE, sDVE.n), (sACT, p7["stact"][p2_])], inc=sDVE)
            p7["mic"][p4_] = sDVE.n

        def p7_b(tt):
            p4_ = tt % 4
            p2_ = tt % 2
            b8 = p4_ * 8
            on("scalar", (lambda En, o=st7_sb[:, b8 + 5: b8 + 6], i=st7_sb[:, b8 + 4: b8 + 5]:
                          En.activation(o, i, AF.Sqrt)),
               waits=[(sDVE, p7["mic"][p4_])], inc=sACT)
            on("vector", (lambda En, o=st7_sb[:, b8 + 6: b8 + 7], i=st7_sb[:, b8 + 5: b8 + 6]:
                          En.reciprocal(o, i)),
               waits=[(sACT, sACT.n)], inc=sDVE)
            p7["stdve"][p4_] = sDVE.n
            on("vector", (lambda En, o=out_sb[p4_][:, :], i=a_sb[p4_][:, :], m=st7_sb[:, b8 + 2: b8 + 3],
                          rr=st7_sb[:, b8 + 6: b8 + 7]:
                          En.tensor_scalar(o, i, m, rr, ALU.subtract, ALU.mult)),
               waits=[(sDVE, sDVE.n), (sOut2[p4_], p7["outst"][p4_])], inc=sDVE)
            p7["apl"][p4_] = sDVE.n
            dma("gpsimd" if p4_ % 2 == 0 else "sync",
                y_out[tt * 128:(tt + 1) * 128, :], out_sb[p4_][:, :], sOut2[p4_],
                waits=[(sDVE, sDVE.n)])
            p7["outst"][p4_] = sOut2[p4_].n

        for g in range(NT // 4):
            for i_ in range(4):
                p7_a(4 * g + i_)
            for i_ in range(4):
                p7_b(4 * g + i_)
        self.on("sync", (lambda En: None),
                waits=[(sOut2[i_], sOut2[i_].n) for i_ in range(4)])

        with nc.Block() as block:
            for eng in self.ENGS:
                getattr(block, eng)(self._runner(eng))
        s7.close()
        ctx.close()
        return nc

    def ln_stats(self, zt, sqt, st, zv, sPE, sACT, sDVE, in_ready=None):
        on = self.on
        D = self.cfg.D
        w0 = in_ready if in_ready is not None else [(sDVE, zv)]
        on("scalar", (lambda En, o=sqt[:, :], i=zt[:, :]: En.activation(o, i, AF.Square)),
           waits=w0 + [(sDVE, sDVE.n)], inc=sACT)
        on("vector", (lambda En, o=st[:, 0:1], i=zt[:, :]: En.reduce_sum(o, i, axis=AX.X)),
           waits=w0 + [(sDVE, sDVE.n)], inc=sDVE)
        on("vector", (lambda En, o=st[:, 1:2], i=sqt[:, :]: En.reduce_sum(o, i, axis=AX.X)),
           waits=[(sACT, sACT.n), (sDVE, sDVE.n)], inc=sDVE)
        on("vector", (lambda En, o=st[:, 2:3], i=st[:, 0:1]: En.tensor_scalar_mul(o, i, 1.0 / D)),
           waits=[(sDVE, sDVE.n)], inc=sDVE)
        on("vector", (lambda En, o=st[:, 3:4], i=st[:, 2:3]:
                      En.tensor_scalar(o, i, st[:, 2:3], -LNEPS, ALU.mult, ALU.add)),
           waits=[(sDVE, sDVE.n)], inc=sDVE)
        on("vector", (lambda En, o=st[:, 4:5], i=st[:, 1:2], s1=st[:, 3:4]:
                      En.tensor_scalar(o, i, 1.0 / D, s1, ALU.mult, ALU.subtract)),
           waits=[(sDVE, sDVE.n)], inc=sDVE)
        on("scalar", (lambda En, o=st[:, 5:6], i=st[:, 4:5]: En.activation(o, i, AF.Sqrt)),
           waits=[(sDVE, sDVE.n)], inc=sACT)
        on("vector", (lambda En, o=st[:, 6:7], i=st[:, 5:6]: En.reciprocal(o, i)),
           waits=[(sACT, sACT.n)], inc=sDVE)

    def _runner(self, eng):
        thunks = self.q[eng]
        def run(E):
            for t in thunks:
                t(E)
        return run


def host_inputs(cfg: Cfg, full):
    c = cfg
    E, C, S, D, NT = c.E, c.C, c.S, c.D, c.NT
    x = np.ascontiguousarray(np.asarray(full["L0_attn_x"], np.float32))
    wqkvT = np.ascontiguousarray(np.asarray(full["L0_attn_Wqkv_w"], np.float32).T)
    bqkv = np.ascontiguousarray(np.asarray(full["L0_attn_Wqkv_b"], np.float32))
    bqkvT = np.ascontiguousarray(bqkv.reshape(3 * D // 128, 128).T)
    woutT = np.ascontiguousarray(np.asarray(full["L0_attn_out_w"], np.float32).T)
    bout = np.ascontiguousarray(np.asarray(full["L0_attn_out_b"], np.float32).reshape(1, D))
    ws = np.ascontiguousarray(np.asarray(full["L0_ffn_Ws"], np.float32))
    bs = np.ascontiguousarray(np.asarray(full["L0_ffn_bs"], np.float32).reshape(1, E))
    w1 = np.ascontiguousarray(np.asarray(full["L0_ffn_W1"], np.float32))
    b1 = np.asarray(full["L0_ffn_b1"], np.float32)
    b1T = np.ascontiguousarray(b1.reshape(E, c.F // 128, 128).transpose(0, 2, 1))
    w2 = np.ascontiguousarray(np.asarray(full["L0_ffn_W2"], np.float32))
    b2 = np.ascontiguousarray(np.asarray(full["L0_ffn_b2"], np.float32).reshape(E, 1, D))
    ident = np.eye(128, dtype=np.float32)
    triu = np.triu(np.ones((128, 128), np.float32), 1)
    import ml_dtypes
    tri01 = np.triu(np.ones((128, 128), np.float32), 0).astype(ml_dtypes.bfloat16)
    tokid = (np.arange(NT)[None, :] * 128 + np.arange(128)[:, None]).astype(np.uint32)
    tokbigE = np.empty((128, NT * E), np.float32)
    for tt in range(NT):
        for e in range(E):
            tokbigE[:, tt * E + e] = BIG + tt * 128 + np.arange(128)
    slotinit = np.empty((E * C, 1), np.uint32)
    for e in range(E):
        slotinit[e * C:(e + 1) * C, 0] = S + np.arange(C)
    maps = []
    for cid in range(c.NC):
        rowidx = np.full((128, 1), 0xFFFFF0, np.uint32)
        rowidx[:c.NC, 0] = np.arange(c.NC) * E + cid
        maps.append({
            "x": x[cid], "xT": np.ascontiguousarray(x[cid].T),
            "wqkvT": wqkvT, "bqkvT": bqkvT, "bqkv_row": bqkv.reshape(1, 3 * D),
            "woutT": woutT, "bout_row": bout, "ws": ws, "bs_row": bs,
            "w1": w1, "b1T": b1T, "w2": w2, "b2_row": b2,
            "ident": ident, "triu": triu,
            "tri01": tri01, "tokid": tokid, "tokbigE": tokbigE, "slotinit": slotinit,
            "rowidx": rowidx,
        })
    return maps


_NC_CACHE = {}

def _get_nc():
    if "nc" not in _NC_CACHE:
        cfg = Cfg(S=4096, D=1024, H=16, F=4096, E=8, NC=8, TB=1024, C=640, BIS=30)
        mk = MK(cfg)
        _NC_CACHE["cfg"] = cfg
        _NC_CACHE["nc"] = mk.build()
    return _NC_CACHE["cfg"], _NC_CACHE["nc"]


def kernel(L0_attn_x, L0_attn_Wqkv_w, L0_attn_Wqkv_b, L0_attn_out_w, L0_attn_out_b,
           L0_ffn_Ws, L0_ffn_bs, L0_ffn_W1, L0_ffn_b1, L0_ffn_W2, L0_ffn_b2):
    from concourse.bass_utils import run_bass_kernel_spmd
    cfg, nc = _get_nc()
    full = {
        "L0_attn_x": L0_attn_x, "L0_attn_Wqkv_w": L0_attn_Wqkv_w,
        "L0_attn_Wqkv_b": L0_attn_Wqkv_b, "L0_attn_out_w": L0_attn_out_w,
        "L0_attn_out_b": L0_attn_out_b, "L0_ffn_Ws": L0_ffn_Ws,
        "L0_ffn_bs": L0_ffn_bs, "L0_ffn_W1": L0_ffn_W1, "L0_ffn_b1": L0_ffn_b1,
        "L0_ffn_W2": L0_ffn_W2, "L0_ffn_b2": L0_ffn_b2,
    }
    maps = host_inputs(cfg, full)
    res = run_bass_kernel_spmd(nc, maps, core_ids=list(range(cfg.NC)))
    out = np.stack([np.asarray(res.results[c]["y"], dtype=np.float32) for c in range(cfg.NC)], axis=0)
    return out
